# version 22
# baseline (speedup 1.0000x reference)
import numpy as np
import ml_dtypes
import concourse.bass as bass
import concourse.mybir as mybir
from concourse.bass_utils import run_bass_kernel_spmd
from contextlib import ExitStack

F32 = mybir.dt.float32
BF16 = mybir.dt.bfloat16
AF = mybir.ActivationFunctionType
ALU = mybir.AluOpType
AX = mybir.AxisListType
BF = ml_dtypes.bfloat16
MM2 = mybir.MatmulPerfMode.DoubleRow

NCORE = 8
B = 2
TOWN = 1024
HALO = 18
TLOC = TOWN + HALO
NT = 9
NTOK = NT * 128
PAD = 16
D = 1024
NCH = 8
LN_EPS = 1e-5
RMS_EPS = 1e-6
GRPS = [(0, 512), (512, 1024), (1024, 1152)]
SPARSE_MOE = True


STREAMS = ("tensor", "vector", "scalar", "gpsimd", "sync", "gpsw")
ENGINE_STREAMS = {"tensor": ("tensor",), "vector": ("vector",),
                  "scalar": ("scalar",), "sync": ("sync",),
                  "gpsimd": ("gpsimd", "gpsw")}


class _Op:
    __slots__ = ("stream", "fn", "idx", "is_dma", "needs_inc", "count",
                 "waits", "snap", "wait_count")

    def __init__(self, stream, fn, idx, is_dma):
        self.stream = stream
        self.fn = fn
        self.idx = idx
        self.is_dma = is_dma
        self.needs_inc = False
        self.count = None
        self.waits = []
        self.snap = None
        self.wait_count = None


class Prog:

    def __init__(self):
        self.ops = []
        self.last_w = {}
        self.readers = {}
        self.clock = {s: {t: -1 for t in STREAMS} for s in STREAMS}
        self.out_dmas = []
        self.arena_knames = set()
        self.epoch_fence = []
        self.last_per_stream = {}

    def op(self, stream, fn, reads=(), writes=(), is_dma=False, is_output=False):
        o = _Op(stream, fn, len(self.ops), is_dma)
        if is_dma:
            o.needs_inc = True
        deps = []
        for k in reads:
            w = self.last_w.get(k)
            if w is not None:
                deps.append(w)
        for k in writes:
            w = self.last_w.get(k)
            if w is not None:
                deps.append(w)
            deps.extend(self.readers.get(k, ()))
            if self.arena_knames and isinstance(k, tuple) and k and k[0] in self.arena_knames:
                deps.extend(self.epoch_fence)
        ck = self.clock[stream]
        need = {}
        for d in deps:
            if d.stream == "tensor" and stream == "tensor":
                continue
            if ck[d.stream] >= d.idx:
                continue
            cur = need.get(d.stream)
            if cur is None or d.idx > cur.idx:
                need[d.stream] = d
        for d in need.values():
            d.needs_inc = True
            if d.snap is None:
                d.snap = dict(self.clock[d.stream])
                d.snap[d.stream] = d.idx
            o.waits.append(d)
            for s2, v in d.snap.items():
                if v > ck[s2]:
                    ck[s2] = v
        for k in writes:
            self.last_w[k] = o
            self.readers[k] = []
        for k in reads:
            self.readers.setdefault(k, []).append(o)
        self.ops.append(o)
        self.last_per_stream[stream] = o
        if is_output:
            o.needs_inc = True
            self.out_dmas.append(o)
        return o

    def dma(self, stream, out_ap, in_ap, reads=(), writes=(), is_output=False, **kw):
        return self.op(stream,
                       (lambda e, o=out_ap, i=in_ap, kw=kw: e.dma_start(out=o, in_=i, **kw)),
                       reads=reads, writes=writes, is_dma=True, is_output=is_output)

    def emit(self, nc, ctx):
        counts = {s: 0 for s in STREAMS}
        for o in self.ops:
            if o.needs_inc:
                counts[o.stream] += 16 if o.is_dma else 1
                o.count = counts[o.stream]
        sems = {s: ctx.enter_context(nc.semaphore("sem_" + s)) for s in STREAMS}
        per = {en: [] for en in ENGINE_STREAMS}
        eng_of = {}
        for en, ss in ENGINE_STREAMS.items():
            for s in ss:
                eng_of[s] = en
        for o in self.ops:
            per[eng_of[o.stream]].append(o)
        for en in ENGINE_STREAMS:
            burst = []

            def flush(burst):
                if burst:
                    final = burst[-1].count
                    for b in burst:
                        b.wait_count = final

            for o in per[en]:
                if o.is_dma and not o.waits and (not burst or burst[-1].stream == o.stream):
                    burst.append(o)
                elif o.is_dma:
                    flush(burst)
                    burst = [o]
                else:
                    flush(burst)
                    burst = []
            flush(burst)
        block = ctx.enter_context(nc.Block())

        def run(eng, en):
            hw = {}
            for o in per[en]:
                for d in o.waits:
                    w = d.wait_count if d.wait_count is not None else d.count
                    if hw.get(d.stream, -1) < w:
                        eng.wait_ge(sems[d.stream], w)
                        hw[d.stream] = w
                inst = o.fn(eng)
                if o.needs_inc:
                    inst.then_inc(sems[o.stream], 16 if o.is_dma else 1)
            for s in ENGINE_STREAMS[en]:
                fin = [o for o in self.out_dmas if o.stream == s]
                if fin:
                    eng.wait_ge(sems[s], max(o.wait_count or o.count for o in fin))

        @block.tensor
        def _(e):
            run(e, "tensor")

        @block.vector
        def _(e):
            run(e, "vector")

        @block.scalar
        def _(e):
            run(e, "scalar")

        @block.gpsimd
        def _(e):
            run(e, "gpsimd")

        @block.sync
        def _(e):
            run(e, "sync")



class Ctx:

    def __init__(self, nc, ctx):
        self.nc = nc
        self.x = ctx
        self.P = Prog()
        self.uid = 0
        sb = self.sb = lambda name, shape, dt=F32: ctx.enter_context(
            nc.sbuf_tensor(name, shape, dt))
        ps = lambda name, shape, dt=F32: ctx.enter_context(
            nc.psum_tensor(name, shape, dt))
        self.x_tm = sb("x_tm", [128, B, NT, 1024])
        self.xf = sb("xf", [128, NCH, B, PAD + NTOK], BF16)
        self.ident = sb("ident_sb", [128, 128], BF16)
        self.ones = sb("ones_bf", [1, 128], BF16)
        self.bn12 = sb("bn12", [128, 2, 12])
        self.mv = sb("mv", [128, 2, 2])
        self.nm = sb("nm", [128, 2, 1])
        self.sd = sb("sd", [128, 2, 1])
        self.rs = sb("rs", [128, 2, 1])
        self.xf_tm = sb("xf_tm", [128, 2, 1024], BF16)
        self.moe_gcb = sb("moe_gcb", [128, 2, 8], BF16)
        self.smalls = {}
        self.epsv = sb("epsv", [128, 2])
        self.psA = [ps(f"psA{i}", [128, 512]) for i in range(8)]
        self.psT = [self.psA[6].bitcast(BF16), self.psA[7].bitcast(BF16)]
        self.ARENA_B = 90368
        self.arena = sb("arena", [128, self.ARENA_B], mybir.dt.uint8)
        self.arena_off = 0

    def name(self, p):
        self.uid += 1
        return f"{p}{self.uid}"

    def carve(self, kname, shape, dt=BF16):
        assert shape[0] == 128
        esz = {F32: 4, BF16: 2, mybir.dt.uint32: 4}[dt]
        n = 1
        for s in shape[1:]:
            n *= s
        off = (self.arena_off + 31) & ~31
        self.arena_off = off + n * esz
        assert self.arena_off <= self.ARENA_B, \
            f"arena overflow: {self.arena_off} > {self.ARENA_B} at {kname}"
        flat = self.arena.bitcast(dt)
        ap = flat[:, off // esz: off // esz + n]
        if len(shape) > 2:
            dims = " ".join(f"d{i}" for i in range(1, len(shape)))
            kw = {f"d{i}": shape[i] for i in range(1, len(shape))}
            ap = ap.rearrange(f"p ({dims}) -> p {dims}", **kw)
        self.P.arena_knames.add(kname)
        return ap, kname

    def arena_reset(self):
        P = self.P
        P.epoch_fence = list(P.last_per_stream.values())
        P.arena_knames = set()
        self.arena_off = 0


def load_const(C, key, ap_in, shape, dt=F32, stream="sync"):
    t = C.sb(C.name("c_"), shape, dt)
    C.P.dma(stream, t[:], ap_in, writes=[(id(t),)])
    return t


def emit_ln(C, b, j, eps, par, dram_out=None):
    P, nc = C.P, C.nc
    x = C.x_tm[:, b, j, :]
    xk = ("x", b, j)
    bn = C.bn12[:, par, :]
    mv = C.mv[:, par, :]
    nm = C.nm[:, par, :]
    sd = C.sd[:, par, :]
    rs = C.rs[:, par, :]
    xt = C.xf_tm[:, par, :]
    kbn, kmv, knm, ksd, krs, kxt = [("ln", par, i) for i in range(6)]
    P.op("vector", lambda e, o=bn[:, 0:6], i=x[:, 0:512]: e.bn_stats(o, i),
         reads=[xk], writes=[kbn])
    P.op("vector", lambda e, o=bn[:, 6:12], i=x[:, 512:1024]: e.bn_stats(o, i),
         reads=[xk], writes=[kbn])
    P.op("vector", lambda e, o=mv, i=bn: e.bn_aggr(o, i), reads=[kbn], writes=[kmv])
    P.op("vector", lambda e, o=nm, i=mv[:, 0:1]: e.tensor_scalar_mul(o, i, -1.0),
         reads=[kmv], writes=[knm])
    ecol = 0 if eps == LN_EPS else 1
    P.op("scalar", lambda e, o=sd, i=mv[:, 1:2], bb=C.epsv[:, ecol:ecol + 1]:
         e.activation(o, i, AF.Sqrt, bias=bb),
         reads=[kmv, ("epsv",)], writes=[ksd])
    P.op("vector", lambda e, o=rs, i=sd: e.reciprocal(o, i), reads=[ksd], writes=[krs])
    if par == 0:
        P.op("vector", lambda e, o=xt, i=x, a=nm, s=rs:
             e.tensor_scalar(o, i, a, s, ALU.add, ALU.mult),
             reads=[xk, knm, krs], writes=[kxt])
    else:
        P.op("vector", lambda e, o=sd, a=nm, b2=rs: e.tensor_mul(o, a, b2),
             reads=[knm, krs], writes=[ksd])
        P.op("scalar", lambda e, o=xt, i=x, bb=sd, s=rs:
             e.activation(o, i, AF.Identity, bias=bb, scale=s),
             reads=[xk, ksd, krs], writes=[kxt])
    if dram_out is not None:
        P.dma("sync", dram_out[(b * NTOK + j * 128):(b * NTOK + (j + 1) * 128), :],
              xt, reads=[kxt], writes=[("xfd",)])
    for c in range(NCH):
        pt = C.psT[c % 2][:, 0:128]
        kpt = ("psA", 6 + c % 2)
        P.op("tensor", lambda e, o=pt, i=xt[:, c * 128:(c + 1) * 128], idn=C.ident[:]:
             e.transpose(o, i, idn), reads=[kxt], writes=[kpt])
        dst = C.xf[:, c, b, PAD + j * 128: PAD + (j + 1) * 128]
        eng = "vector" if c % 2 == 0 else "scalar"
        if eng == "vector":
            P.op(eng, lambda e, o=dst, i=pt: e.tensor_copy(o, i),
                 reads=[kpt], writes=[("xf", c, b, j)])
        else:
            P.op(eng, lambda e, o=dst, i=pt: e.activation(o, i, AF.Copy),
                 reads=[kpt], writes=[("xf", c, b, j)])


def emit_ln_all(C, eps, dram_out=None):
    for b in range(B):
        for j in range(NT):
            emit_ln(C, b, j, eps, (b * NT + j) % 2, dram_out)


def xf_keys(C, b, t0, t1, shift=0):
    lo = max(0, (t0 + shift) // 128)
    hi = min(NT - 1, (t1 - 1 + shift) // 128)
    return [("xf", c, b, j) for c in range(NCH) for j in range(lo, hi + 1)]



def emit_s1(C, wr_ap, wk_ap, wv_ap, mu_ap):
    P, nc = C.P, C.nc
    C.arena_reset()
    w_r, kwr = C.carve(C.name("w_r"), [128, NCH, 1024], BF16)
    w_k, kwk = C.carve(C.name("w_k"), [128, NCH, 1024], BF16)
    w_v, kwv = C.carve(C.name("w_v"), [128, NCH, 1024], BF16)
    mu, kmu = C.carve(C.name("mu"), [128, 16], F32)
    for t, ap, key in ((w_r, wr_ap, kwr), (w_k, wk_ap, kwk), (w_v, wv_ap, kwv)):
        P.dma("gpsimd", t[:], ap.rearrange("(c p) m -> p c m", p=128), writes=[(key,)])
    P.dma("sync", mu[:], mu_ap, writes=[(kmu,)])
    P.op("vector", lambda e, o=w_v[:], i=w_v[:]: e.tensor_scalar_mul(o, i, 0.5),
         reads=[(kwv,)], writes=[(kwv,)])

    emit_ln_all(C, LN_EPS)

    xr, kxr = C.carve(C.name("xr"), [128, 2, NCH, 256], BF16)
    xk, kxk = C.carve(C.name("xk"), [128, 2, NCH, 256], BF16)
    dbuf, kd = C.carve(C.name("d"), [128, 2, 256], BF16)
    kk, kkk = C.carve(C.name("kk"), [128, 2, NCH, 256], BF16)
    krelu, kkr = C.carve(C.name("krelu"), [128, 2, 256], BF16)
    r_tm, krt = C.carve(C.name("r_tm"), [128, 2, 1024], F32)
    s1tmp, kst = C.carve(C.name("s1tmp"), [128, 2, 512], F32)

    GR1 = [(g * 256, min(g * 256 + 256, NTOK)) for g in range(5)]
    groups = [(b, t0, t1) for b in range(B) for (t0, t1) in GR1]

    def s1_mix(gidx):
        b, t0, t1 = groups[gidx]
        TT = t1 - t0
        gp = gidx % 2
        rdeps = xf_keys(C, b, t0, t1) + xf_keys(C, b, t0 - 1, t1 - 1) + [("xfpad",)]
        for c in range(NCH):
            cur = C.xf[:, c, b, PAD + t0: PAD + t1]
            prv = C.xf[:, c, b, PAD + t0 - 1: PAD + t1 - 1]
            P.op("vector", lambda e, o=dbuf[:, gp, 0:TT], i=cur, p=prv: e.tensor_sub(o, i, p),
                 reads=rdeps, writes=[(kd, gp)])
            P.op("vector", lambda e, o=xr[:, gp, c, 0:TT], i=dbuf[:, gp, 0:TT],
                 s=mu[:, c:c + 1], p=prv:
                 e.scalar_tensor_tensor(o, i, s, p, ALU.mult, ALU.add),
                 reads=[(kd, gp), (kmu,)] + rdeps, writes=[(kxr, gp, c)])
            P.op("vector", lambda e, o=xk[:, gp, c, 0:TT], i=dbuf[:, gp, 0:TT],
                 s=mu[:, 8 + c:9 + c], p=prv:
                 e.scalar_tensor_tensor(o, i, s, p, ALU.mult, ALU.add),
                 reads=[(kd, gp), (kmu,)] + rdeps, writes=[(kxk, gp, c)])

    def s1_comp(gidx):
        b, t0, t1 = groups[gidx]
        TT = t1 - t0
        gp = gidx % 2
        for co in range(NCH):
                kp = C.psA[co % 2]
                kkp = ("psA", co % 2)
                for ci in range(NCH):
                    P.op("tensor", lambda e, o=kp[:, 0:TT],
                         l=w_k[:, ci, co * 128:(co + 1) * 128],
                         r=xk[:, gp, ci, 0:TT], s=(ci == 0), t=(ci == NCH - 1):
                         e.matmul(o, l, r, start=s, stop=t),
                         reads=[(kwk,)] + [(kxk, gp, ci) for ci in range(NCH)], writes=[kkp])
                P.op("scalar", lambda e, o=krelu[:, gp, 0:TT], i=kp[:, 0:TT]:
                     e.activation(o, i, AF.Relu), reads=[kkp], writes=[(kkr, gp)])
                P.op("vector", lambda e, o=kk[:, gp, co, 0:TT], i=krelu[:, gp, 0:TT]:
                     e.tensor_mul(o, i, i), reads=[(kkr, gp)], writes=[(kkk, gp, co)])
        for jt in range(t0 // 128, t1 // 128):
            toff = jt * 128 - t0
            par = jt % 2
            rp = [C.psA[2], C.psA[3]]
            vp = [C.psA[4], C.psA[5]]
            for half in range(2):
                krp = ("psA", 2 + half)
                for ci in range(NCH):
                    P.op("tensor", lambda e, o=rp[half][:],
                         l=xr[:, gp, ci, toff:toff + 128],
                         r=w_r[:, ci, half * 512:(half + 1) * 512],
                         s=(ci == 0), t=(ci == NCH - 1):
                         e.matmul(o, l, r, start=s, stop=t),
                         reads=[(kwr,)] + [(kxr, gp, ci) for ci in range(NCH)], writes=[krp])
                P.op("scalar", lambda e, o=r_tm[:, par, half * 512:(half + 1) * 512],
                     i=rp[half][:]: e.activation(o, i, AF.Sigmoid),
                     reads=[krp], writes=[(krt, par)])
            for half in range(2):
                kvp = ("psA", 4 + half)
                for ck in range(NCH):
                    P.op("tensor", lambda e, o=vp[half][:],
                         l=kk[:, gp, ck, toff:toff + 128],
                         r=w_v[:, ck, half * 512:(half + 1) * 512],
                         s=(ck == 0), t=(ck == NCH - 1):
                         e.matmul(o, l, r, start=s, stop=t),
                         reads=[(kwv,)] + [(kkk, gp, ck2) for ck2 in range(NCH)], writes=[kvp])
                xsl = C.x_tm[:, b, jt, half * 512:(half + 1) * 512]
                kx = ("x", b, jt)
                P.op("vector", lambda e, o=s1tmp[:, half, :], i=vp[half][:],
                     r=r_tm[:, par, half * 512:(half + 1) * 512]: e.tensor_mul(o, i, r),
                     reads=[kvp, (krt, par)], writes=[(kst, half)])
                P.op("vector", lambda e, o=xsl, i=xsl, t2=s1tmp[:, half, :]:
                     e.tensor_add(o, i, t2),
                     reads=[kx, (kst, half)], writes=[kx])

    for gidx in range(len(groups)):
        s1_mix(gidx)
        if gidx > 0:
            s1_comp(gidx - 1)
    s1_comp(len(groups) - 1)



def emit_s2(C, wqkvT_ap, wprojT_ap, bqkv_ap, bproj_ap, msc_ap, ebc_ap):
    P, nc = C.P, C.nc
    C.arena_reset()
    wqkv, kW = C.carve(C.name("wqkv"), [128, NCH, 3072], BF16)
    wproj, kP2 = C.carve(C.name("wproj"), [128, NCH, 1024], BF16)
    bqkv, kB = C.carve(C.name("bqkv"), [128, 24], F32)
    msc, kM = C.carve(C.name("msc"), [128, NCH, 16], BF16)
    P.dma("gpsimd", wqkv[:], wqkvT_ap.rearrange("(c p) m -> p c m", p=128), writes=[(kW,)])
    P.dma("gpsimd", wproj[:], wprojT_ap.rearrange("(c p) m -> p c m", p=128), writes=[(kP2,)])
    P.dma("sync", bqkv[:], bqkv_ap, writes=[(kB,)])
    P.dma("sync", msc[:], msc_ap, writes=[(kM,)])
    bproj = C.sb(C.name("bproj_"), [1, 1024], BF16)
    ebc = C.sb(C.name("ebc_"), [16, NCH, 128], BF16)
    kBP, kE = (id(bproj),), (id(ebc),)
    P.dma("gpsimd", bproj[:], bproj_ap, writes=[kBP])
    P.dma("sync", ebc[:], ebc_ap, writes=[kE])

    emit_ln_all(C, LN_EPS)

    qkv, kQ = C.carve(C.name("qkv"), [128, 24, B, 128], BF16)
    ssb, kS = C.carve(C.name("ssb"), [128, 2 * 128], F32)
    dsb, kD2 = C.carve(C.name("dsb"), [128, 2 * 128], F32)
    asb, kA = C.carve(C.name("asb"), [128, 2 * 128], BF16)
    dv, kV = C.carve(C.name("dv"), [128, NCH, 128], BF16)
    pbuf, kPB = C.carve(C.name("pbuf"), [128, 4, 128], BF16)
    ofm, kO = C.carve(C.name("ofm"), [128, NCH, B, 128], BF16)
    otmp, kOT = C.carve(C.name("otmp"), [128, 128], BF16)
    TT = 128

    for jt in range(NT):
        t0 = jt * 128
        for b in range(B):
            for oc in range(24):
                pp = C.psA[oc % 2]
                kpp = ("psA", oc % 2)
                for ci in range(NCH):
                    P.op("tensor", lambda e, o=pp[:, 0:TT],
                         l=wqkv[:, ci, oc * 128:(oc + 1) * 128],
                         r=C.xf[:, ci, b, PAD + t0:PAD + t0 + TT],
                         s=(ci == 0), t=(ci == NCH - 1):
                         e.matmul(o, l, r, start=s, stop=t),
                         reads=[(kW,)] + [("xf", ci2, b, jt) for ci2 in range(NCH)],
                         writes=[kpp])
                P.op("scalar", lambda e, o=qkv[:, oc, b, :], i=pp[:, 0:TT],
                     bb=bqkv[:, oc:oc + 1]:
                     e.activation(o, i, AF.Identity, bias=bb),
                     reads=[kpp, (kB,)], writes=[(kQ, oc, b)])
        sps = {}
        for idx, (i, j) in enumerate([(0, 0), (0, 1), (1, 0), (1, 1)]):
            pp = C.psA[2 + idx]
            sps[(i, j)] = pp
            for c in range(NCH):
                pb = pbuf[:, c % 4, :]
                kpb = (kPB, c % 4)
                P.op("vector", lambda e, o=pb, a=qkv[:, c, i, :], b2=qkv[:, 8 + c, j, :]:
                     e.tensor_mul(o, a, b2),
                     reads=[(kQ, c, i), (kQ, 8 + c, j)], writes=[kpb])
                P.op("tensor", lambda e, o=pp[0:16, 0:TT], l=msc[:, c, :], r=pb,
                     s=(c == 0), t=(c == NCH - 1):
                     e.matmul(o, l, r, start=s, stop=t),
                     reads=[(kM,), kpb], writes=[("psA", 2 + idx)])
        for i in range(2):
            P.op("vector", lambda e, o=ssb[0:16, i * TT:(i + 1) * TT],
                 p=sps[(i, 1)][0:16, 0:TT]:
                 e.tensor_copy(o, p), reads=[("psA", 3 + 2 * i)], writes=[(kS, i)])
            P.op("vector", lambda e, o=dsb[0:16, i * TT:(i + 1) * TT],
                 p=sps[(i, 0)][0:16, 0:TT], s2=ssb[0:16, i * TT:(i + 1) * TT]:
                 e.tensor_sub(o, p, s2),
                 reads=[("psA", 2 + 2 * i), (kS, i)], writes=[(kD2, i)])
            P.op("scalar", lambda e, o=asb[0:16, i * TT:(i + 1) * TT],
                 i2=dsb[0:16, i * TT:(i + 1) * TT]:
                 e.activation(o, i2, AF.Sigmoid, scale=0.125),
                 reads=[(kD2, i)], writes=[(kA, i)])
        for c in range(NCH):
            P.op("vector", lambda e, o=dv[:, c, :], a=qkv[:, 16 + c, 0, :],
                 b2=qkv[:, 16 + c, 1, :]: e.tensor_sub(o, a, b2),
                 reads=[(kQ, 16 + c, 0), (kQ, 16 + c, 1)], writes=[(kV, c)])
        for i in range(2):
            for c in range(NCH):
                pp = C.psA[c % 2]
                kpp = ("psA", c % 2)
                P.op("tensor", lambda e, o=pp[:, 0:TT], l=ebc[:, c, :],
                     r=asb[0:16, i * TT:(i + 1) * TT]:
                     e.matmul(o, l, r, start=True, stop=True),
                     reads=[kE, (kA, i)], writes=[kpp])
                P.op("vector", lambda e, o=otmp[:], a=pp[:, 0:TT], b2=dv[:, c, :]:
                     e.tensor_mul(o, a, b2),
                     reads=[kpp, (kV, c)], writes=[(kOT,)])
                P.op("vector", lambda e, o=ofm[:, c, i, :], a=otmp[:],
                     b2=qkv[:, 16 + c, 1, :]: e.tensor_add(o, a, b2),
                     reads=[(kOT,), (kQ, 16 + c, 1)], writes=[(kO, c, i)])
        for b in range(B):
            for half in range(2):
                pp = C.psA[2 + 2 * (jt % 2) + half]
                kpp = ("psA", 2 + 2 * (jt % 2) + half)
                P.op("tensor", lambda e, o=pp[:], l=C.ones[0:1, :],
                     r=bproj[0:1, half * 512:(half + 1) * 512]:
                     e.matmul(o, l, r, start=True, stop=False),
                     reads=[("ones",), kBP], writes=[kpp])
                for c in range(NCH):
                    P.op("tensor", lambda e, o=pp[:], l=ofm[:, c, b, :],
                         r=wproj[:, c, half * 512:(half + 1) * 512],
                         t=(c == NCH - 1):
                         e.matmul(o, l, r, start=False, stop=t),
                         reads=[(kP2,)] + [(kO, c2, b) for c2 in range(NCH)],
                         writes=[kpp])
                xsl = C.x_tm[:, b, jt, half * 512:(half + 1) * 512]
                P.op("vector", lambda e, o=xsl, i=xsl, p=pp[:]: e.tensor_add(o, i, p),
                     reads=[("x", b, jt), kpp], writes=[("x", b, jt)])



def emit_moe(C, gw_ap, w1_ap, b1_ap, w2_ap, b2_ap):
    P, nc = C.P, C.nc
    C.arena_reset()
    gw, kG = C.carve(C.name("gw"), [128, NCH, 8], BF16)
    b1s, kB1 = C.carve(C.name("b1s"), [128, 8, 32], F32)
    P.dma("gpsimd", gw[:], gw_ap, writes=[(kG,)])
    P.dma("sync", b1s[:], b1_ap, writes=[(kB1,)])
    if "b2s" in C.smalls:
        b2s, gfm = C.smalls["b2s"], C.smalls["gfm"]
        kB2, kGF = (id(b2s),), (id(gfm),)
    else:
        b2s = C.smalls["b2s"] = C.sb(C.name("b2s_"), [8, 1024], BF16)
        gfm = C.smalls["gfm"] = C.sb(C.name("gfm_"), [8, 128], BF16)
        kB2, kGF = (id(b2s),), (id(gfm),)
        P.dma("gpsimd", b2s[:], b2_ap, writes=[kB2])

    emit_ln_all(C, LN_EPS)

    gates, kGT = C.carve(C.name("gates"), [128, B, NT, 8], F32)
    gsc, kGS = C.carve(C.name("gsc"), [128, 2, 32], F32)
    for b in range(B):
        for jt in range(NT):
            par = (b * NT + jt) % 2
            pp = C.psA[par]
            kpp = ("psA", par)
            for ci in range(NCH):
                P.op("tensor", lambda e, o=pp[:, 0:8], l=C.xf[:, ci, b, PAD + jt * 128:PAD + (jt + 1) * 128],
                     r=gw[:, ci, :], s=(ci == 0), t=(ci == NCH - 1):
                     e.matmul(o, l, r, start=s, stop=t),
                     reads=[(kG,)] + [("xf", c2, b, jt) for c2 in range(NCH)],
                     writes=[kpp])
            g = gsc[:, par, :]
            kg = (kGS, par)
            m1, m2, dif, wa, wb = (g[:, 0:1], g[:, 1:2], g[:, 2:3], g[:, 3:4], g[:, 4:5])
            eq1, eq2, msk, t1g = (g[:, 8:16], g[:, 16:24], g[:, 24:32], g[:, 5:6])
            P.op("vector", lambda e, o=m1, i=pp[:, 0:8]: e.reduce_max(o, i, axis=AX.X),
                 reads=[kpp], writes=[kg])
            P.op("vector", lambda e, o=eq1, i=pp[:, 0:8], s=m1:
                 e.tensor_scalar(o, i, s, 0.5, ALU.is_ge, ALU.mult),
                 reads=[kpp, kg], writes=[kg])
            P.op("vector", lambda e, o=msk, i=eq1, s=pp[:, 0:8]:
                 e.scalar_tensor_tensor(o, i, -1e9, s, ALU.mult, ALU.add),
                 reads=[kpp, kg], writes=[kg])
            P.op("vector", lambda e, o=m2, i=msk: e.reduce_max(o, i, axis=AX.X),
                 reads=[kg], writes=[kg])
            P.op("vector", lambda e, o=eq2, i=msk, s=m2:
                 e.tensor_scalar(o, i, s, 0.5, ALU.is_ge, ALU.mult),
                 reads=[kg], writes=[kg])
            P.op("vector", lambda e, o=dif, a=m1, b2_=m2: e.tensor_sub(o, a, b2_),
                 reads=[kg], writes=[kg])
            P.op("scalar", lambda e, o=wa, i=dif: e.activation(o, i, AF.Sigmoid),
                 reads=[kg], writes=[kg])
            P.op("scalar", lambda e, o=wb, i=dif: e.activation(o, i, AF.Sigmoid, scale=-1.0),
                 reads=[kg], writes=[kg])
            P.op("vector", lambda e, o=msk, i=eq1, s=wa:
                 e.tensor_scalar(o, i, s, None, ALU.mult),
                 reads=[kg], writes=[kg])
            P.op("vector", lambda e, o=gates[:, b, jt, :], i=eq2, s=wb, a=msk:
                 e.scalar_tensor_tensor(o, i, s, a, ALU.mult, ALU.add),
                 reads=[kg], writes=[(kGT, b, jt)])
            gcb, kgcb = (C.moe_gcb, ("moe_gcb", par))
            P.op("vector", lambda e, o=gcb[:, par, :], i=gates[:, b, jt, :]:
                 e.tensor_copy(o, i), reads=[(kGT, b, jt)], writes=[kgcb])
            pt = C.psT[par][:, 0:128]
            kpt = ("psA", 6 + par)
            P.op("tensor", lambda e, o=pt[0:8, :], i=gcb[:, par, :], idn=C.ident[:]:
                 e.transpose(o, i, idn), reads=[kgcb, ("ident",)], writes=[kpt])
            P.op("vector", lambda e, o=gfm[0:8, :], i=pt[0:8, :]: e.tensor_copy(o, i),
                 reads=[kpt], writes=[kGF])
            for half in range(2):
                bp = C.psA[2 + half]
                kbp = ("psA", 2 + half)
                P.op("tensor", lambda e, o=bp[:], l=gfm[0:8, :],
                     r=b2s[0:8, half * 512:(half + 1) * 512]:
                     e.matmul(o, l, r, start=True, stop=True),
                     reads=[kGF, kB2], writes=[kbp])
                xsl = C.x_tm[:, b, jt, half * 512:(half + 1) * 512]
                P.op("vector", lambda e, o=xsl, i=xsl, p=bp[:]: e.tensor_add(o, i, p),
                     reads=[("x", b, jt), kbp], writes=[("x", b, jt)])

    w1r = []
    w2r = []
    for par in range(2):
        t1, k1 = C.carve(C.name("w1r"), [128, NCH, 1024], BF16)
        t2, k2 = C.carve(C.name("w2r"), [128, NCH, 1024], BF16)
        w1r.append((t1, k1))
        w2r.append((t2, k2))
    h, kH = C.carve(C.name("h"), [128, NCH, 512], BF16)

    for e_ in range(8):
        for q in range(4):
            par = (e_ * 4 + q) % 2
            (w1t, kw1) = w1r[par]
            (w2t, kw2) = w2r[par]
            P.dma("gpsimd", w1t[:],
                  w1_ap[e_, :, q * 1024:(q + 1) * 1024].rearrange("(c p) m -> p c m", p=128),
                  writes=[(kw1,)])
            P.dma("gpsimd", w2t[:],
                  w2_ap[e_, q * 1024:(q + 1) * 1024, :].rearrange("(c p) m -> p c m", p=128),
                  writes=[(kw2,)])
            for b in range(B):
                for (t0, t1) in GRPS:
                    TT = t1 - t0
                    xfdeps = xf_keys(C, b, t0, t1)
                    for hc in range(NCH):
                        hp = C.psA[hc % 2]
                        khp = ("psA", hc % 2)
                        for ci in range(NCH):
                            P.op("tensor", lambda e, o=hp[:, 0:TT],
                                 l=w1t[:, ci, hc * 128:(hc + 1) * 128],
                                 r=C.xf[:, ci, b, PAD + t0:PAD + t1],
                                 s=(ci == 0), t=(ci == NCH - 1):
                                 e.matmul(o, l, r, start=s, stop=t),
                                 reads=[(kw1,)] + xfdeps, writes=[khp])
                        P.op("scalar", lambda e, o=h[:, hc, 0:TT], i=hp[:, 0:TT],
                             bb=b1s[:, e_, q * 8 + hc:q * 8 + hc + 1]:
                             e.activation(o, i, AF.Silu, bias=bb),
                             reads=[khp, (kB1,)], writes=[(kH, hc)])
                    for jt in range(t0 // 128, t1 // 128):
                        toff = jt * 128 - t0
                        yps = [C.psA[2 + 2 * (jt % 2)], C.psA[3 + 2 * (jt % 2)]]
                        kyps = [("psA", 2 + 2 * (jt % 2)), ("psA", 3 + 2 * (jt % 2))]
                        for hc in range(NCH):
                            for half in range(2):
                                P.op("tensor", lambda e, o=yps[half][:],
                                     l=h[:, hc, toff:toff + 128],
                                     r=w2t[:, hc, half * 512:(half + 1) * 512],
                                     s=(hc == 0), t=(hc == NCH - 1):
                                     e.matmul(o, l, r, start=s, stop=t),
                                     reads=[(kw2,)] + [(kH, h2) for h2 in range(NCH)],
                                     writes=[kyps[half]])
                        for half in range(2):
                            xsl = C.x_tm[:, b, jt, half * 512:(half + 1) * 512]
                            kx = ("x", b, jt)
                            P.op("vector", lambda e, o=xsl, i=yps[half][:],
                                 s=gates[:, b, jt, e_:e_ + 1], a=xsl:
                                 e.scalar_tensor_tensor(o, i, s, a, ALU.mult, ALU.add),
                                 reads=[kyps[half], (kGT, b, jt), kx], writes=[kx])


GRP = 384
NG = 19
NTAB = NG * GRP
WROWS = 8 * 8 * 128


def moe_dram(C):
    if not hasattr(C, "xfd"):
        nc = C.nc
        C.xfd = nc.dram_tensor("xfd", [B * NTOK, 1024], BF16, kind="Internal").ap()
        C.tabd = nc.dram_tensor("tabd", [NTAB, 2], F32, kind="Internal").ap()
        C.yslot = nc.dram_tensor("yslot", [NTAB, 1024], F32, kind="Internal").ap()
    return C.xfd, C.tabd, C.yslot


def moe_regs(C):
    if not hasattr(C, "moeregs"):
        C.moeregs = {}

        def setup(e, d=C.moeregs):
            for nm, val in (("tab", NTAB - 1), ("tok", B * NTOK - 1),
                            ("wrow", WROWS - 1), ("b2row", 7)):
                r = e.alloc_register("bc_" + nm)
                inst = e.reg_mov(r, val)
                d[nm] = r
            return inst

        C.P.op("gpsimd", setup)
    return C.moeregs


def emit_moe_sparse(C, gw_ap, w1s_ap, w2s_ap, b2_ap,
                    l2_ap, l18_ap, if32_ap, rconst_ap,
                    thr_ap, iotag_ap, iotap_ap):
    P, nc = C.P, C.nc
    xfd, tabd, yslot = moe_dram(C)
    rgs = moe_regs(C)
    C.arena_reset()
    gw, kG = C.carve(C.name("gw"), [128, NCH, 8], BF16)
    l2c, kL2 = C.carve(C.name("l2c"), [128, 128], F32)
    if32, kI32 = C.carve(C.name("if32"), [128, 128], F32)
    l18, kL18 = C.carve(C.name("l18"), [128, 18], F32)
    rcn, kRC = C.carve(C.name("rcn"), [128, B, NT, 3], F32)
    thr, kTH = C.carve(C.name("thr"), [128, 11], F32)
    iotag, kIG = C.carve(C.name("iotag"), [128, NG], F32)
    iotap, kIP = C.carve(C.name("iotap"), [128, NG], F32)
    twos, kTW = C.carve(C.name("twos"), [128, 1], F32)
    onesr, kOR = C.carve(C.name("onesr"), [128, 128], F32)
    P.dma("gpsimd", gw[:], gw_ap, writes=[(kG,)])
    P.dma("sync", l2c[:], l2_ap, writes=[(kL2,)])
    P.dma("sync", if32[:], if32_ap, writes=[(kI32,)])
    P.dma("sync", l18[0:18, 0:18], l18_ap, writes=[(kL18,)])
    P.dma("sync", rcn[:], rconst_ap, writes=[(kRC,)])
    P.dma("sync", thr[:], thr_ap, writes=[(kTH,)])
    P.dma("sync", iotag[:], iotag_ap, writes=[(kIG,)])
    P.dma("sync", iotap[:], iotap_ap, writes=[(kIP,)])
    P.op("vector", lambda e: e.memset(twos[:], 2.0), writes=[(kTW,)])
    P.op("vector", lambda e: e.memset(onesr[:], 1.0), writes=[(kOR,)])
    if "b2s" in C.smalls:
        b2s = C.smalls["b2s"]
        kB2 = (id(b2s),)
    else:
        b2s = C.smalls["b2s"] = C.sb(C.name("b2s_"), [8, 1024], BF16)
        kB2 = (id(b2s),)
        P.dma("gpsimd", b2s[:], b2_ap, writes=[kB2])

    pre, kPRE = C.carve(C.name("pre"), [128, NTAB // 128, 2], F32)
    P.op("vector", lambda e: e.memset(pre[:], 1.0e9), writes=[(kPRE,)])
    P.dma("gpsimd", tabd.rearrange("(t p) w -> p t w", p=128), pre[:],
          reads=[(kPRE,)], writes=[("tabd",)])

    emit_ln_all(C, LN_EPS, dram_out=xfd)

    gates, kGT = C.carve(C.name("gates"), [128, B, NT, 8], F32)
    asn12, kAS = C.carve(C.name("asn12"), [128, B, NT, 16], F32)
    gsc, kGS = C.carve(C.name("gsc"), [128, 2, 32], F32)
    asnc, kAC = C.carve(C.name("asnc"), [128, 2, 8], F32)

    for b in range(B):
        for jt in range(NT):
            ji = b * NT + jt
            par = ji % 2
            pp = C.psA[par]
            kpp = ("psA", par)
            for ci in range(NCH):
                P.op("tensor", lambda e, o=pp[:, 0:8], l=C.xf[:, ci, b, PAD + jt * 128:PAD + (jt + 1) * 128],
                     r=gw[:, ci, :], s=(ci == 0), t=(ci == NCH - 1):
                     e.matmul(o, l, r, start=s, stop=t),
                     reads=[(kG,)] + [("xf", c2, b, jt) for c2 in range(NCH)],
                     writes=[kpp])
            g = gsc[:, par, :]
            kg = (kGS, par)
            m1, m2, dif, wa, wb = (g[:, 0:1], g[:, 1:2], g[:, 2:3], g[:, 3:4], g[:, 4:5])
            msk = g[:, 24:32]
            eq1 = asn12[:, b, jt, 0:8]
            eq2 = asn12[:, b, jt, 8:16]
            kq = (kAS, ji)
            v05 = rcn[:, b, jt, 0:1]
            P.op("vector", lambda e, o=m1, i=pp[:, 0:8]: e.reduce_max(o, i, axis=AX.X),
                 reads=[kpp], writes=[kg])
            P.op("vector", lambda e, o=eq1, i=pp[:, 0:8], s=m1, s2=v05:
                 e.tensor_scalar(o, i, s, s2, ALU.is_ge, ALU.mult),
                 reads=[kpp, kg, (kRC,)], writes=[kq])
            P.op("vector", lambda e, o=msk, i=eq1, s=pp[:, 0:8]:
                 e.scalar_tensor_tensor(o, i, -2e9, s, ALU.mult, ALU.add),
                 reads=[kpp, kq], writes=[kg])
            P.op("vector", lambda e, o=m2, i=msk: e.reduce_max(o, i, axis=AX.X),
                 reads=[kg], writes=[kg])
            P.op("vector", lambda e, o=eq2, i=msk, s=m2, s2=v05:
                 e.tensor_scalar(o, i, s, s2, ALU.is_ge, ALU.mult),
                 reads=[kg, (kRC,)], writes=[kq])
            P.op("vector", lambda e, o=dif, a=m1, b2_=m2: e.tensor_sub(o, a, b2_),
                 reads=[kg], writes=[kg])
            P.op("scalar", lambda e, o=wa, i=dif: e.activation(o, i, AF.Sigmoid),
                 reads=[kg], writes=[kg])
            P.op("scalar", lambda e, o=wb, i=dif: e.activation(o, i, AF.Sigmoid, scale=-1.0),
                 reads=[kg], writes=[kg])
            P.op("vector", lambda e, o=msk, i=eq1, s=wa:
                 e.tensor_scalar(o, i, s, None, ALU.mult),
                 reads=[kg, kq], writes=[kg])
            P.op("vector", lambda e, o=gates[:, b, jt, :], i=eq2, s=wb, a=msk:
                 e.scalar_tensor_tensor(o, i, s, a, ALU.mult, ALU.add),
                 reads=[kg, kq], writes=[(kGT, b, jt)])
            P.op("vector", lambda e, o=asnc[:, par, :], a=eq1, b2_=eq2:
                 e.tensor_add(o, a, b2_), reads=[kq], writes=[(kAC, par)])
            P.op("tensor", lambda e, o=C.psA[4][0:8, ji:ji + 1], l=asnc[:, par, :],
                 r=twos[:], s=True, t=True: e.matmul(o, l, r, start=s, stop=t),
                 reads=[(kAC, par), (kTW,)], writes=[("psA", 4)])

    cnt_sb, kCS = C.carve(C.name("cnt_sb"), [128, 18], F32)
    cstat, kCT2 = C.carve(C.name("cstat"), [128, 16], F32)
    ge11, kGE = C.carve(C.name("ge11"), [128, 11], F32)
    cntT, kCT = C.carve(C.name("cntT"), [128, 8], F32)
    baseT, kBT = C.carve(C.name("baseT"), [128, 8], F32)
    bases18, kBA = C.carve(C.name("bases18"), [128, 8], F32)
    geof, kGO = C.carve(C.name("geof"), [128, NG], F32)
    eofr, kEO = C.carve(C.name("eofr"), [128, NG], F32)
    widxf, kWF = C.carve(C.name("widxf"), [128, 8, NG], F32)
    widxu, kWU = C.carve(C.name("widxu"), [128, 8, NG], mybir.dt.uint32)
    gebase, kGB = C.carve(C.name("gebase"), [128, NG], F32)
    ohg, kOH = C.carve(C.name("ohg"), [128, NG], F32)

    P.op("vector", lambda e, o=cnt_sb[0:8, :], i=C.psA[4][0:8, 0:18]:
         e.tensor_copy(o, i), reads=[("psA", 4)], writes=[(kCS,)])
    P.op("vector", lambda e, o=cstat[0:8, 0:1], i=cnt_sb[0:8, :]:
         e.reduce_sum(o, i, axis=AX.X), reads=[(kCS,)], writes=[(kCT2,)])
    P.op("vector", lambda e, o=ge11[0:8, :], i=thr[0:8, :], s=cstat[0:8, 0:1]:
         e.tensor_scalar(o, i, s, None, ALU.is_le),
         reads=[(kTH,), (kCT2,)], writes=[(kGE,)])
    P.op("vector", lambda e, o=cstat[0:8, 1:2], i=ge11[0:8, :]:
         e.reduce_sum(o, i, axis=AX.X), reads=[(kGE,)], writes=[(kCT2,)])
    P.op("tensor", lambda e, o=C.psA[5][0:8, 0:1], l=l18[0:8, 0:8],
         r=cstat[0:8, 1:2], s=True, t=True: e.matmul(o, l, r, start=s, stop=t),
         reads=[(kL18,), (kCT2,)], writes=[("psA", 5)])
    P.op("vector", lambda e, o=cstat[0:8, 2:3], i=C.psA[5][0:8, 0:1]:
         e.tensor_copy(o, i), reads=[("psA", 5)], writes=[(kCT2,)])
    P.op("vector", lambda e, o=cstat[0:8, 3:4], a=cstat[0:8, 2:3], b2_=cstat[0:8, 1:2]:
         e.tensor_add(o, a, b2_), reads=[(kCT2,)], writes=[(kCT2,)])
    P.op("vector", lambda e, o=cstat[0:8, 4:5], i=cstat[0:8, 2:3]:
         e.tensor_scalar_mul(o, i, float(GRP)), reads=[(kCT2,)], writes=[(kCT2,)])
    P.op("tensor", lambda e, o=C.psA[6][0:18, 0:8],
         i=cnt_sb[0:8, 0:18], idn=if32[0:8, 0:8]: e.transpose(o, i, idn),
         reads=[(kCS,), (kI32,)], writes=[("psA", 6)])
    P.op("vector", lambda e, o=cntT[0:18, :], i=C.psA[6][0:18, 0:8]:
         e.tensor_copy(o, i), reads=[("psA", 6)], writes=[(kCT,)])
    P.op("tensor", lambda e, o=C.psA[7][0:1, 0:8], i=cstat[0:8, 4:5],
         idn=if32[0:8, 0:8]: e.transpose(o, i, idn),
         reads=[(kCT2,), (kI32,)], writes=[("psA", 7)])
    P.op("vector", lambda e, o=baseT[0:1, :], i=C.psA[7][0:1, 0:8]:
         e.tensor_copy(o, i), reads=[("psA", 7)], writes=[(kBT,)])
    P.op("tensor", lambda e, o=C.psA[2][0:18, 0:8], l=l18[0:18, 0:18],
         r=cntT[0:18, :], s=True, t=False: e.matmul(o, l, r, start=s, stop=t),
         reads=[(kL18,), (kCT,)], writes=[("psA", 2)])
    P.op("tensor", lambda e, o=C.psA[2][0:18, 0:8], l=onesr[0:1, 0:18],
         r=baseT[0:1, :], s=False, t=True: e.matmul(o, l, r, start=s, stop=t),
         reads=[(kOR,), (kBT,)], writes=[("psA", 2)])
    P.op("vector", lambda e, o=bases18[0:18, :], i=C.psA[2][0:18, 0:8]:
         e.tensor_copy(o, i), reads=[("psA", 2)], writes=[(kBA,)])
    P.op("vector", lambda e, o=geof[0:8, :], i=iotag[0:8, :], s=cstat[0:8, 3:4]:
         e.tensor_scalar(o, i, s, None, ALU.is_ge),
         reads=[(kIG,), (kCT2,)], writes=[(kGO,)])
    P.op("tensor", lambda e, o=C.psA[5][0:1, 0:NG], l=onesr[0:8, 0:1],
         r=geof[0:8, :], s=True, t=True: e.matmul(o, l, r, start=s, stop=t),
         reads=[(kOR,), (kGO,)], writes=[("psA", 5)])
    P.op("vector", lambda e, o=eofr[0:1, :], i=C.psA[5][0:1, 0:NG]:
         e.tensor_copy(o, i), reads=[("psA", 5)], writes=[(kEO,)])
    P.op("tensor", lambda e, o=C.psA[5][0:128, 0:NG], l=onesr[0:1, :],
         r=eofr[0:1, :], s=True, t=True: e.matmul(o, l, r, start=s, stop=t),
         reads=[(kOR,), (kEO,)], writes=[("psA", 5)])
    P.op("vector", lambda e, o=gebase[0:8, :], i=iotag[0:8, :], s=cstat[0:8, 2:3]:
         e.tensor_scalar(o, i, s, None, ALU.is_ge),
         reads=[(kIG,), (kCT2,)], writes=[(kGB,)])
    P.op("vector", lambda e, o=ohg[0:8, :], a=gebase[0:8, :], b2_=geof[0:8, :]:
         e.tensor_sub(o, a, b2_), reads=[(kGB,), (kGO,)], writes=[(kOH,)])
    P.op("vector", lambda e, o=widxf[:, 0, :], i=C.psA[5][0:128, 0:NG],
         t2=iotap[:]: e.scalar_tensor_tensor(o, i, 1024.0, t2, ALU.mult, ALU.add),
         reads=[("psA", 5), (kIP,)], writes=[(kWF,)])
    for f in range(1, 8):
        P.op("vector", lambda e, o=widxf[:, f, :], i=widxf[:, 0, :], v=float(f * 128):
             e.tensor_scalar_add(o, i, v), reads=[(kWF,)], writes=[(kWF,)])
    P.op("vector", lambda e, o=widxu[:], i=widxf[:]: e.tensor_copy(o, i),
         reads=[(kWF,)], writes=[(kWU,)])

    brow, kBR = C.carve(C.name("brow"), [128, 2, 8], F32)
    asn2t, kA2 = C.carve(C.name("asn2t"), [128, 2, 16], F32)
    prodt, kPR = C.carve(C.name("prodt"), [128, 2, 8], F32)
    gp8, kGP = C.carve(C.name("gp8"), [128, 2, 8], F32)
    idxg, kIX = C.carve(C.name("idxg"), [128, 2, 2, 3], F32)
    slots_tok, kST = C.carve(C.name("slots_tok"), [128, B, NT, 2], mybir.dt.uint32)

    for b in range(B):
        for jt in range(NT):
            ji = b * NT + jt
            par = ji % 2
            P.op("vector", lambda e, o=asnc[:, par, :], a=asn12[:, b, jt, 0:8],
                 b2_=asn12[:, b, jt, 8:16]: e.tensor_add(o, a, b2_),
                 reads=[(kAS, ji)], writes=[(kAC, par)])
            P.op("tensor", lambda e, o=C.psA[3][0:1, 0:8], l=if32[0:18, ji:ji + 1],
                 r=bases18[0:18, :], s=True, t=True: e.matmul(o, l, r, start=s, stop=t),
                 reads=[(kI32,), (kBA,)], writes=[("psA", 3)])
            P.op("vector", lambda e, o=brow[0:1, par, :], i=C.psA[3][0:1, 0:8]:
                 e.tensor_copy(o, i), reads=[("psA", 3)], writes=[(kBR, par)])
            sg = C.psA[2]
            ksg = ("psA", 2)
            P.op("tensor", lambda e, o=sg[:, 0:8], l=l2c[:], r=asnc[:, par, :],
                 s=True, t=False: e.matmul(o, l, r, start=s, stop=t),
                 reads=[(kL2,), (kAC, par)], writes=[ksg])
            P.op("tensor", lambda e, o=sg[:, 0:8], l=onesr[0:1, :],
                 r=brow[0:1, par, :], s=False, t=True: e.matmul(o, l, r, start=s, stop=t),
                 reads=[(kOR,), (kBR, par)], writes=[ksg])
            P.op("vector", lambda e, o=asn2t[:, par, :], i=asn12[:, b, jt, :]:
                 e.tensor_scalar_mul(o, i, 2.0), reads=[(kAS, ji)], writes=[(kA2, par)])
            for c in range(2):
                P.op("vector", lambda e, o=prodt[:, par, :], a=asn2t[:, par, c * 8:(c + 1) * 8],
                     b2_=sg[:, 0:8]: e.tensor_mul(o, a, b2_),
                     reads=[(kA2, par), ksg], writes=[(kPR, par)])
                P.op("vector", lambda e, o=idxg[:, par, c, 2:3],
                     i=prodt[:, par, :]:
                     e.reduce_sum(o, i, axis=AX.X), reads=[(kPR, par)], writes=[(kIX, par, c)])
                P.op("vector", lambda e, o=idxg[:, par, c, 2:3], i=idxg[:, par, c, 2:3],
                     a=rcn[:, b, jt, 1:2]: e.tensor_add(o, i, a),
                     reads=[(kIX, par, c), (kRC,)], writes=[(kIX, par, c)])
                P.op("vector", lambda e, o=idxg[:, par, c, 0:1], i=rcn[:, b, jt, 2:3]:
                     e.tensor_copy(o, i), reads=[(kRC,)], writes=[(kIX, par, c)])
                P.op("vector", lambda e, o=gp8[:, par, :], a=asn2t[:, par, c * 8:(c + 1) * 8],
                     b2_=gates[:, b, jt, :]: e.tensor_mul(o, a, b2_),
                     reads=[(kA2, par), (kGT, b, jt)], writes=[(kGP, par)])
                P.op("vector", lambda e, o=idxg[:, par, c, 1:2], i=gp8[:, par, :]:
                     e.reduce_sum(o, i, axis=AX.X), reads=[(kGP, par)], writes=[(kIX, par, c)])
                P.op("vector", lambda e, o=slots_tok[:, b, jt, c:c + 1], i=idxg[:, par, c, 2:3]:
                     e.tensor_copy(o, i), reads=[(kIX, par, c)], writes=[(kST, ji, c)])
                P.op("gpsw",
                     lambda e, o=tabd, ix=slots_tok[:, b, jt, c:c + 1], i=idxg[:, par, c, 0:2],
                     rg=rgs:
                     e.indirect_dma_start(out=o, out_offset=bass.IndirectOffsetOnAxis(ix, 0),
                                          in_=i, in_offset=None,
                                          bounds_check=rg["tab"], oob_is_err=False),
                     reads=[(kST, ji, c), (kIX, par, c)], writes=[("tabd",)],
                     is_dma=True)

    tg, kTGb = C.carve(C.name("tg"), [128, 2, 3, 2], F32)
    idxu, kIU = C.carve(C.name("idxu"), [128, 2, 3], mybir.dt.uint32)
    xg_tm, kXT = C.carve(C.name("xg_tm"), [128, 3, 1024], BF16)
    xg_fm, kXF = C.carve(C.name("xg_fm"), [128, 2, NCH, GRP], BF16)
    h_sb, kH = C.carve(C.name("h_sb"), [128, 2, 4, GRP], BF16)
    ygs, kYG = C.carve(C.name("ygs"), [128, 3, 1024], F32)
    w1g, kW1 = C.carve(C.name("w1g"), [128, 2, 4104], BF16)
    w1b = w1g.bitcast(F32)
    w2g, kW2 = C.carve(C.name("w2g"), [128, 2, 4096], BF16)
    ohb, kOB = C.carve(C.name("ohb"), [128, 2, 128], BF16)
    w1p = w1g[:, :, 0:4096].rearrange("p r (cp i m) -> p r cp i m", cp=4, i=2, m=512)
    w2p = w2g.rearrange("p r (cp i m) -> p r cp i m", cp=2, i=2, m=1024)

    def g_gather(gidx):
        parg = gidx % 2
        P.dma("sync", tg[:, parg, :, :],
              tabd[gidx * GRP:(gidx + 1) * GRP, :].rearrange("(t p) w -> p t w", p=128),
              reads=[("tabd",)], writes=[(kTGb, parg)])
        P.op("vector", lambda e, o=idxu[:, parg, :], i=tg[:, parg, :, 0]:
             e.tensor_copy(o, i), reads=[(kTGb, parg)], writes=[(kIU, parg)])
        for ct in range(3):
            pt2 = ct
            P.op("gpsw",
                 lambda e, o=xg_tm[:, pt2, :], ix=idxu[:, parg, ct:ct + 1], rg=rgs:
                 e.indirect_dma_start(out=o, out_offset=None, in_=xfd,
                                      in_offset=bass.IndirectOffsetOnAxis(ix, 0),
                                      bounds_check=rg["tok"], oob_is_err=False),
                 reads=[(kIU, parg), ("xfd",)], writes=[(kXT, pt2)], is_dma=True)

    def g_transpose(gidx):
        parg = gidx % 2
        for ct in range(3):
            pt2 = ct
            for cc in range(NCH):
                pt = C.psT[cc % 2][:, 0:128]
                kpt = ("psA", 6 + cc % 2)
                P.op("tensor", lambda e, o=pt, i=xg_tm[:, pt2, cc * 128:(cc + 1) * 128],
                     idn=C.ident[:]: e.transpose(o, i, idn),
                     reads=[(kXT, pt2), ("ident",)], writes=[kpt])
                P.op("vector", lambda e, o=xg_fm[:, parg, cc, ct * 128:(ct + 1) * 128],
                     i=pt: e.tensor_copy(o, i), reads=[kpt], writes=[(kXF, parg, cc)])

    for gidx in range(NG):
        parg = gidx % 2
        if gidx == 0:
            g_gather(0)
        g_transpose(gidx)
        if gidx + 1 < NG:
            g_gather(gidx + 1)
        P.op("vector", lambda e, o=ohb[0:8, parg, :], i=onesr[0:8, 0:128],
             s=ohg[0:8, gidx:gidx + 1]:
             e.tensor_scalar(o, i, s, None, ALU.mult),
             reads=[(kOR,), (kOH,)], writes=[(kOB, parg)])
        def emit_w2(f):
            parf = f % 2
            for ct in range(3):
                for half in range(2):
                    yb = C.psA[ct * 2 + half]
                    kyb = ("psA", ct * 2 + half)
                    for hc in range(4):
                        P.op("tensor", lambda e, o=yb[:],
                             l=h_sb[:, parf, hc, ct * 128:(ct + 1) * 128],
                             r=w2g[:, parf, hc * 1024 + half * 512: hc * 1024 + (half + 1) * 512],
                             s=(f == 0 and hc == 0):
                             e.matmul(o, l, r, start=s, stop=False),
                             reads=[(kW2, parf)] + [(kH, parf, h2) for h2 in range(4)],
                             writes=[kyb])

        for f in range(8):
            parf = f % 2
            P.op("gpsw",
                 lambda e, o=w1g[:, parf, :], ix=widxu[:, f, gidx:gidx + 1], rg=rgs:
                 e.indirect_dma_start(out=o, out_offset=None, in_=w1s_ap,
                                      in_offset=bass.IndirectOffsetOnAxis(ix, 0),
                                      bounds_check=rg["wrow"], oob_is_err=False),
                 reads=[(kWU,)], writes=[(kW1, parf)], is_dma=True)
            P.op("gpsw",
                 lambda e, o=w2g[:, parf, :], ix=widxu[:, f, gidx:gidx + 1], rg=rgs:
                 e.indirect_dma_start(out=o, out_offset=None, in_=w2s_ap,
                                      in_offset=bass.IndirectOffsetOnAxis(ix, 0),
                                      bounds_check=rg["wrow"], oob_is_err=False),
                 reads=[(kWU,)], writes=[(kW2, parf)], is_dma=True)
            for hc in range(4):
                hp = C.psA[6 + hc % 2]
                khp = ("psA", 6 + hc % 2)
                for ci in range(NCH):
                    P.op("tensor", lambda e, o=hp[:, 0:GRP],
                         l=w1g[:, parf, ci * 512 + hc * 128: ci * 512 + (hc + 1) * 128],
                         r=xg_fm[:, parg, ci, :],
                         s=(ci == 0), t=(ci == NCH - 1):
                         e.matmul(o, l, r, start=s, stop=t),
                         reads=[(kW1, parf)] + [(kXF, parg, c2) for c2 in range(NCH)],
                         writes=[khp])
                P.op("scalar", lambda e, o=h_sb[:, parf, hc, :], i=hp[:, 0:GRP],
                     bb=w1b[:, parf, 2048 + hc:2049 + hc]:
                     e.activation(o, i, AF.Silu, bias=bb),
                     reads=[khp, (kW1, parf)], writes=[(kH, parf, hc)])
            if f > 0:
                emit_w2(f - 1)
        emit_w2(7)
        for ct in range(3):
            for half in range(2):
                yb = C.psA[ct * 2 + half]
                kyb = ("psA", ct * 2 + half)
                P.op("tensor", lambda e, o=yb[:], l=ohb[0:8, parg, :],
                     r=b2s[0:8, half * 512:(half + 1) * 512]:
                     e.matmul(o, l, r, start=False, stop=True),
                     reads=[(kOB, parg), kB2], writes=[kyb])
        for ct in range(3):
            for half in range(2):
                yb = C.psA[ct * 2 + half]
                kyb = ("psA", ct * 2 + half)
                P.op("vector", lambda e, o=ygs[:, ct, half * 512:(half + 1) * 512],
                     i=yb[:], s=tg[:, parg, ct, 1:2]:
                     e.tensor_scalar(o, i, s, None, ALU.mult),
                     reads=[kyb, (kTGb, parg)], writes=[(kYG, ct)])
        P.dma("scalar", yslot[gidx * GRP:(gidx + 1) * GRP, :].rearrange("(t p) d -> p t d", p=128),
              ygs[:], reads=[(kYG, 0), (kYG, 1), (kYG, 2)], writes=[("yslot",)])

    rb, kRB = C.carve(C.name("rb"), [128, 2, 1024], F32)
    for b in range(B):
        for jt in range(NT):
            for c in range(2):
                par = c
                P.op("gpsw",
                     lambda e, o=rb[:, par, :], ix=slots_tok[:, b, jt, c:c + 1], rg=rgs:
                     e.indirect_dma_start(out=o, out_offset=None, in_=yslot,
                                          in_offset=bass.IndirectOffsetOnAxis(ix, 0),
                                          bounds_check=rg["tab"], oob_is_err=False),
                     reads=[(kST, b * NT + jt, c), ("yslot",)], writes=[(kRB, par)],
                     is_dma=True)
                xsl = C.x_tm[:, b, jt, :]
                P.op("vector", lambda e, o=xsl, i=xsl, a=rb[:, par, :]:
                     e.tensor_add(o, i, a),
                     reads=[("x", b, jt), (kRB, par)], writes=[("x", b, jt)])



def emit_s5(C, wq_ap, wk_ap, wv_ap, wo_ap, qkn_ap, alibi_ap, alibi0_ap):
    P, nc = C.P, C.nc
    C.arena_reset()
    ws = {}
    for nm, ap in (("wq", wq_ap), ("wk", wk_ap), ("wv", wv_ap), ("wo", wo_ap)):
        t, k = C.carve(C.name(nm), [128, NCH, 1024], BF16)
        P.dma("gpsimd", t[:], ap.rearrange("(c p) m -> p c m", p=128), writes=[(k,)])
        ws[nm] = (t, k)
    qkn, kN = C.carve(C.name("qkn"), [128, 16], F32)
    P.dma("sync", qkn[:], qkn_ap, writes=[(kN,)])
    alibi, kAL = C.carve(C.name("alibi"), [128, 8, 144], BF16)
    P.dma("sync", alibi[:], alibi_ap, writes=[(kAL,)])
    alibi0, kAL0 = C.carve(C.name("alibi0"), [128, 8, 48], BF16)
    P.dma("sync", alibi0[:], alibi0_ap, writes=[(kAL0,)])

    emit_ln_all(C, LN_EPS)

    q_fm, kQF = C.carve(C.name("q_fm"), [128, NCH, 128], BF16)
    k_fm, kKF = C.carve(C.name("k_fm"), [128, NCH, 3, 128], BF16)
    v_tm, kVT = C.carve(C.name("v_tm"), [128, 3, 1024], BF16)
    at_fm, kAT = C.carve(C.name("at_fm"), [128, NCH, 128], BF16)
    pbn, kPN = C.carve(C.name("pbn"), [128, 3, 144], BF16)
    pta, kTA = C.carve(C.name("pta"), [128, 3, 128], BF16)
    ptb, kTB = C.carve(C.name("ptb"), [128, 3, 128], BF16)
    qs_tm, kQS = C.carve(C.name("qs_tm"), [128, 1024], BF16)
    st5, kS5 = C.carve(C.name("st5"), [128, 4, 8], F32)

    def proj_rms(b, tok0, dst, kdst, dstoff, wname, normcol, par):
        nt = min(128, NTOK - tok0)
        (wt, kw) = ws[wname]
        st = st5[:, par, :]
        kst = (kS5, par)
        xdeps = xf_keys(C, b, max(tok0, 0), tok0 + nt) + [("xfpad",)]
        pq = [C.psA[4], C.psA[5]]
        for half in range(2):
            kqp = ("psA", 4 + half)
            for ci in range(NCH):
                P.op("tensor", lambda e, o=pq[half][0:nt, :],
                     l=C.xf[:, ci, b, PAD + tok0:PAD + tok0 + nt],
                     r=wt[:, ci, half * 512:(half + 1) * 512],
                     s=(ci == 0), t=(ci == NCH - 1):
                     e.matmul(o, l, r, start=s, stop=t),
                     reads=[(kw,)] + xdeps, writes=[kqp])
        if normcol is None:
            for half in range(2):
                P.op("scalar", lambda e, o=dst[:, dstoff + half * 512:dstoff + (half + 1) * 512],
                     i=pq[half][:]: e.activation(o, i, AF.Identity),
                     reads=[("psA", 4 + half)], writes=[(kdst, dstoff)])
            return
        for half in range(2):
            P.op("scalar", lambda e, o=qs_tm[0:nt, half * 512:(half + 1) * 512],
                 i=pq[half][0:nt, :], a=st[0:nt, half:half + 1]:
                 e.activation(o, i, AF.Square, accum_out=a),
                 reads=[("psA", 4 + half)], writes=[kst, (kQS, "scratch")])
        P.op("vector", lambda e, o=st[0:nt, 2:3], a=st[0:nt, 0:1], b2=st[0:nt, 1:2]:
             e.tensor_add(o, a, b2), reads=[kst], writes=[kst])
        P.op("scalar", lambda e, o=st[0:nt, 3:4], i=st[0:nt, 2:3], bb=C.epsv[0:nt, 1:2]:
             e.activation(o, i, AF.Sqrt, scale=1.0 / 1024.0, bias=bb),
             reads=[kst, ("epsv",)], writes=[kst])
        P.op("vector", lambda e, o=st[0:nt, 4:5], i=st[0:nt, 3:4]: e.reciprocal(o, i),
             reads=[kst], writes=[kst])
        for half in range(2):
            P.op("vector", lambda e, o=qs_tm[0:nt, half * 512:(half + 1) * 512],
                 i=pq[half][0:nt, :], s=st[0:nt, 4:5]:
                 e.tensor_scalar(o, i, s, None, ALU.mult),
                 reads=[("psA", 4 + half), kst, (kQS, "scratch")], writes=[(kQS,)])
        for c in range(NCH):
            pt = C.psT[c % 2][:, 0:128]
            kpt = ("psA", 6 + c % 2)
            P.op("tensor", lambda e, o=pt[:, 0:nt], i=qs_tm[0:nt, c * 128:(c + 1) * 128],
                 idn=C.ident[0:nt, 0:nt]: e.transpose(o, i, idn),
                 reads=[(kQS,)], writes=[kpt])
            P.op("vector", lambda e, o=dst[:, c, dstoff:dstoff + nt], i=pt[:, 0:nt],
                 s=qkn[:, normcol + c:normcol + c + 1]:
                 e.tensor_scalar(o, i, s, None, ALU.mult),
                 reads=[kpt, (kN,)], writes=[(kdst, dstoff)])

    def proj_v(b, tok0, vi):
        nt = min(128, NTOK - tok0)
        (wt, kw) = ws["wv"]
        xdeps = xf_keys(C, b, max(0, tok0), tok0 + nt) + [("xfpad",)]
        for half in range(2):
            pv = C.psA[4 + half]
            kvp = ("psA", 4 + half)
            for ci in range(NCH):
                P.op("tensor", lambda e, o=pv[0:nt, :],
                     l=C.xf[:, ci, b, PAD + tok0:PAD + tok0 + nt],
                     r=wt[:, ci, half * 512:(half + 1) * 512],
                     s=(ci == 0), t=(ci == NCH - 1):
                     e.matmul(o, l, r, start=s, stop=t),
                     reads=[(kw,)] + xdeps, writes=[kvp])
            P.op("scalar", lambda e, o=v_tm[0:nt, vi, half * 512:(half + 1) * 512],
                 i=pv[0:nt, :]: e.activation(o, i, AF.Identity),
                 reads=[kvp], writes=[(kVT, vi)])

    def proj_rms2(b, tok0, sl):
        nt = min(128, NTOK - tok0)
        (wt, kw) = ws["wk"]
        par = sl % 2
        st = st5[:, par, :]
        kst = (kS5, par)
        xdeps = xf_keys(C, b, max(tok0, 0), tok0 + nt) + [("xfpad",)]
        pq = [C.psA[4], C.psA[5]]
        for half in range(2):
            kqp = ("psA", 4 + half)
            for ci in range(NCH):
                P.op("tensor", lambda e, o=pq[half][0:nt, :],
                     l=C.xf[:, ci, b, PAD + tok0:PAD + tok0 + nt],
                     r=wt[:, ci, half * 512:(half + 1) * 512],
                     s=(ci == 0), t=(ci == NCH - 1):
                     e.matmul(o, l, r, start=s, stop=t),
                     reads=[(kw,)] + xdeps, writes=[kqp])
        for half in range(2):
            P.op("scalar", lambda e, o=qs_tm[0:nt, half * 512:(half + 1) * 512],
                 i=pq[half][0:nt, :], a=st[0:nt, half:half + 1]:
                 e.activation(o, i, AF.Square, accum_out=a),
                 reads=[("psA", 4 + half)], writes=[kst, (kQS, "scratch")])
        P.op("vector", lambda e, o=st[0:nt, 2:3], a=st[0:nt, 0:1], b2=st[0:nt, 1:2]:
             e.tensor_add(o, a, b2), reads=[kst], writes=[kst])
        P.op("scalar", lambda e, o=st[0:nt, 3:4], i=st[0:nt, 2:3], bb=C.epsv[0:nt, 1:2]:
             e.activation(o, i, AF.Sqrt, scale=1.0 / 1024.0, bias=bb),
             reads=[kst, ("epsv",)], writes=[kst])
        P.op("vector", lambda e, o=st[0:nt, 4:5], i=st[0:nt, 3:4]: e.reciprocal(o, i),
             reads=[kst], writes=[kst])
        for half in range(2):
            P.op("vector", lambda e, o=qs_tm[0:nt, half * 512:(half + 1) * 512],
                 i=pq[half][0:nt, :], s=st[0:nt, 4:5]:
                 e.tensor_scalar(o, i, s, None, ALU.mult),
                 reads=[("psA", 4 + half), kst, (kQS, "scratch")], writes=[(kQS,)])
        for c in range(NCH):
            pt = C.psT[c % 2][:, 0:128]
            kpt = ("psA", 6 + c % 2)
            P.op("tensor", lambda e, o=pt[:, 0:nt], i=qs_tm[0:nt, c * 128:(c + 1) * 128],
                 idn=C.ident[0:nt, 0:nt]: e.transpose(o, i, idn),
                 reads=[(kQS,)], writes=[kpt])
            P.op("vector", lambda e, o=k_fm[:, c, sl, 0:nt], i=pt[:, 0:nt],
                 s=qkn[:, 8 + c:9 + c]:
                 e.tensor_scalar(o, i, s, None, ALU.mult),
                 reads=[kpt, (kN,)], writes=[(kKF, sl)])

    (wot, kwo) = ws["wo"]
    for b in range(B):
        for jt in range(NT):
            if jt == 0:
                proj_rms2(b, -16, 0)
                proj_v(b, -16, 0)
            proj_rms2(b, jt * 128 + 112, (jt + 1) % 3)
            proj_v(b, jt * 128 + 112, (jt + 1) % 3)
            proj_rms(b, jt * 128, q_fm, kQF, 0, "wq", 0, jt % 2)
            sa = jt % 3
            sb2 = (jt + 1) % 3

            def emit_qk(hh):
                par = (jt * 8 + hh) % 2
                sp = C.psA[0 + 2 * par]
                ksp = ("psA", 0 + 2 * par)
                P.op("tensor", lambda e, o=sp[:, 0:128],
                     l=q_fm[:, hh, :], r=k_fm[:, hh, sa, :]:
                     e.matmul(o, l, r, start=True, stop=True),
                     reads=[(kQF, 0), (kKF, sa)], writes=[ksp])
                P.op("tensor", lambda e, o=sp[:, 128:144],
                     l=q_fm[:, hh, :], r=k_fm[:, hh, sb2, 0:16]:
                     e.matmul(o, l, r, start=True, stop=True),
                     reads=[(kQF, 0), (kKF, sb2)], writes=[ksp])

            def emit_soft(hh):
                par = (jt * 8 + hh) % 2
                par3 = (jt * 8 + hh) % 3
                sp = C.psA[0 + 2 * par]
                ksp = ("psA", 0 + 2 * par)
                sc = sp[:, 144:288]
                ksc = ksp
                P.op("vector", lambda e, o=sc[:, 0:144], i=sp[:, 0:144],
                     a=alibi[:, hh, :]:
                     e.scalar_tensor_tensor(o, i, 0.08838834764831845, a,
                                            ALU.mult, ALU.add),
                     reads=[ksp, (kAL,)], writes=[ksc])
                if jt == 0:
                    P.op("vector", lambda e, o=sc[:, 0:48], i=sp[:, 0:48],
                         a=alibi0[:, hh, :]:
                         e.scalar_tensor_tensor(o, i, 0.08838834764831845, a,
                                                ALU.mult, ALU.add),
                         reads=[ksp, (kAL0,)], writes=[ksc])
                nm2 = st5[:, par, 6:7]
                P.op("vector", lambda e, o=nm2, i=sc[:, 0:144]:
                     e.reduce_max(o, i, axis=AX.X, negate=True),
                     reads=[ksc], writes=[(kS5, par, "nm")])
                den = st5[:, par, 7:8]
                P.op("scalar", lambda e, o=pbn[:, par3, :], i=sc[:, 0:144], bb=nm2, a=den:
                     e.activation(o, i, AF.Exp, bias=bb, accum_out=a),
                     reads=[ksc, (kS5, par, "nm")],
                     writes=[(kPN, par3), (kS5, par, "den")])
                rde = st5[:, par, 5:6]
                P.op("vector", lambda e, o=rde, i=den: e.reciprocal(o, i),
                     reads=[(kS5, par, "den")], writes=[(kS5, par, "rde")])
                P.op("scalar", lambda e, o=pbn[:, par3, :], i=pbn[:, par3, :], s=rde:
                     e.activation(o, i, AF.Identity, scale=s),
                     reads=[(kPN, par3), (kS5, par, "rde")], writes=[(kPN, par3)])

            def emit_pv(hh):
                par = (jt * 8 + hh) % 2
                par3 = (jt * 8 + hh) % 3
                pt = C.psT[par][:, 0:128]
                kpt = ("psA", 6 + par)
                P.op("tensor", lambda e, o=pt, i=pbn[:, par3, 0:128],
                     idn=C.ident[:]: e.transpose(o, i, idn),
                     reads=[(kPN, par3)], writes=[kpt])
                P.op("vector", lambda e, o=pta[:, par3, :], i=pt: e.tensor_copy(o, i),
                     reads=[kpt], writes=[(kTA, par3)])
                P.op("tensor", lambda e, o=pt[0:16, :], i=pbn[:, par3, 128:144],
                     idn=C.ident[:]: e.transpose(o, i, idn),
                     reads=[(kPN, par3)], writes=[kpt])
                P.op("vector", lambda e, o=ptb[0:16, par3, :], i=pt[0:16, :]:
                     e.tensor_copy(o, i), reads=[kpt], writes=[(kTB, par3)])
                op_ = C.psA[1 + 2 * par]
                kop = ("psA", 1 + 2 * par)
                P.op("tensor", lambda e, o=op_[:, 0:128],
                     l=v_tm[:, sa, hh * 128:(hh + 1) * 128], r=pta[:, par3, :]:
                     e.matmul(o, l, r, start=True, stop=False),
                     reads=[(kVT, sa), (kTA, par3)], writes=[kop])
                P.op("tensor", lambda e, o=op_[:, 0:128],
                     l=v_tm[0:16, sb2, hh * 128:(hh + 1) * 128],
                     r=ptb[0:16, par3, :]:
                     e.matmul(o, l, r, start=False, stop=True),
                     reads=[(kVT, sb2), (kTB, par3)], writes=[kop])
                P.op("scalar", lambda e, o=at_fm[:, hh, :],
                     i=op_[:, 0:128]: e.activation(o, i, AF.Identity),
                     reads=[kop], writes=[(kAT, hh)])

            emit_qk(0)
            for hh in range(8):
                if hh + 1 < 8:
                    emit_qk(hh + 1)
                emit_soft(hh)
                emit_pv(hh)
            for half in range(2):
                pp = C.psA[4 + half]
                kpp = ("psA", 4 + half)
                for c in range(NCH):
                    P.op("tensor", lambda e, o=pp[:],
                         l=at_fm[:, c, :],
                         r=wot[:, c, half * 512:(half + 1) * 512],
                         s=(c == 0), t=(c == NCH - 1):
                         e.matmul(o, l, r, start=s, stop=t),
                         reads=[(kwo,)] + [(kAT, c2) for c2 in range(NCH)],
                         writes=[kpp])
                xsl = C.x_tm[:, b, jt, half * 512:(half + 1) * 512]
                kx = ("x", b, jt)
                P.op("vector", lambda e, o=xsl, i=xsl, p=pp[:]: e.tensor_add(o, i, p),
                     reads=[kx, kpp], writes=[kx])



def declare_io(nc, names_shapes, kind):
    return {n: nc.dram_tensor(n, s, F32, kind=kind).ap() for n, s in names_shapes}


SUB_WEIGHTS = {
    "ffn0": [("wrT", [D, D], F32), ("wkT", [D, D], F32), ("wvT", [D, D], F32),
             ("mu", [128, 16], F32)],
    "attn0": [("wqkvT", [D, 3072], F32), ("wprojT", [D, D], F32),
              ("bqkv", [128, 24], F32), ("bproj", [1, D], F32),
              ("msc", [128, NCH, 16], BF16), ("ebc", [16, NCH, 128], BF16)],
    "moe": [("gw", [128, NCH, 8], F32), ("w1s", [WROWS, 4104], BF16),
            ("w2s", [WROWS, 4096], BF16),
            ("b2", [8, D], BF16), ("l2", [128, 128], F32), ("l18", [18, 18], F32),
            ("if32", [128, 128], F32), ("thr", [128, 11], F32),
            ("iotag", [128, NG], F32), ("iotap", [128, NG], F32),
            ("rconst", [128, B, NT, 3], F32)],
    "attn1": [("wq", [D, D], F32), ("wk", [D, D], F32), ("wv", [D, D], F32),
              ("wo", [D, D], F32), ("qkn", [128, 16], F32),
              ("alibi", [128, 8, 144], BF16), ("alibi0", [128, 8, 48], BF16)],
}


def build_program(subs):
    nc = bass.Bass("TRN2", target_bir_lowering=False)
    x_in = nc.dram_tensor("x_in", [B, NTOK, 1024], F32, kind="ExternalInput").ap()
    x_out = nc.dram_tensor("x_out", [B, NTOK, 1024], F32, kind="ExternalOutput").ap()
    ident_in = nc.dram_tensor("ident", [128, 128], BF16, kind="ExternalInput").ap()
    win = {}
    for s in subs:
        for n, shp, dt in SUB_WEIGHTS[s]:
            key = f"{s}_{n}"
            if key not in win:
                win[key] = nc.dram_tensor(key, shp, dt, kind="ExternalInput").ap()
    with ExitStack() as ctx:
        C = Ctx(nc, ctx)
        P = C.P
        P.dma("sync", C.x_tm[:], x_in.rearrange("b (t p) d -> p b t d", p=128),
              writes=[("x", b, j) for b in range(B) for j in range(NT)])
        P.dma("sync", C.ident[:], ident_in, writes=[("ident",)])
        P.op("vector", lambda e: e.memset(C.xf[:, :, :, 0:PAD], 0.0),
             writes=[("xfpad",)])
        P.op("vector", lambda e: e.memset(C.xf[:, :, :, PAD + NTOK - 1:PAD + NTOK], 0.0),
             writes=[("xftail",)])
        P.op("vector", lambda e: e.memset(C.epsv[:, 0:1], LN_EPS), writes=[("epsv",)])
        P.op("vector", lambda e: e.memset(C.epsv[:, 1:2], RMS_EPS), writes=[("epsv",)])
        P.op("vector", lambda e: e.memset(C.ones[:], 1.0), writes=[("ones",)])
        for s in subs:
            if s == "ffn0":
                emit_s1(C, win["ffn0_wrT"], win["ffn0_wkT"], win["ffn0_wvT"],
                        win["ffn0_mu"])
            elif s == "attn0":
                emit_s2(C, win["attn0_wqkvT"], win["attn0_wprojT"],
                        win["attn0_bqkv"], win["attn0_bproj"],
                        win["attn0_msc"], win["attn0_ebc"])
            elif s == "moe":
                emit_moe_sparse(C, win["moe_gw"], win["moe_w1s"], win["moe_w2s"],
                                win["moe_b2"], win["moe_l2"], win["moe_l18"],
                                win["moe_if32"], win["moe_rconst"],
                                win["moe_thr"], win["moe_iotag"], win["moe_iotap"])
            elif s == "attn1":
                emit_s5(C, win["attn1_wq"], win["attn1_wk"], win["attn1_wv"],
                        win["attn1_wo"], win["attn1_qkn"], win["attn1_alibi"],
                        win["attn1_alibi0"])
            else:
                raise NotImplementedError(s)
        P.dma("sync", x_out.rearrange("b (t p) d -> p b t d", p=128), C.x_tm[:],
              reads=[("x", b, j) for b in range(B) for j in range(NT)],
              is_output=True)
        P.emit(nc, ctx)
    return nc



def shard_x(x_full):
    xp = np.zeros((B, HALO + NCORE * TOWN + (NTOK - TLOC), D), np.float32)
    xp[:, HALO:HALO + NCORE * TOWN] = x_full
    shards = []
    for c in range(NCORE):
        sl = np.zeros((B, NTOK, D), np.float32)
        sl[:, :TLOC] = xp[:, c * TOWN: c * TOWN + TLOC]
        shards.append(sl)
    return shards


def unshard_x(results, key="x_out"):
    return np.concatenate([r[key][:, HALO:HALO + TOWN] for r in results], axis=1)


_IDENT = np.eye(128, dtype=BF)


def run_subblocks(subs, x_full, weight_map):
    nc = build_program(subs)
    shards = shard_x(x_full)
    in_maps = []
    for c in range(NCORE):
        m = {"x_in": shards[c], "ident": _IDENT}
        for k, v in weight_map.items():
            m[k] = v[c] if isinstance(v, list) else v
        in_maps.append(m)
    res = run_bass_kernel_spmd(nc, in_maps, core_ids=list(range(NCORE)))
    return unshard_x(res.results)


def attn0_weights(Wqkv, bqkv, Wproj, bproj):
    H16 = 16
    msc = np.zeros((128, NCH, 16), dtype=BF)
    ebc = np.zeros((16, NCH, 128), dtype=BF)
    for c in range(NCH):
        for p in range(128):
            h = 2 * c + (1 if p >= 64 else 0)
            msc[p, c, h] = 1
            ebc[h, c, p] = 1
    return {
        "attn0_wqkvT": np.ascontiguousarray(np.asarray(Wqkv, np.float32).T),
        "attn0_wprojT": np.ascontiguousarray(np.asarray(Wproj, np.float32).T),
        "attn0_bqkv": np.ascontiguousarray(np.asarray(bqkv, np.float32).reshape(24, 128).T),
        "attn0_bproj": np.asarray(bproj, np.float32).reshape(1, D),
        "attn0_msc": msc,
        "attn0_ebc": ebc,
    }


def moe_weights(gate_w, W1, b1, W2, b2):
    gate_w = np.asarray(gate_w, np.float32)
    gw = np.ascontiguousarray(gate_w.T.reshape(8, 128, 8).transpose(1, 0, 2))
    W1 = np.asarray(W1, np.float32)
    W2 = np.asarray(W2, np.float32)
    b1 = np.asarray(b1, np.float32)
    a = W1.reshape(8, 8, 128, 8, 512).transpose(0, 3, 2, 1, 4).reshape(8, 8, 128, 4096)
    w1s = np.zeros((8, 8, 128, 4104), dtype=BF)
    w1s[..., :4096] = a.astype(BF)
    b1r = np.ascontiguousarray(b1.reshape(8, 8, 4, 128).transpose(0, 1, 3, 2)).astype(np.float32)
    w1s[..., 4096:4104] = b1r.view(BF).reshape(8, 8, 128, 8)
    w1s = np.ascontiguousarray(w1s.reshape(WROWS, 4104))
    w2s = np.ascontiguousarray(
        W2.reshape(8, 8, 4, 128, 1024).transpose(0, 1, 3, 2, 4)
        .reshape(WROWS, 4096)).astype(BF)
    l2 = np.zeros((128, 128), np.float32)
    for ss in range(128):
        l2[ss, ss + 1:] = 2.0
    l18 = np.zeros((18, 18), np.float32)
    for ii in range(18):
        l18[ii, ii + 1:] = 1.0
    if32 = np.eye(128, dtype=np.float32)
    thr = np.zeros((128, 11), np.float32)
    for k in range(11):
        thr[:, k] = GRP * (k + 1) - (GRP - 1)
    iotag = np.tile(np.arange(NG, dtype=np.float32)[None, :], (128, 1))
    iotap = np.tile(np.arange(128, dtype=np.float32)[:, None], (1, NG))
    rconst = np.zeros((128, B, NT, 3), np.float32)
    for b_ in range(B):
        for jt in range(NT):
            for p in range(128):
                t = jt * 128 + p
                rconst[p, b_, jt, 0] = 0.5 if t < TLOC else 0.0
                rconst[p, b_, jt, 1] = 0.0 if t < TLOC else 1.0e9
                rconst[p, b_, jt, 2] = b_ * NTOK + t
    return {"moe_gw": gw, "moe_w1s": w1s, "moe_w2s": w2s,
            "moe_b2": np.ascontiguousarray(np.asarray(b2, np.float32).reshape(8, D)).astype(BF),
            "moe_l2": l2, "moe_l18": l18, "moe_if32": if32, "moe_thr": thr,
            "moe_iotag": iotag, "moe_iotap": iotap,
            "moe_rconst": rconst}


def alibi_tiles(core):
    A = np.full((128, 8, 2, 144), -1e9, np.float32)
    for h in range(8):
        sl = 0.5 ** h
        for i in range(128):
            for jj in range(144):
                rel = jj - 16 - i
                if -16 <= rel <= 0:
                    A[i, h, 1, jj] = sl * rel
                    if core != 0 or jj - 16 >= HALO:
                        A[i, h, 0, jj] = sl * rel
    return A


def alibi_split(core):
    A = alibi_tiles(core)
    gen = np.ascontiguousarray(A[:, :, 1, :]).astype(BF)
    patch = np.ascontiguousarray(A[:, :, 0, 0:48]).astype(BF)
    return gen, patch


def attn1_weights(wq, wk, wv, wo, qn, kn):
    qkn = np.concatenate([np.asarray(qn, np.float32).reshape(8, 128).T,
                          np.asarray(kn, np.float32).reshape(8, 128).T], axis=1)
    return {"attn1_wq": np.asarray(wq, np.float32),
            "attn1_wk": np.asarray(wk, np.float32),
            "attn1_wv": np.asarray(wv, np.float32),
            "attn1_wo": np.asarray(wo, np.float32),
            "attn1_qkn": np.ascontiguousarray(qkn),
            "attn1_alibi": [alibi_split(c)[0] for c in range(NCORE)],
            "attn1_alibi0": [alibi_split(c)[1] for c in range(NCORE)]}


def ffn0_weights(W_r, W_k, W_v, mu_r, mu_k):
    mu = np.concatenate([np.asarray(mu_r, np.float32).reshape(8, 128).T,
                         np.asarray(mu_k, np.float32).reshape(8, 128).T], axis=1)
    return {
        "ffn0_wrT": np.ascontiguousarray(np.asarray(W_r, np.float32).T),
        "ffn0_wkT": np.ascontiguousarray(np.asarray(W_k, np.float32).T),
        "ffn0_wvT": np.ascontiguousarray(np.asarray(W_v, np.float32).T),
        "ffn0_mu": np.ascontiguousarray(mu),
    }


def kernel(**inputs):
    x = np.asarray(inputs["L0_attn_x"], np.float32)
    wm_ffn0 = ffn0_weights(inputs["L0_ffn_W_r"], inputs["L0_ffn_W_k"],
                           inputs["L0_ffn_W_v"], inputs["L0_ffn_mu_r"],
                           inputs["L0_ffn_mu_k"])
    wm_attn0 = attn0_weights(inputs["L0_attn_Wqkv"], inputs["L0_attn_bqkv"],
                             inputs["L0_attn_Wproj"], inputs["L0_attn_bproj"])
    wm_moe = moe_weights(inputs["L1_ffn_gate_w"], inputs["L1_ffn_W1"],
                         inputs["L1_ffn_b1"], inputs["L1_ffn_W2"],
                         inputs["L1_ffn_b2"])
    wm_attn1 = attn1_weights(inputs["L1_attn_wq"], inputs["L1_attn_wk"],
                             inputs["L1_attn_wv"], inputs["L1_attn_wo"],
                             inputs["L1_attn_q_norm_w"], inputs["L1_attn_k_norm_w"])
    wm = {}
    for m in (wm_ffn0, wm_attn0, wm_moe, wm_attn1):
        wm.update(m)
    x = run_subblocks(["ffn0", "attn0", "ffn0", "moe", "attn1", "moe"], x, wm)
    return x


# version 23
# speedup vs baseline: 1.0865x; 1.0865x over previous
import numpy as np
import ml_dtypes
import concourse.bass as bass
import concourse.mybir as mybir
from concourse.bass_utils import run_bass_kernel_spmd
from contextlib import ExitStack

F32 = mybir.dt.float32
BF16 = mybir.dt.bfloat16
AF = mybir.ActivationFunctionType
ALU = mybir.AluOpType
AX = mybir.AxisListType
BF = ml_dtypes.bfloat16
MM2 = mybir.MatmulPerfMode.DoubleRow

NCORE = 8
B = 2
TOWN = 1024
HALO = 18
TLOC = TOWN + HALO
NT = 9
NTOK = NT * 128
PAD = 16
D = 1024
NCH = 8
LN_EPS = 1e-5
RMS_EPS = 1e-6
GRPS = [(0, 512), (512, 1024), (1024, 1152)]
SPARSE_MOE = True


STREAMS = ("tensor", "vector", "scalar", "gpsimd", "sync", "gpsw")
ENGINE_STREAMS = {"tensor": ("tensor",), "vector": ("vector",),
                  "scalar": ("scalar",), "sync": ("sync",),
                  "gpsimd": ("gpsimd", "gpsw")}


class _Op:
    __slots__ = ("stream", "fn", "idx", "is_dma", "needs_inc", "count",
                 "waits", "snap", "wait_count")

    def __init__(self, stream, fn, idx, is_dma):
        self.stream = stream
        self.fn = fn
        self.idx = idx
        self.is_dma = is_dma
        self.needs_inc = False
        self.count = None
        self.waits = []
        self.snap = None
        self.wait_count = None


class Prog:

    def __init__(self):
        self.ops = []
        self.last_w = {}
        self.readers = {}
        self.clock = {s: {t: -1 for t in STREAMS} for s in STREAMS}
        self.out_dmas = []
        self.arena_knames = set()
        self.epoch_fence = []
        self.last_per_stream = {}

    def op(self, stream, fn, reads=(), writes=(), is_dma=False, is_output=False):
        o = _Op(stream, fn, len(self.ops), is_dma)
        if is_dma:
            o.needs_inc = True
        deps = []
        for k in reads:
            w = self.last_w.get(k)
            if w is not None:
                deps.append(w)
        for k in writes:
            w = self.last_w.get(k)
            if w is not None:
                deps.append(w)
            deps.extend(self.readers.get(k, ()))
            if self.arena_knames and isinstance(k, tuple) and k and k[0] in self.arena_knames:
                deps.extend(self.epoch_fence)
        ck = self.clock[stream]
        need = {}
        for d in deps:
            if d.stream == "tensor" and stream == "tensor":
                continue
            if ck[d.stream] >= d.idx:
                continue
            cur = need.get(d.stream)
            if cur is None or d.idx > cur.idx:
                need[d.stream] = d
        for d in need.values():
            d.needs_inc = True
            if d.snap is None:
                d.snap = dict(self.clock[d.stream])
                d.snap[d.stream] = d.idx
            o.waits.append(d)
            for s2, v in d.snap.items():
                if v > ck[s2]:
                    ck[s2] = v
        for k in writes:
            self.last_w[k] = o
            self.readers[k] = []
        for k in reads:
            self.readers.setdefault(k, []).append(o)
        self.ops.append(o)
        self.last_per_stream[stream] = o
        if is_output:
            o.needs_inc = True
            self.out_dmas.append(o)
        return o

    def dma(self, stream, out_ap, in_ap, reads=(), writes=(), is_output=False, **kw):
        return self.op(stream,
                       (lambda e, o=out_ap, i=in_ap, kw=kw: e.dma_start(out=o, in_=i, **kw)),
                       reads=reads, writes=writes, is_dma=True, is_output=is_output)

    def emit(self, nc, ctx):
        counts = {s: 0 for s in STREAMS}
        for o in self.ops:
            if o.needs_inc:
                counts[o.stream] += 16 if o.is_dma else 1
                o.count = counts[o.stream]
        sems = {s: ctx.enter_context(nc.semaphore("sem_" + s)) for s in STREAMS}
        per = {en: [] for en in ENGINE_STREAMS}
        eng_of = {}
        for en, ss in ENGINE_STREAMS.items():
            for s in ss:
                eng_of[s] = en
        for o in self.ops:
            per[eng_of[o.stream]].append(o)
        for en in ENGINE_STREAMS:
            burst = []

            def flush(burst):
                if burst:
                    final = burst[-1].count
                    for b in burst:
                        b.wait_count = final

            for o in per[en]:
                if o.is_dma and not o.waits and (not burst or burst[-1].stream == o.stream):
                    burst.append(o)
                elif o.is_dma:
                    flush(burst)
                    burst = [o]
                else:
                    flush(burst)
                    burst = []
            flush(burst)
        block = ctx.enter_context(nc.Block())

        def run(eng, en):
            hw = {}
            for o in per[en]:
                for d in o.waits:
                    w = d.wait_count if d.wait_count is not None else d.count
                    if hw.get(d.stream, -1) < w:
                        eng.wait_ge(sems[d.stream], w)
                        hw[d.stream] = w
                inst = o.fn(eng)
                if o.needs_inc:
                    inst.then_inc(sems[o.stream], 16 if o.is_dma else 1)
            for s in ENGINE_STREAMS[en]:
                fin = [o for o in self.out_dmas if o.stream == s]
                if fin:
                    eng.wait_ge(sems[s], max(o.wait_count or o.count for o in fin))

        @block.tensor
        def _(e):
            run(e, "tensor")

        @block.vector
        def _(e):
            run(e, "vector")

        @block.scalar
        def _(e):
            run(e, "scalar")

        @block.gpsimd
        def _(e):
            run(e, "gpsimd")

        @block.sync
        def _(e):
            run(e, "sync")



class Ctx:

    def __init__(self, nc, ctx):
        self.nc = nc
        self.x = ctx
        self.P = Prog()
        self.uid = 0
        sb = self.sb = lambda name, shape, dt=F32: ctx.enter_context(
            nc.sbuf_tensor(name, shape, dt))
        ps = lambda name, shape, dt=F32: ctx.enter_context(
            nc.psum_tensor(name, shape, dt))
        self.x_tm = sb("x_tm", [128, B, NT, 1024])
        self.xf = sb("xf", [128, NCH, B, PAD + NTOK], BF16)
        self.ident = sb("ident_sb", [128, 128], BF16)
        self.ones = sb("ones_bf", [1, 128], BF16)
        self.bn12 = sb("bn12", [128, 2, 12])
        self.mv = sb("mv", [128, 2, 2])
        self.nm = sb("nm", [128, 2, 1])
        self.sd = sb("sd", [128, 2, 1])
        self.rs = sb("rs", [128, 2, 1])
        self.xf_tm = sb("xf_tm", [128, 2, 1024], BF16)
        self.moe_gcb = sb("moe_gcb", [128, 2, 8], BF16)
        self.smalls = {}
        self.epsv = sb("epsv", [128, 2])
        self.psA = [ps(f"psA{i}", [128, 512]) for i in range(8)]
        self.psT = [self.psA[6].bitcast(BF16), self.psA[7].bitcast(BF16)]
        self.ARENA_B = 90368
        self.arena = sb("arena", [128, self.ARENA_B], mybir.dt.uint8)
        self.arena_off = 0

    def name(self, p):
        self.uid += 1
        return f"{p}{self.uid}"

    def carve(self, kname, shape, dt=BF16):
        assert shape[0] == 128
        esz = {F32: 4, BF16: 2, mybir.dt.uint32: 4}[dt]
        n = 1
        for s in shape[1:]:
            n *= s
        off = (self.arena_off + 31) & ~31
        self.arena_off = off + n * esz
        assert self.arena_off <= self.ARENA_B, \
            f"arena overflow: {self.arena_off} > {self.ARENA_B} at {kname}"
        flat = self.arena.bitcast(dt)
        ap = flat[:, off // esz: off // esz + n]
        if len(shape) > 2:
            dims = " ".join(f"d{i}" for i in range(1, len(shape)))
            kw = {f"d{i}": shape[i] for i in range(1, len(shape))}
            ap = ap.rearrange(f"p ({dims}) -> p {dims}", **kw)
        self.P.arena_knames.add(kname)
        return ap, kname

    def arena_reset(self):
        P = self.P
        P.epoch_fence = list(P.last_per_stream.values())
        P.arena_knames = set()
        self.arena_off = 0


def load_const(C, key, ap_in, shape, dt=F32, stream="sync"):
    t = C.sb(C.name("c_"), shape, dt)
    C.P.dma(stream, t[:], ap_in, writes=[(id(t),)])
    return t


def emit_ln(C, b, j, eps, par, dram_out=None):
    P, nc = C.P, C.nc
    x = C.x_tm[:, b, j, :]
    xk = ("x", b, j)
    st = C.bn12[:, par, :]
    xt = C.xf_tm[:, par, :]
    kst, kxt = ("ln", par, 0), ("ln", par, 5)
    ecol = 0 if eps == LN_EPS else 1
    if b == 0:
        mv = C.mv[:, par, :]
        nm = C.nm[:, par, :]
        sd = C.sd[:, par, :]
        rs = C.rs[:, par, :]
        P.op("vector", lambda e, o=st[:, 0:6], i=x[:, 0:512]: e.bn_stats(o, i),
             reads=[xk], writes=[kst])
        P.op("vector", lambda e, o=st[:, 6:12], i=x[:, 512:1024]: e.bn_stats(o, i),
             reads=[xk], writes=[kst])
        P.op("vector", lambda e, o=mv, i=st: e.bn_aggr(o, i), reads=[kst], writes=[kst])
        P.op("vector", lambda e, o=nm, i=mv[:, 0:1]: e.tensor_scalar_mul(o, i, -1.0),
             reads=[kst], writes=[kst])
        P.op("scalar", lambda e, o=sd, i=mv[:, 1:2], bb=C.epsv[:, ecol:ecol + 1]:
             e.activation(o, i, AF.Sqrt, bias=bb),
             reads=[kst, ("epsv",)], writes=[kst])
        P.op("vector", lambda e, o=rs, i=sd: e.reciprocal(o, i), reads=[kst], writes=[kst])
        P.op("vector", lambda e, o=xt, i=x, a=nm, s=rs:
             e.tensor_scalar(o, i, a, s, ALU.add, ALU.mult),
             reads=[xk, kst], writes=[kxt])
    else:
        scr = xt[:, 0:512]
        P.op("scalar", lambda e, o=scr, i=x[:, 0:512], a=st[:, 0:1]:
             e.activation(o, i, AF.Copy, accum_out=a), reads=[xk], writes=[kst, kxt])
        P.op("scalar", lambda e, o=scr, i=x[:, 512:1024], a=st[:, 1:2]:
             e.activation(o, i, AF.Copy, accum_out=a), reads=[xk], writes=[kst, kxt])
        P.op("scalar", lambda e, o=scr, i=x[:, 0:512], a=st[:, 2:3]:
             e.activation(o, i, AF.Square, accum_out=a), reads=[xk], writes=[kst, kxt])
        P.op("scalar", lambda e, o=scr, i=x[:, 512:1024], a=st[:, 3:4]:
             e.activation(o, i, AF.Square, accum_out=a), reads=[xk], writes=[kst, kxt])
        P.op("vector", lambda e, o=st[:, 5:6], a=st[:, 0:1], b2=st[:, 1:2]:
             e.tensor_add(o, a, b2), reads=[kst], writes=[kst])
        P.op("vector", lambda e, o=st[:, 5:6], i=st[:, 5:6]:
             e.tensor_scalar_mul(o, i, -1.0 / 1024.0), reads=[kst], writes=[kst])
        P.op("vector", lambda e, o=st[:, 6:7], a=st[:, 2:3], b2=st[:, 3:4]:
             e.tensor_add(o, a, b2), reads=[kst], writes=[kst])
        P.op("vector", lambda e, o=st[:, 7:8], a=st[:, 5:6], b2=st[:, 5:6]:
             e.tensor_mul(o, a, b2), reads=[kst], writes=[kst])
        P.op("vector", lambda e, o=st[:, 8:9], i=st[:, 6:7], t2=st[:, 7:8]:
             e.scalar_tensor_tensor(o, i, 1.0 / 1024.0, t2, ALU.mult, ALU.subtract),
             reads=[kst], writes=[kst])
        P.op("scalar", lambda e, o=st[:, 9:10], i=st[:, 8:9], bb=C.epsv[:, ecol:ecol + 1]:
             e.activation(o, i, AF.Sqrt, bias=bb),
             reads=[kst, ("epsv",)], writes=[kst])
        P.op("vector", lambda e, o=st[:, 10:11], i=st[:, 9:10]: e.reciprocal(o, i),
             reads=[kst], writes=[kst])
        P.op("vector", lambda e, o=st[:, 11:12], a=st[:, 5:6], b2=st[:, 10:11]:
             e.tensor_mul(o, a, b2), reads=[kst], writes=[kst])
        P.op("scalar", lambda e, o=xt, i=x, bb=st[:, 11:12], s=st[:, 10:11]:
             e.activation(o, i, AF.Identity, bias=bb, scale=s),
             reads=[xk, kst], writes=[kxt])
    if dram_out is not None:
        P.dma("sync", dram_out[(b * NTOK + j * 128):(b * NTOK + (j + 1) * 128), :],
              xt, reads=[kxt], writes=[("xfd",)])
    for c in range(NCH):
        pt = C.psT[c % 2][:, 0:128]
        kpt = ("psA", 6 + c % 2)
        P.op("tensor", lambda e, o=pt, i=xt[:, c * 128:(c + 1) * 128], idn=C.ident[:]:
             e.transpose(o, i, idn), reads=[kxt], writes=[kpt])
        dst = C.xf[:, c, b, PAD + j * 128: PAD + (j + 1) * 128]
        eng = "vector" if c % 2 == 0 else "scalar"
        if eng == "vector":
            P.op(eng, lambda e, o=dst, i=pt: e.tensor_copy(o, i),
                 reads=[kpt], writes=[("xf", c, b, j)])
        else:
            P.op(eng, lambda e, o=dst, i=pt: e.activation(o, i, AF.Copy),
                 reads=[kpt], writes=[("xf", c, b, j)])


def emit_ln_all(C, eps, dram_out=None):
    for b in range(B):
        for j in range(NT):
            emit_ln(C, b, j, eps, (b * NT + j) % 2, dram_out)


def xf_keys(C, b, t0, t1, shift=0):
    lo = max(0, (t0 + shift) // 128)
    hi = min(NT - 1, (t1 - 1 + shift) // 128)
    return [("xf", c, b, j) for c in range(NCH) for j in range(lo, hi + 1)]



def emit_s1(C, wr_ap, wk_ap, wv_ap, mu_ap):
    P, nc = C.P, C.nc
    C.arena_reset()
    w_r, kwr = C.carve(C.name("w_r"), [128, NCH, 1024], BF16)
    w_k, kwk = C.carve(C.name("w_k"), [128, NCH, 1024], BF16)
    w_v, kwv = C.carve(C.name("w_v"), [128, NCH, 1024], BF16)
    mu, kmu = C.carve(C.name("mu"), [128, 16], F32)
    for t, ap, key in ((w_r, wr_ap, kwr), (w_k, wk_ap, kwk), (w_v, wv_ap, kwv)):
        P.dma("gpsimd", t[:], ap.rearrange("(c p) m -> p c m", p=128), writes=[(key,)])
    P.dma("sync", mu[:], mu_ap, writes=[(kmu,)])
    P.op("vector", lambda e, o=w_v[:], i=w_v[:]: e.tensor_scalar_mul(o, i, 0.5),
         reads=[(kwv,)], writes=[(kwv,)])

    emit_ln_all(C, LN_EPS)

    xr, kxr = C.carve(C.name("xr"), [128, 2, NCH, 256], BF16)
    xk, kxk = C.carve(C.name("xk"), [128, 2, NCH, 256], BF16)
    dbuf, kd = C.carve(C.name("d"), [128, 2, 256], BF16)
    kk, kkk = C.carve(C.name("kk"), [128, 2, NCH, 256], BF16)
    krelu, kkr = C.carve(C.name("krelu"), [128, 2, 256], BF16)
    r_tm, krt = C.carve(C.name("r_tm"), [128, 2, 1024], F32)
    s1tmp, kst = C.carve(C.name("s1tmp"), [128, 2, 512], F32)

    GR1 = [(g * 256, min(g * 256 + 256, NTOK)) for g in range(5)]
    groups = [(b, t0, t1) for b in range(B) for (t0, t1) in GR1]

    def s1_mix(gidx):
        b, t0, t1 = groups[gidx]
        TT = t1 - t0
        gp = gidx % 2
        rdeps = xf_keys(C, b, t0, t1) + xf_keys(C, b, t0 - 1, t1 - 1) + [("xfpad",)]
        for c in range(NCH):
            cur = C.xf[:, c, b, PAD + t0: PAD + t1]
            prv = C.xf[:, c, b, PAD + t0 - 1: PAD + t1 - 1]
            P.op("vector", lambda e, o=dbuf[:, gp, 0:TT], i=cur, p=prv: e.tensor_sub(o, i, p),
                 reads=rdeps, writes=[(kd, gp)])
            P.op("vector", lambda e, o=xr[:, gp, c, 0:TT], i=dbuf[:, gp, 0:TT],
                 s=mu[:, c:c + 1], p=prv:
                 e.scalar_tensor_tensor(o, i, s, p, ALU.mult, ALU.add),
                 reads=[(kd, gp), (kmu,)] + rdeps, writes=[(kxr, gp, c)])
            P.op("vector", lambda e, o=xk[:, gp, c, 0:TT], i=dbuf[:, gp, 0:TT],
                 s=mu[:, 8 + c:9 + c], p=prv:
                 e.scalar_tensor_tensor(o, i, s, p, ALU.mult, ALU.add),
                 reads=[(kd, gp), (kmu,)] + rdeps, writes=[(kxk, gp, c)])

    def s1_comp(gidx):
        b, t0, t1 = groups[gidx]
        TT = t1 - t0
        gp = gidx % 2
        for co in range(NCH):
                kp = C.psA[co % 2]
                kkp = ("psA", co % 2)
                for ci in range(NCH):
                    P.op("tensor", lambda e, o=kp[:, 0:TT],
                         l=w_k[:, ci, co * 128:(co + 1) * 128],
                         r=xk[:, gp, ci, 0:TT], s=(ci == 0), t=(ci == NCH - 1):
                         e.matmul(o, l, r, start=s, stop=t),
                         reads=[(kwk,)] + [(kxk, gp, ci) for ci in range(NCH)], writes=[kkp])
                P.op("scalar", lambda e, o=krelu[:, gp, 0:TT], i=kp[:, 0:TT]:
                     e.activation(o, i, AF.Relu), reads=[kkp], writes=[(kkr, gp)])
                P.op("vector", lambda e, o=kk[:, gp, co, 0:TT], i=krelu[:, gp, 0:TT]:
                     e.tensor_mul(o, i, i), reads=[(kkr, gp)], writes=[(kkk, gp, co)])
        for jt in range(t0 // 128, t1 // 128):
            toff = jt * 128 - t0
            par = jt % 2
            rp = [C.psA[2], C.psA[3]]
            vp = [C.psA[4], C.psA[5]]
            for half in range(2):
                krp = ("psA", 2 + half)
                for ci in range(NCH):
                    P.op("tensor", lambda e, o=rp[half][:],
                         l=xr[:, gp, ci, toff:toff + 128],
                         r=w_r[:, ci, half * 512:(half + 1) * 512],
                         s=(ci == 0), t=(ci == NCH - 1):
                         e.matmul(o, l, r, start=s, stop=t),
                         reads=[(kwr,)] + [(kxr, gp, ci) for ci in range(NCH)], writes=[krp])
                P.op("scalar", lambda e, o=r_tm[:, par, half * 512:(half + 1) * 512],
                     i=rp[half][:]: e.activation(o, i, AF.Sigmoid),
                     reads=[krp], writes=[(krt, par)])
            for half in range(2):
                kvp = ("psA", 4 + half)
                for ck in range(NCH):
                    P.op("tensor", lambda e, o=vp[half][:],
                         l=kk[:, gp, ck, toff:toff + 128],
                         r=w_v[:, ck, half * 512:(half + 1) * 512],
                         s=(ck == 0), t=(ck == NCH - 1):
                         e.matmul(o, l, r, start=s, stop=t),
                         reads=[(kwv,)] + [(kkk, gp, ck2) for ck2 in range(NCH)], writes=[kvp])
                xsl = C.x_tm[:, b, jt, half * 512:(half + 1) * 512]
                kx = ("x", b, jt)
                P.op("vector", lambda e, o=s1tmp[:, half, :], i=vp[half][:],
                     r=r_tm[:, par, half * 512:(half + 1) * 512]: e.tensor_mul(o, i, r),
                     reads=[kvp, (krt, par)], writes=[(kst, half)])
                P.op("vector", lambda e, o=xsl, i=xsl, t2=s1tmp[:, half, :]:
                     e.tensor_add(o, i, t2),
                     reads=[kx, (kst, half)], writes=[kx])

    for gidx in range(len(groups)):
        s1_mix(gidx)
        if gidx > 0:
            s1_comp(gidx - 1)
    s1_comp(len(groups) - 1)



def emit_s2(C, wqkvT_ap, wprojT_ap, bqkv_ap, bproj_ap, msc_ap, ebc_ap):
    P, nc = C.P, C.nc
    C.arena_reset()
    wqkv, kW = C.carve(C.name("wqkv"), [128, NCH, 3072], BF16)
    wproj, kP2 = C.carve(C.name("wproj"), [128, NCH, 1024], BF16)
    bqkv, kB = C.carve(C.name("bqkv"), [128, 24], F32)
    msc, kM = C.carve(C.name("msc"), [128, NCH, 16], BF16)
    P.dma("gpsimd", wqkv[:], wqkvT_ap.rearrange("(c p) m -> p c m", p=128), writes=[(kW,)])
    P.dma("gpsimd", wproj[:], wprojT_ap.rearrange("(c p) m -> p c m", p=128), writes=[(kP2,)])
    P.dma("sync", bqkv[:], bqkv_ap, writes=[(kB,)])
    P.dma("sync", msc[:], msc_ap, writes=[(kM,)])
    bproj = C.sb(C.name("bproj_"), [1, 1024], BF16)
    ebc = C.sb(C.name("ebc_"), [16, NCH, 128], BF16)
    kBP, kE = (id(bproj),), (id(ebc),)
    P.dma("gpsimd", bproj[:], bproj_ap, writes=[kBP])
    P.dma("sync", ebc[:], ebc_ap, writes=[kE])

    emit_ln_all(C, LN_EPS)

    qkv, kQ = C.carve(C.name("qkv"), [128, 24, B, 128], BF16)
    ssb, kS = C.carve(C.name("ssb"), [128, 2 * 128], F32)
    dsb, kD2 = C.carve(C.name("dsb"), [128, 2 * 128], F32)
    asb, kA = C.carve(C.name("asb"), [128, 2 * 128], BF16)
    dv, kV = C.carve(C.name("dv"), [128, NCH, 128], BF16)
    pbuf, kPB = C.carve(C.name("pbuf"), [128, 4, 128], BF16)
    ofm, kO = C.carve(C.name("ofm"), [128, NCH, B, 128], BF16)
    otmp, kOT = C.carve(C.name("otmp"), [128, 128], BF16)
    TT = 128

    for jt in range(NT):
        t0 = jt * 128
        for b in range(B):
            for oc in range(24):
                pp = C.psA[oc % 2]
                kpp = ("psA", oc % 2)
                for ci in range(NCH):
                    P.op("tensor", lambda e, o=pp[:, 0:TT],
                         l=wqkv[:, ci, oc * 128:(oc + 1) * 128],
                         r=C.xf[:, ci, b, PAD + t0:PAD + t0 + TT],
                         s=(ci == 0), t=(ci == NCH - 1):
                         e.matmul(o, l, r, start=s, stop=t),
                         reads=[(kW,)] + [("xf", ci2, b, jt) for ci2 in range(NCH)],
                         writes=[kpp])
                P.op("scalar", lambda e, o=qkv[:, oc, b, :], i=pp[:, 0:TT],
                     bb=bqkv[:, oc:oc + 1]:
                     e.activation(o, i, AF.Identity, bias=bb),
                     reads=[kpp, (kB,)], writes=[(kQ, oc, b)])
        sps = {}
        for idx, (i, j) in enumerate([(0, 0), (0, 1), (1, 0), (1, 1)]):
            pp = C.psA[2 + idx]
            sps[(i, j)] = pp
            for c in range(NCH):
                pb = pbuf[:, c % 4, :]
                kpb = (kPB, c % 4)
                P.op("vector", lambda e, o=pb, a=qkv[:, c, i, :], b2=qkv[:, 8 + c, j, :]:
                     e.tensor_mul(o, a, b2),
                     reads=[(kQ, c, i), (kQ, 8 + c, j)], writes=[kpb])
                P.op("tensor", lambda e, o=pp[0:16, 0:TT], l=msc[:, c, :], r=pb,
                     s=(c == 0), t=(c == NCH - 1):
                     e.matmul(o, l, r, start=s, stop=t),
                     reads=[(kM,), kpb], writes=[("psA", 2 + idx)])
        for i in range(2):
            P.op("vector", lambda e, o=ssb[0:16, i * TT:(i + 1) * TT],
                 p=sps[(i, 1)][0:16, 0:TT]:
                 e.tensor_copy(o, p), reads=[("psA", 3 + 2 * i)], writes=[(kS, i)])
            P.op("vector", lambda e, o=dsb[0:16, i * TT:(i + 1) * TT],
                 p=sps[(i, 0)][0:16, 0:TT], s2=ssb[0:16, i * TT:(i + 1) * TT]:
                 e.tensor_sub(o, p, s2),
                 reads=[("psA", 2 + 2 * i), (kS, i)], writes=[(kD2, i)])
            P.op("scalar", lambda e, o=asb[0:16, i * TT:(i + 1) * TT],
                 i2=dsb[0:16, i * TT:(i + 1) * TT]:
                 e.activation(o, i2, AF.Sigmoid, scale=0.125),
                 reads=[(kD2, i)], writes=[(kA, i)])
        for c in range(NCH):
            P.op("vector", lambda e, o=dv[:, c, :], a=qkv[:, 16 + c, 0, :],
                 b2=qkv[:, 16 + c, 1, :]: e.tensor_sub(o, a, b2),
                 reads=[(kQ, 16 + c, 0), (kQ, 16 + c, 1)], writes=[(kV, c)])
        for i in range(2):
            for c in range(NCH):
                pp = C.psA[c % 2]
                kpp = ("psA", c % 2)
                P.op("tensor", lambda e, o=pp[:, 0:TT], l=ebc[:, c, :],
                     r=asb[0:16, i * TT:(i + 1) * TT]:
                     e.matmul(o, l, r, start=True, stop=True),
                     reads=[kE, (kA, i)], writes=[kpp])
                P.op("vector", lambda e, o=otmp[:], a=pp[:, 0:TT], b2=dv[:, c, :]:
                     e.tensor_mul(o, a, b2),
                     reads=[kpp, (kV, c)], writes=[(kOT,)])
                P.op("vector", lambda e, o=ofm[:, c, i, :], a=otmp[:],
                     b2=qkv[:, 16 + c, 1, :]: e.tensor_add(o, a, b2),
                     reads=[(kOT,), (kQ, 16 + c, 1)], writes=[(kO, c, i)])
        for b in range(B):
            for half in range(2):
                pp = C.psA[2 + 2 * (jt % 2) + half]
                kpp = ("psA", 2 + 2 * (jt % 2) + half)
                P.op("tensor", lambda e, o=pp[:], l=C.ones[0:1, :],
                     r=bproj[0:1, half * 512:(half + 1) * 512]:
                     e.matmul(o, l, r, start=True, stop=False),
                     reads=[("ones",), kBP], writes=[kpp])
                for c in range(NCH):
                    P.op("tensor", lambda e, o=pp[:], l=ofm[:, c, b, :],
                         r=wproj[:, c, half * 512:(half + 1) * 512],
                         t=(c == NCH - 1):
                         e.matmul(o, l, r, start=False, stop=t),
                         reads=[(kP2,)] + [(kO, c2, b) for c2 in range(NCH)],
                         writes=[kpp])
                xsl = C.x_tm[:, b, jt, half * 512:(half + 1) * 512]
                P.op("vector", lambda e, o=xsl, i=xsl, p=pp[:]: e.tensor_add(o, i, p),
                     reads=[("x", b, jt), kpp], writes=[("x", b, jt)])



def emit_moe(C, gw_ap, w1_ap, b1_ap, w2_ap, b2_ap):
    P, nc = C.P, C.nc
    C.arena_reset()
    gw, kG = C.carve(C.name("gw"), [128, NCH, 8], BF16)
    b1s, kB1 = C.carve(C.name("b1s"), [128, 8, 32], F32)
    P.dma("gpsimd", gw[:], gw_ap, writes=[(kG,)])
    P.dma("sync", b1s[:], b1_ap, writes=[(kB1,)])
    if "b2s" in C.smalls:
        b2s, gfm = C.smalls["b2s"], C.smalls["gfm"]
        kB2, kGF = (id(b2s),), (id(gfm),)
    else:
        b2s = C.smalls["b2s"] = C.sb(C.name("b2s_"), [8, 1024], BF16)
        gfm = C.smalls["gfm"] = C.sb(C.name("gfm_"), [8, 128], BF16)
        kB2, kGF = (id(b2s),), (id(gfm),)
        P.dma("gpsimd", b2s[:], b2_ap, writes=[kB2])

    emit_ln_all(C, LN_EPS)

    gates, kGT = C.carve(C.name("gates"), [128, B, NT, 8], F32)
    gsc, kGS = C.carve(C.name("gsc"), [128, 2, 32], F32)
    for b in range(B):
        for jt in range(NT):
            par = (b * NT + jt) % 2
            pp = C.psA[par]
            kpp = ("psA", par)
            for ci in range(NCH):
                P.op("tensor", lambda e, o=pp[:, 0:8], l=C.xf[:, ci, b, PAD + jt * 128:PAD + (jt + 1) * 128],
                     r=gw[:, ci, :], s=(ci == 0), t=(ci == NCH - 1):
                     e.matmul(o, l, r, start=s, stop=t),
                     reads=[(kG,)] + [("xf", c2, b, jt) for c2 in range(NCH)],
                     writes=[kpp])
            g = gsc[:, par, :]
            kg = (kGS, par)
            m1, m2, dif, wa, wb = (g[:, 0:1], g[:, 1:2], g[:, 2:3], g[:, 3:4], g[:, 4:5])
            eq1, eq2, msk, t1g = (g[:, 8:16], g[:, 16:24], g[:, 24:32], g[:, 5:6])
            P.op("vector", lambda e, o=m1, i=pp[:, 0:8]: e.reduce_max(o, i, axis=AX.X),
                 reads=[kpp], writes=[kg])
            P.op("vector", lambda e, o=eq1, i=pp[:, 0:8], s=m1:
                 e.tensor_scalar(o, i, s, 0.5, ALU.is_ge, ALU.mult),
                 reads=[kpp, kg], writes=[kg])
            P.op("vector", lambda e, o=msk, i=eq1, s=pp[:, 0:8]:
                 e.scalar_tensor_tensor(o, i, -1e9, s, ALU.mult, ALU.add),
                 reads=[kpp, kg], writes=[kg])
            P.op("vector", lambda e, o=m2, i=msk: e.reduce_max(o, i, axis=AX.X),
                 reads=[kg], writes=[kg])
            P.op("vector", lambda e, o=eq2, i=msk, s=m2:
                 e.tensor_scalar(o, i, s, 0.5, ALU.is_ge, ALU.mult),
                 reads=[kg], writes=[kg])
            P.op("vector", lambda e, o=dif, a=m1, b2_=m2: e.tensor_sub(o, a, b2_),
                 reads=[kg], writes=[kg])
            P.op("scalar", lambda e, o=wa, i=dif: e.activation(o, i, AF.Sigmoid),
                 reads=[kg], writes=[kg])
            P.op("scalar", lambda e, o=wb, i=dif: e.activation(o, i, AF.Sigmoid, scale=-1.0),
                 reads=[kg], writes=[kg])
            P.op("vector", lambda e, o=msk, i=eq1, s=wa:
                 e.tensor_scalar(o, i, s, None, ALU.mult),
                 reads=[kg], writes=[kg])
            P.op("vector", lambda e, o=gates[:, b, jt, :], i=eq2, s=wb, a=msk:
                 e.scalar_tensor_tensor(o, i, s, a, ALU.mult, ALU.add),
                 reads=[kg], writes=[(kGT, b, jt)])
            gcb, kgcb = (C.moe_gcb, ("moe_gcb", par))
            P.op("vector", lambda e, o=gcb[:, par, :], i=gates[:, b, jt, :]:
                 e.tensor_copy(o, i), reads=[(kGT, b, jt)], writes=[kgcb])
            pt = C.psT[par][:, 0:128]
            kpt = ("psA", 6 + par)
            P.op("tensor", lambda e, o=pt[0:8, :], i=gcb[:, par, :], idn=C.ident[:]:
                 e.transpose(o, i, idn), reads=[kgcb, ("ident",)], writes=[kpt])
            P.op("vector", lambda e, o=gfm[0:8, :], i=pt[0:8, :]: e.tensor_copy(o, i),
                 reads=[kpt], writes=[kGF])
            for half in range(2):
                bp = C.psA[2 + half]
                kbp = ("psA", 2 + half)
                P.op("tensor", lambda e, o=bp[:], l=gfm[0:8, :],
                     r=b2s[0:8, half * 512:(half + 1) * 512]:
                     e.matmul(o, l, r, start=True, stop=True),
                     reads=[kGF, kB2], writes=[kbp])
                xsl = C.x_tm[:, b, jt, half * 512:(half + 1) * 512]
                P.op("vector", lambda e, o=xsl, i=xsl, p=bp[:]: e.tensor_add(o, i, p),
                     reads=[("x", b, jt), kbp], writes=[("x", b, jt)])

    w1r = []
    w2r = []
    for par in range(2):
        t1, k1 = C.carve(C.name("w1r"), [128, NCH, 1024], BF16)
        t2, k2 = C.carve(C.name("w2r"), [128, NCH, 1024], BF16)
        w1r.append((t1, k1))
        w2r.append((t2, k2))
    h, kH = C.carve(C.name("h"), [128, NCH, 512], BF16)

    for e_ in range(8):
        for q in range(4):
            par = (e_ * 4 + q) % 2
            (w1t, kw1) = w1r[par]
            (w2t, kw2) = w2r[par]
            P.dma("gpsimd", w1t[:],
                  w1_ap[e_, :, q * 1024:(q + 1) * 1024].rearrange("(c p) m -> p c m", p=128),
                  writes=[(kw1,)])
            P.dma("gpsimd", w2t[:],
                  w2_ap[e_, q * 1024:(q + 1) * 1024, :].rearrange("(c p) m -> p c m", p=128),
                  writes=[(kw2,)])
            for b in range(B):
                for (t0, t1) in GRPS:
                    TT = t1 - t0
                    xfdeps = xf_keys(C, b, t0, t1)
                    for hc in range(NCH):
                        hp = C.psA[hc % 2]
                        khp = ("psA", hc % 2)
                        for ci in range(NCH):
                            P.op("tensor", lambda e, o=hp[:, 0:TT],
                                 l=w1t[:, ci, hc * 128:(hc + 1) * 128],
                                 r=C.xf[:, ci, b, PAD + t0:PAD + t1],
                                 s=(ci == 0), t=(ci == NCH - 1):
                                 e.matmul(o, l, r, start=s, stop=t),
                                 reads=[(kw1,)] + xfdeps, writes=[khp])
                        P.op("scalar", lambda e, o=h[:, hc, 0:TT], i=hp[:, 0:TT],
                             bb=b1s[:, e_, q * 8 + hc:q * 8 + hc + 1]:
                             e.activation(o, i, AF.Silu, bias=bb),
                             reads=[khp, (kB1,)], writes=[(kH, hc)])
                    for jt in range(t0 // 128, t1 // 128):
                        toff = jt * 128 - t0
                        yps = [C.psA[2 + 2 * (jt % 2)], C.psA[3 + 2 * (jt % 2)]]
                        kyps = [("psA", 2 + 2 * (jt % 2)), ("psA", 3 + 2 * (jt % 2))]
                        for hc in range(NCH):
                            for half in range(2):
                                P.op("tensor", lambda e, o=yps[half][:],
                                     l=h[:, hc, toff:toff + 128],
                                     r=w2t[:, hc, half * 512:(half + 1) * 512],
                                     s=(hc == 0), t=(hc == NCH - 1):
                                     e.matmul(o, l, r, start=s, stop=t),
                                     reads=[(kw2,)] + [(kH, h2) for h2 in range(NCH)],
                                     writes=[kyps[half]])
                        for half in range(2):
                            xsl = C.x_tm[:, b, jt, half * 512:(half + 1) * 512]
                            kx = ("x", b, jt)
                            P.op("vector", lambda e, o=xsl, i=yps[half][:],
                                 s=gates[:, b, jt, e_:e_ + 1], a=xsl:
                                 e.scalar_tensor_tensor(o, i, s, a, ALU.mult, ALU.add),
                                 reads=[kyps[half], (kGT, b, jt), kx], writes=[kx])


GRP = 384
NG = 19
NTAB = NG * GRP
WROWS = 8 * 8 * 128


def moe_dram(C):
    if not hasattr(C, "xfd"):
        nc = C.nc
        C.xfd = nc.dram_tensor("xfd", [B * NTOK, 1024], BF16, kind="Internal").ap()
        C.tabd = nc.dram_tensor("tabd", [NTAB, 2], F32, kind="Internal").ap()
        C.yslot = nc.dram_tensor("yslot", [NTAB, 1024], F32, kind="Internal").ap()
    return C.xfd, C.tabd, C.yslot


def moe_regs(C):
    if not hasattr(C, "moeregs"):
        C.moeregs = {}

        def setup(e, d=C.moeregs):
            for nm, val in (("tab", NTAB - 1), ("tok", B * NTOK - 1),
                            ("wrow", WROWS - 1), ("b2row", 7)):
                r = e.alloc_register("bc_" + nm)
                inst = e.reg_mov(r, val)
                d[nm] = r
            return inst

        C.P.op("gpsimd", setup)
    return C.moeregs


def emit_moe_sparse(C, gw_ap, w1s_ap, w2s_ap, b2_ap,
                    l2_ap, l18_ap, if32_ap, rconst_ap,
                    thr_ap, iotag_ap, iotap_ap):
    P, nc = C.P, C.nc
    xfd, tabd, yslot = moe_dram(C)
    rgs = moe_regs(C)
    C.arena_reset()
    gw, kG = C.carve(C.name("gw"), [128, NCH, 8], BF16)
    l2c, kL2 = C.carve(C.name("l2c"), [128, 128], F32)
    if32, kI32 = C.carve(C.name("if32"), [128, 128], F32)
    l18, kL18 = C.carve(C.name("l18"), [128, 18], F32)
    rcn, kRC = C.carve(C.name("rcn"), [128, B, NT, 3], F32)
    thr, kTH = C.carve(C.name("thr"), [128, 11], F32)
    iotag, kIG = C.carve(C.name("iotag"), [128, NG], F32)
    iotap, kIP = C.carve(C.name("iotap"), [128, NG], F32)
    twos, kTW = C.carve(C.name("twos"), [128, 1], F32)
    onesr, kOR = C.carve(C.name("onesr"), [128, 128], F32)
    P.dma("gpsimd", gw[:], gw_ap, writes=[(kG,)])
    P.dma("sync", l2c[:], l2_ap, writes=[(kL2,)])
    P.dma("sync", if32[:], if32_ap, writes=[(kI32,)])
    P.dma("sync", l18[0:18, 0:18], l18_ap, writes=[(kL18,)])
    P.dma("sync", rcn[:], rconst_ap, writes=[(kRC,)])
    P.dma("sync", thr[:], thr_ap, writes=[(kTH,)])
    P.dma("sync", iotag[:], iotag_ap, writes=[(kIG,)])
    P.dma("sync", iotap[:], iotap_ap, writes=[(kIP,)])
    P.op("vector", lambda e: e.memset(twos[:], 2.0), writes=[(kTW,)])
    P.op("vector", lambda e: e.memset(onesr[:], 1.0), writes=[(kOR,)])
    if "b2s" in C.smalls:
        b2s = C.smalls["b2s"]
        kB2 = (id(b2s),)
    else:
        b2s = C.smalls["b2s"] = C.sb(C.name("b2s_"), [8, 1024], BF16)
        kB2 = (id(b2s),)
        P.dma("gpsimd", b2s[:], b2_ap, writes=[kB2])

    pre, kPRE = C.carve(C.name("pre"), [128, NTAB // 128, 2], F32)
    P.op("vector", lambda e: e.memset(pre[:], 1.0e9), writes=[(kPRE,)])
    P.dma("gpsimd", tabd.rearrange("(t p) w -> p t w", p=128), pre[:],
          reads=[(kPRE,)], writes=[("tabd",)])

    emit_ln_all(C, LN_EPS, dram_out=xfd)

    gates, kGT = C.carve(C.name("gates"), [128, B, NT, 8], F32)
    asn12, kAS = C.carve(C.name("asn12"), [128, B, NT, 16], F32)
    gsc, kGS = C.carve(C.name("gsc"), [128, 2, 32], F32)
    asnc, kAC = C.carve(C.name("asnc"), [128, 2, 8], F32)

    for b in range(B):
        for jt in range(NT):
            ji = b * NT + jt
            par = ji % 2
            pp = C.psA[par]
            kpp = ("psA", par)
            for ci in range(NCH):
                P.op("tensor", lambda e, o=pp[:, 0:8], l=C.xf[:, ci, b, PAD + jt * 128:PAD + (jt + 1) * 128],
                     r=gw[:, ci, :], s=(ci == 0), t=(ci == NCH - 1):
                     e.matmul(o, l, r, start=s, stop=t),
                     reads=[(kG,)] + [("xf", c2, b, jt) for c2 in range(NCH)],
                     writes=[kpp])
            g = gsc[:, par, :]
            kg = (kGS, par)
            m1, m2, dif, wa, wb = (g[:, 0:1], g[:, 1:2], g[:, 2:3], g[:, 3:4], g[:, 4:5])
            msk = g[:, 24:32]
            eq1 = asn12[:, b, jt, 0:8]
            eq2 = asn12[:, b, jt, 8:16]
            kq = (kAS, ji)
            v05 = rcn[:, b, jt, 0:1]
            P.op("vector", lambda e, o=m1, i=pp[:, 0:8]: e.reduce_max(o, i, axis=AX.X),
                 reads=[kpp], writes=[kg])
            P.op("vector", lambda e, o=eq1, i=pp[:, 0:8], s=m1, s2=v05:
                 e.tensor_scalar(o, i, s, s2, ALU.is_ge, ALU.mult),
                 reads=[kpp, kg, (kRC,)], writes=[kq])
            P.op("vector", lambda e, o=msk, i=eq1, s=pp[:, 0:8]:
                 e.scalar_tensor_tensor(o, i, -2e9, s, ALU.mult, ALU.add),
                 reads=[kpp, kq], writes=[kg])
            P.op("vector", lambda e, o=m2, i=msk: e.reduce_max(o, i, axis=AX.X),
                 reads=[kg], writes=[kg])
            P.op("vector", lambda e, o=eq2, i=msk, s=m2, s2=v05:
                 e.tensor_scalar(o, i, s, s2, ALU.is_ge, ALU.mult),
                 reads=[kg, (kRC,)], writes=[kq])
            P.op("vector", lambda e, o=dif, a=m1, b2_=m2: e.tensor_sub(o, a, b2_),
                 reads=[kg], writes=[kg])
            P.op("scalar", lambda e, o=wa, i=dif: e.activation(o, i, AF.Sigmoid),
                 reads=[kg], writes=[kg])
            P.op("scalar", lambda e, o=wb, i=dif: e.activation(o, i, AF.Sigmoid, scale=-1.0),
                 reads=[kg], writes=[kg])
            P.op("vector", lambda e, o=msk, i=eq1, s=wa:
                 e.tensor_scalar(o, i, s, None, ALU.mult),
                 reads=[kg, kq], writes=[kg])
            P.op("vector", lambda e, o=gates[:, b, jt, :], i=eq2, s=wb, a=msk:
                 e.scalar_tensor_tensor(o, i, s, a, ALU.mult, ALU.add),
                 reads=[kg, kq], writes=[(kGT, b, jt)])
            P.op("vector", lambda e, o=asnc[:, par, :], a=eq1, b2_=eq2:
                 e.tensor_add(o, a, b2_), reads=[kq], writes=[(kAC, par)])
            P.op("tensor", lambda e, o=C.psA[4][0:8, ji:ji + 1], l=asnc[:, par, :],
                 r=twos[:], s=True, t=True: e.matmul(o, l, r, start=s, stop=t),
                 reads=[(kAC, par), (kTW,)], writes=[("psA", 4)])

    cnt_sb, kCS = C.carve(C.name("cnt_sb"), [128, 18], F32)
    cstat, kCT2 = C.carve(C.name("cstat"), [128, 16], F32)
    ge11, kGE = C.carve(C.name("ge11"), [128, 11], F32)
    cntT, kCT = C.carve(C.name("cntT"), [128, 8], F32)
    baseT, kBT = C.carve(C.name("baseT"), [128, 8], F32)
    bases18, kBA = C.carve(C.name("bases18"), [128, 8], F32)
    geof, kGO = C.carve(C.name("geof"), [128, NG], F32)
    eofr, kEO = C.carve(C.name("eofr"), [128, NG], F32)
    widxf, kWF = C.carve(C.name("widxf"), [128, 8, NG], F32)
    widxu, kWU = C.carve(C.name("widxu"), [128, 8, NG], mybir.dt.uint32)
    gebase, kGB = C.carve(C.name("gebase"), [128, NG], F32)
    ohg, kOH = C.carve(C.name("ohg"), [128, NG], F32)

    P.op("vector", lambda e, o=cnt_sb[0:8, :], i=C.psA[4][0:8, 0:18]:
         e.tensor_copy(o, i), reads=[("psA", 4)], writes=[(kCS,)])
    P.op("vector", lambda e, o=cstat[0:8, 0:1], i=cnt_sb[0:8, :]:
         e.reduce_sum(o, i, axis=AX.X), reads=[(kCS,)], writes=[(kCT2,)])
    P.op("vector", lambda e, o=ge11[0:8, :], i=thr[0:8, :], s=cstat[0:8, 0:1]:
         e.tensor_scalar(o, i, s, None, ALU.is_le),
         reads=[(kTH,), (kCT2,)], writes=[(kGE,)])
    P.op("vector", lambda e, o=cstat[0:8, 1:2], i=ge11[0:8, :]:
         e.reduce_sum(o, i, axis=AX.X), reads=[(kGE,)], writes=[(kCT2,)])
    P.op("tensor", lambda e, o=C.psA[5][0:8, 0:1], l=l18[0:8, 0:8],
         r=cstat[0:8, 1:2], s=True, t=True: e.matmul(o, l, r, start=s, stop=t),
         reads=[(kL18,), (kCT2,)], writes=[("psA", 5)])
    P.op("vector", lambda e, o=cstat[0:8, 2:3], i=C.psA[5][0:8, 0:1]:
         e.tensor_copy(o, i), reads=[("psA", 5)], writes=[(kCT2,)])
    P.op("vector", lambda e, o=cstat[0:8, 3:4], a=cstat[0:8, 2:3], b2_=cstat[0:8, 1:2]:
         e.tensor_add(o, a, b2_), reads=[(kCT2,)], writes=[(kCT2,)])
    P.op("vector", lambda e, o=cstat[0:8, 4:5], i=cstat[0:8, 2:3]:
         e.tensor_scalar_mul(o, i, float(GRP)), reads=[(kCT2,)], writes=[(kCT2,)])
    P.op("tensor", lambda e, o=C.psA[6][0:18, 0:8],
         i=cnt_sb[0:8, 0:18], idn=if32[0:8, 0:8]: e.transpose(o, i, idn),
         reads=[(kCS,), (kI32,)], writes=[("psA", 6)])
    P.op("vector", lambda e, o=cntT[0:18, :], i=C.psA[6][0:18, 0:8]:
         e.tensor_copy(o, i), reads=[("psA", 6)], writes=[(kCT,)])
    P.op("tensor", lambda e, o=C.psA[7][0:1, 0:8], i=cstat[0:8, 4:5],
         idn=if32[0:8, 0:8]: e.transpose(o, i, idn),
         reads=[(kCT2,), (kI32,)], writes=[("psA", 7)])
    P.op("vector", lambda e, o=baseT[0:1, :], i=C.psA[7][0:1, 0:8]:
         e.tensor_copy(o, i), reads=[("psA", 7)], writes=[(kBT,)])
    P.op("tensor", lambda e, o=C.psA[2][0:18, 0:8], l=l18[0:18, 0:18],
         r=cntT[0:18, :], s=True, t=False: e.matmul(o, l, r, start=s, stop=t),
         reads=[(kL18,), (kCT,)], writes=[("psA", 2)])
    P.op("tensor", lambda e, o=C.psA[2][0:18, 0:8], l=onesr[0:1, 0:18],
         r=baseT[0:1, :], s=False, t=True: e.matmul(o, l, r, start=s, stop=t),
         reads=[(kOR,), (kBT,)], writes=[("psA", 2)])
    P.op("vector", lambda e, o=bases18[0:18, :], i=C.psA[2][0:18, 0:8]:
         e.tensor_copy(o, i), reads=[("psA", 2)], writes=[(kBA,)])
    P.op("vector", lambda e, o=geof[0:8, :], i=iotag[0:8, :], s=cstat[0:8, 3:4]:
         e.tensor_scalar(o, i, s, None, ALU.is_ge),
         reads=[(kIG,), (kCT2,)], writes=[(kGO,)])
    P.op("tensor", lambda e, o=C.psA[5][0:1, 0:NG], l=onesr[0:8, 0:1],
         r=geof[0:8, :], s=True, t=True: e.matmul(o, l, r, start=s, stop=t),
         reads=[(kOR,), (kGO,)], writes=[("psA", 5)])
    P.op("vector", lambda e, o=eofr[0:1, :], i=C.psA[5][0:1, 0:NG]:
         e.tensor_copy(o, i), reads=[("psA", 5)], writes=[(kEO,)])
    P.op("tensor", lambda e, o=C.psA[5][0:128, 0:NG], l=onesr[0:1, :],
         r=eofr[0:1, :], s=True, t=True: e.matmul(o, l, r, start=s, stop=t),
         reads=[(kOR,), (kEO,)], writes=[("psA", 5)])
    P.op("vector", lambda e, o=gebase[0:8, :], i=iotag[0:8, :], s=cstat[0:8, 2:3]:
         e.tensor_scalar(o, i, s, None, ALU.is_ge),
         reads=[(kIG,), (kCT2,)], writes=[(kGB,)])
    P.op("vector", lambda e, o=ohg[0:8, :], a=gebase[0:8, :], b2_=geof[0:8, :]:
         e.tensor_sub(o, a, b2_), reads=[(kGB,), (kGO,)], writes=[(kOH,)])
    P.op("vector", lambda e, o=widxf[:, 0, :], i=C.psA[5][0:128, 0:NG],
         t2=iotap[:]: e.scalar_tensor_tensor(o, i, 1024.0, t2, ALU.mult, ALU.add),
         reads=[("psA", 5), (kIP,)], writes=[(kWF,)])
    for f in range(1, 8):
        P.op("vector", lambda e, o=widxf[:, f, :], i=widxf[:, 0, :], v=float(f * 128):
             e.tensor_scalar_add(o, i, v), reads=[(kWF,)], writes=[(kWF,)])
    P.op("vector", lambda e, o=widxu[:], i=widxf[:]: e.tensor_copy(o, i),
         reads=[(kWF,)], writes=[(kWU,)])

    brow, kBR = C.carve(C.name("brow"), [128, 2, 8], F32)
    asn2t, kA2 = C.carve(C.name("asn2t"), [128, 2, 16], F32)
    prodt, kPR = C.carve(C.name("prodt"), [128, 2, 8], F32)
    gp8, kGP = C.carve(C.name("gp8"), [128, 2, 8], F32)
    idxg, kIX = C.carve(C.name("idxg"), [128, 2, 2, 3], F32)
    slots_tok, kST = C.carve(C.name("slots_tok"), [128, B, NT, 2], mybir.dt.uint32)

    for b in range(B):
        for jt in range(NT):
            ji = b * NT + jt
            par = ji % 2
            P.op("vector", lambda e, o=asnc[:, par, :], a=asn12[:, b, jt, 0:8],
                 b2_=asn12[:, b, jt, 8:16]: e.tensor_add(o, a, b2_),
                 reads=[(kAS, ji)], writes=[(kAC, par)])
            P.op("tensor", lambda e, o=C.psA[3][0:1, 0:8], l=if32[0:18, ji:ji + 1],
                 r=bases18[0:18, :], s=True, t=True: e.matmul(o, l, r, start=s, stop=t),
                 reads=[(kI32,), (kBA,)], writes=[("psA", 3)])
            P.op("vector", lambda e, o=brow[0:1, par, :], i=C.psA[3][0:1, 0:8]:
                 e.tensor_copy(o, i), reads=[("psA", 3)], writes=[(kBR, par)])
            sg = C.psA[2]
            ksg = ("psA", 2)
            P.op("tensor", lambda e, o=sg[:, 0:8], l=l2c[:], r=asnc[:, par, :],
                 s=True, t=False: e.matmul(o, l, r, start=s, stop=t),
                 reads=[(kL2,), (kAC, par)], writes=[ksg])
            P.op("tensor", lambda e, o=sg[:, 0:8], l=onesr[0:1, :],
                 r=brow[0:1, par, :], s=False, t=True: e.matmul(o, l, r, start=s, stop=t),
                 reads=[(kOR,), (kBR, par)], writes=[ksg])
            P.op("vector", lambda e, o=asn2t[:, par, :], i=asn12[:, b, jt, :]:
                 e.tensor_scalar_mul(o, i, 2.0), reads=[(kAS, ji)], writes=[(kA2, par)])
            for c in range(2):
                P.op("vector", lambda e, o=prodt[:, par, :], a=asn2t[:, par, c * 8:(c + 1) * 8],
                     b2_=sg[:, 0:8]: e.tensor_mul(o, a, b2_),
                     reads=[(kA2, par), ksg], writes=[(kPR, par)])
                P.op("vector", lambda e, o=idxg[:, par, c, 2:3],
                     i=prodt[:, par, :]:
                     e.reduce_sum(o, i, axis=AX.X), reads=[(kPR, par)], writes=[(kIX, par, c)])
                P.op("vector", lambda e, o=idxg[:, par, c, 2:3], i=idxg[:, par, c, 2:3],
                     a=rcn[:, b, jt, 1:2]: e.tensor_add(o, i, a),
                     reads=[(kIX, par, c), (kRC,)], writes=[(kIX, par, c)])
                P.op("vector", lambda e, o=idxg[:, par, c, 0:1], i=rcn[:, b, jt, 2:3]:
                     e.tensor_copy(o, i), reads=[(kRC,)], writes=[(kIX, par, c)])
                P.op("vector", lambda e, o=gp8[:, par, :], a=asn2t[:, par, c * 8:(c + 1) * 8],
                     b2_=gates[:, b, jt, :]: e.tensor_mul(o, a, b2_),
                     reads=[(kA2, par), (kGT, b, jt)], writes=[(kGP, par)])
                P.op("vector", lambda e, o=idxg[:, par, c, 1:2], i=gp8[:, par, :]:
                     e.reduce_sum(o, i, axis=AX.X), reads=[(kGP, par)], writes=[(kIX, par, c)])
                P.op("vector", lambda e, o=slots_tok[:, b, jt, c:c + 1], i=idxg[:, par, c, 2:3]:
                     e.tensor_copy(o, i), reads=[(kIX, par, c)], writes=[(kST, ji, c)])
                P.op("gpsw",
                     lambda e, o=tabd, ix=slots_tok[:, b, jt, c:c + 1], i=idxg[:, par, c, 0:2],
                     rg=rgs:
                     e.indirect_dma_start(out=o, out_offset=bass.IndirectOffsetOnAxis(ix, 0),
                                          in_=i, in_offset=None,
                                          bounds_check=rg["tab"], oob_is_err=False),
                     reads=[(kST, ji, c), (kIX, par, c)], writes=[("tabd",)],
                     is_dma=True)

    tg, kTGb = C.carve(C.name("tg"), [128, 2, 3, 2], F32)
    idxu, kIU = C.carve(C.name("idxu"), [128, 2, 3], mybir.dt.uint32)
    xg_tm, kXT = C.carve(C.name("xg_tm"), [128, 3, 1024], BF16)
    xg_fm, kXF = C.carve(C.name("xg_fm"), [128, 2, NCH, GRP], BF16)
    h_sb, kH = C.carve(C.name("h_sb"), [128, 2, 4, GRP], BF16)
    ygs, kYG = C.carve(C.name("ygs"), [128, 3, 1024], F32)
    w1g, kW1 = C.carve(C.name("w1g"), [128, 2, 4104], BF16)
    w1b = w1g.bitcast(F32)
    w2g, kW2 = C.carve(C.name("w2g"), [128, 2, 4096], BF16)
    ohb, kOB = C.carve(C.name("ohb"), [128, 2, 128], BF16)
    w1p = w1g[:, :, 0:4096].rearrange("p r (cp i m) -> p r cp i m", cp=4, i=2, m=512)
    w2p = w2g.rearrange("p r (cp i m) -> p r cp i m", cp=2, i=2, m=1024)

    def g_gather(gidx):
        parg = gidx % 2
        P.dma("sync", tg[:, parg, :, :],
              tabd[gidx * GRP:(gidx + 1) * GRP, :].rearrange("(t p) w -> p t w", p=128),
              reads=[("tabd",)], writes=[(kTGb, parg)])
        P.op("vector", lambda e, o=idxu[:, parg, :], i=tg[:, parg, :, 0]:
             e.tensor_copy(o, i), reads=[(kTGb, parg)], writes=[(kIU, parg)])
        for ct in range(3):
            pt2 = ct
            P.op("gpsw",
                 lambda e, o=xg_tm[:, pt2, :], ix=idxu[:, parg, ct:ct + 1], rg=rgs:
                 e.indirect_dma_start(out=o, out_offset=None, in_=xfd,
                                      in_offset=bass.IndirectOffsetOnAxis(ix, 0),
                                      bounds_check=rg["tok"], oob_is_err=False),
                 reads=[(kIU, parg), ("xfd",)], writes=[(kXT, pt2)], is_dma=True)

    def g_transpose(gidx):
        parg = gidx % 2
        for ct in range(3):
            pt2 = ct
            for cc in range(NCH):
                pt = C.psT[cc % 2][:, 0:128]
                kpt = ("psA", 6 + cc % 2)
                P.op("tensor", lambda e, o=pt, i=xg_tm[:, pt2, cc * 128:(cc + 1) * 128],
                     idn=C.ident[:]: e.transpose(o, i, idn),
                     reads=[(kXT, pt2), ("ident",)], writes=[kpt])
                P.op("vector", lambda e, o=xg_fm[:, parg, cc, ct * 128:(ct + 1) * 128],
                     i=pt: e.tensor_copy(o, i), reads=[kpt], writes=[(kXF, parg, cc)])

    for gidx in range(NG):
        parg = gidx % 2
        if gidx == 0:
            g_gather(0)
        g_transpose(gidx)
        if gidx + 1 < NG:
            g_gather(gidx + 1)
        P.op("vector", lambda e, o=ohb[0:8, parg, :], i=onesr[0:8, 0:128],
             s=ohg[0:8, gidx:gidx + 1]:
             e.tensor_scalar(o, i, s, None, ALU.mult),
             reads=[(kOR,), (kOH,)], writes=[(kOB, parg)])
        def emit_w2(f):
            parf = f % 2
            for ct in range(3):
                for half in range(2):
                    yb = C.psA[ct * 2 + half]
                    kyb = ("psA", ct * 2 + half)
                    for hc in range(4):
                        P.op("tensor", lambda e, o=yb[:],
                             l=h_sb[:, parf, hc, ct * 128:(ct + 1) * 128],
                             r=w2g[:, parf, hc * 1024 + half * 512: hc * 1024 + (half + 1) * 512],
                             s=(f == 0 and hc == 0):
                             e.matmul(o, l, r, start=s, stop=False),
                             reads=[(kW2, parf)] + [(kH, parf, h2) for h2 in range(4)],
                             writes=[kyb])

        for f in range(8):
            parf = f % 2
            P.op("gpsw",
                 lambda e, o=w1g[:, parf, :], ix=widxu[:, f, gidx:gidx + 1], rg=rgs:
                 e.indirect_dma_start(out=o, out_offset=None, in_=w1s_ap,
                                      in_offset=bass.IndirectOffsetOnAxis(ix, 0),
                                      bounds_check=rg["wrow"], oob_is_err=False),
                 reads=[(kWU,)], writes=[(kW1, parf)], is_dma=True)
            P.op("gpsw",
                 lambda e, o=w2g[:, parf, :], ix=widxu[:, f, gidx:gidx + 1], rg=rgs:
                 e.indirect_dma_start(out=o, out_offset=None, in_=w2s_ap,
                                      in_offset=bass.IndirectOffsetOnAxis(ix, 0),
                                      bounds_check=rg["wrow"], oob_is_err=False),
                 reads=[(kWU,)], writes=[(kW2, parf)], is_dma=True)
            for hc in range(4):
                hp = C.psA[6 + hc % 2]
                khp = ("psA", 6 + hc % 2)
                for ci in range(NCH):
                    P.op("tensor", lambda e, o=hp[:, 0:GRP],
                         l=w1g[:, parf, ci * 512 + hc * 128: ci * 512 + (hc + 1) * 128],
                         r=xg_fm[:, parg, ci, :],
                         s=(ci == 0), t=(ci == NCH - 1):
                         e.matmul(o, l, r, start=s, stop=t),
                         reads=[(kW1, parf)] + [(kXF, parg, c2) for c2 in range(NCH)],
                         writes=[khp])
                P.op("scalar", lambda e, o=h_sb[:, parf, hc, :], i=hp[:, 0:GRP],
                     bb=w1b[:, parf, 2048 + hc:2049 + hc]:
                     e.activation(o, i, AF.Silu, bias=bb),
                     reads=[khp, (kW1, parf)], writes=[(kH, parf, hc)])
            if f > 0:
                emit_w2(f - 1)
        emit_w2(7)
        for ct in range(3):
            for half in range(2):
                yb = C.psA[ct * 2 + half]
                kyb = ("psA", ct * 2 + half)
                P.op("tensor", lambda e, o=yb[:], l=ohb[0:8, parg, :],
                     r=b2s[0:8, half * 512:(half + 1) * 512]:
                     e.matmul(o, l, r, start=False, stop=True),
                     reads=[(kOB, parg), kB2], writes=[kyb])
        for ct in range(3):
            for half in range(2):
                yb = C.psA[ct * 2 + half]
                kyb = ("psA", ct * 2 + half)
                P.op("vector", lambda e, o=ygs[:, ct, half * 512:(half + 1) * 512],
                     i=yb[:], s=tg[:, parg, ct, 1:2]:
                     e.tensor_scalar(o, i, s, None, ALU.mult),
                     reads=[kyb, (kTGb, parg)], writes=[(kYG, ct)])
        P.dma("scalar", yslot[gidx * GRP:(gidx + 1) * GRP, :].rearrange("(t p) d -> p t d", p=128),
              ygs[:], reads=[(kYG, 0), (kYG, 1), (kYG, 2)], writes=[("yslot",)])

    rb, kRB = C.carve(C.name("rb"), [128, 2, 1024], F32)
    for b in range(B):
        for jt in range(NT):
            for c in range(2):
                par = c
                P.op("gpsw",
                     lambda e, o=rb[:, par, :], ix=slots_tok[:, b, jt, c:c + 1], rg=rgs:
                     e.indirect_dma_start(out=o, out_offset=None, in_=yslot,
                                          in_offset=bass.IndirectOffsetOnAxis(ix, 0),
                                          bounds_check=rg["tab"], oob_is_err=False),
                     reads=[(kST, b * NT + jt, c), ("yslot",)], writes=[(kRB, par)],
                     is_dma=True)
                xsl = C.x_tm[:, b, jt, :]
                P.op("vector", lambda e, o=xsl, i=xsl, a=rb[:, par, :]:
                     e.tensor_add(o, i, a),
                     reads=[("x", b, jt), (kRB, par)], writes=[("x", b, jt)])



def emit_s5(C, wq_ap, wk_ap, wv_ap, wo_ap, qkn_ap, alibi_ap, alibi0_ap):
    P, nc = C.P, C.nc
    C.arena_reset()
    ws = {}
    for nm, ap in (("wq", wq_ap), ("wk", wk_ap), ("wv", wv_ap), ("wo", wo_ap)):
        t, k = C.carve(C.name(nm), [128, NCH, 1024], BF16)
        P.dma("gpsimd", t[:], ap.rearrange("(c p) m -> p c m", p=128), writes=[(k,)])
        ws[nm] = (t, k)
    qkn, kN = C.carve(C.name("qkn"), [128, 16], F32)
    P.dma("sync", qkn[:], qkn_ap, writes=[(kN,)])
    alibi, kAL = C.carve(C.name("alibi"), [128, 8, 144], BF16)
    P.dma("sync", alibi[:], alibi_ap, writes=[(kAL,)])
    alibi0, kAL0 = C.carve(C.name("alibi0"), [128, 8, 48], BF16)
    P.dma("sync", alibi0[:], alibi0_ap, writes=[(kAL0,)])

    emit_ln_all(C, LN_EPS)

    q_fm, kQF = C.carve(C.name("q_fm"), [128, NCH, 128], BF16)
    k_fm, kKF = C.carve(C.name("k_fm"), [128, NCH, 3, 128], BF16)
    v_tm, kVT = C.carve(C.name("v_tm"), [128, 3, 1024], BF16)
    at_fm, kAT = C.carve(C.name("at_fm"), [128, NCH, 128], BF16)
    pbn, kPN = C.carve(C.name("pbn"), [128, 3, 144], BF16)
    pta, kTA = C.carve(C.name("pta"), [128, 3, 128], BF16)
    ptb, kTB = C.carve(C.name("ptb"), [128, 3, 128], BF16)
    qs_tm, kQS = C.carve(C.name("qs_tm"), [128, 1024], BF16)
    st5, kS5 = C.carve(C.name("st5"), [128, 4, 8], F32)

    def proj_rms(b, tok0, dst, kdst, dstoff, wname, normcol, par):
        nt = min(128, NTOK - tok0)
        (wt, kw) = ws[wname]
        st = st5[:, par, :]
        kst = (kS5, par)
        xdeps = xf_keys(C, b, max(tok0, 0), tok0 + nt) + [("xfpad",)]
        pq = [C.psA[4], C.psA[5]]
        for half in range(2):
            kqp = ("psA", 4 + half)
            for ci in range(NCH):
                P.op("tensor", lambda e, o=pq[half][0:nt, :],
                     l=C.xf[:, ci, b, PAD + tok0:PAD + tok0 + nt],
                     r=wt[:, ci, half * 512:(half + 1) * 512],
                     s=(ci == 0), t=(ci == NCH - 1):
                     e.matmul(o, l, r, start=s, stop=t),
                     reads=[(kw,)] + xdeps, writes=[kqp])
        if normcol is None:
            for half in range(2):
                P.op("scalar", lambda e, o=dst[:, dstoff + half * 512:dstoff + (half + 1) * 512],
                     i=pq[half][:]: e.activation(o, i, AF.Identity),
                     reads=[("psA", 4 + half)], writes=[(kdst, dstoff)])
            return
        for half in range(2):
            P.op("scalar", lambda e, o=qs_tm[0:nt, half * 512:(half + 1) * 512],
                 i=pq[half][0:nt, :], a=st[0:nt, half:half + 1]:
                 e.activation(o, i, AF.Square, accum_out=a),
                 reads=[("psA", 4 + half)], writes=[kst, (kQS, "scratch")])
        P.op("vector", lambda e, o=st[0:nt, 2:3], a=st[0:nt, 0:1], b2=st[0:nt, 1:2]:
             e.tensor_add(o, a, b2), reads=[kst], writes=[kst])
        P.op("scalar", lambda e, o=st[0:nt, 3:4], i=st[0:nt, 2:3], bb=C.epsv[0:nt, 1:2]:
             e.activation(o, i, AF.Sqrt, scale=1.0 / 1024.0, bias=bb),
             reads=[kst, ("epsv",)], writes=[kst])
        P.op("vector", lambda e, o=st[0:nt, 4:5], i=st[0:nt, 3:4]: e.reciprocal(o, i),
             reads=[kst], writes=[kst])
        for half in range(2):
            P.op("vector", lambda e, o=qs_tm[0:nt, half * 512:(half + 1) * 512],
                 i=pq[half][0:nt, :], s=st[0:nt, 4:5]:
                 e.tensor_scalar(o, i, s, None, ALU.mult),
                 reads=[("psA", 4 + half), kst, (kQS, "scratch")], writes=[(kQS,)])
        for c in range(NCH):
            pt = C.psT[c % 2][:, 0:128]
            kpt = ("psA", 6 + c % 2)
            P.op("tensor", lambda e, o=pt[:, 0:nt], i=qs_tm[0:nt, c * 128:(c + 1) * 128],
                 idn=C.ident[0:nt, 0:nt]: e.transpose(o, i, idn),
                 reads=[(kQS,)], writes=[kpt])
            P.op("vector", lambda e, o=dst[:, c, dstoff:dstoff + nt], i=pt[:, 0:nt],
                 s=qkn[:, normcol + c:normcol + c + 1]:
                 e.tensor_scalar(o, i, s, None, ALU.mult),
                 reads=[kpt, (kN,)], writes=[(kdst, dstoff)])

    def proj_v(b, tok0, vi):
        nt = min(128, NTOK - tok0)
        (wt, kw) = ws["wv"]
        xdeps = xf_keys(C, b, max(0, tok0), tok0 + nt) + [("xfpad",)]
        for half in range(2):
            pv = C.psA[4 + half]
            kvp = ("psA", 4 + half)
            for ci in range(NCH):
                P.op("tensor", lambda e, o=pv[0:nt, :],
                     l=C.xf[:, ci, b, PAD + tok0:PAD + tok0 + nt],
                     r=wt[:, ci, half * 512:(half + 1) * 512],
                     s=(ci == 0), t=(ci == NCH - 1):
                     e.matmul(o, l, r, start=s, stop=t),
                     reads=[(kw,)] + xdeps, writes=[kvp])
            P.op("scalar", lambda e, o=v_tm[0:nt, vi, half * 512:(half + 1) * 512],
                 i=pv[0:nt, :]: e.activation(o, i, AF.Identity),
                 reads=[kvp], writes=[(kVT, vi)])

    def proj_rms2(b, tok0, sl):
        nt = min(128, NTOK - tok0)
        (wt, kw) = ws["wk"]
        par = sl % 2
        st = st5[:, par, :]
        kst = (kS5, par)
        xdeps = xf_keys(C, b, max(tok0, 0), tok0 + nt) + [("xfpad",)]
        pq = [C.psA[4], C.psA[5]]
        for half in range(2):
            kqp = ("psA", 4 + half)
            for ci in range(NCH):
                P.op("tensor", lambda e, o=pq[half][0:nt, :],
                     l=C.xf[:, ci, b, PAD + tok0:PAD + tok0 + nt],
                     r=wt[:, ci, half * 512:(half + 1) * 512],
                     s=(ci == 0), t=(ci == NCH - 1):
                     e.matmul(o, l, r, start=s, stop=t),
                     reads=[(kw,)] + xdeps, writes=[kqp])
        for half in range(2):
            P.op("scalar", lambda e, o=qs_tm[0:nt, half * 512:(half + 1) * 512],
                 i=pq[half][0:nt, :], a=st[0:nt, half:half + 1]:
                 e.activation(o, i, AF.Square, accum_out=a),
                 reads=[("psA", 4 + half)], writes=[kst, (kQS, "scratch")])
        P.op("vector", lambda e, o=st[0:nt, 2:3], a=st[0:nt, 0:1], b2=st[0:nt, 1:2]:
             e.tensor_add(o, a, b2), reads=[kst], writes=[kst])
        P.op("scalar", lambda e, o=st[0:nt, 3:4], i=st[0:nt, 2:3], bb=C.epsv[0:nt, 1:2]:
             e.activation(o, i, AF.Sqrt, scale=1.0 / 1024.0, bias=bb),
             reads=[kst, ("epsv",)], writes=[kst])
        P.op("vector", lambda e, o=st[0:nt, 4:5], i=st[0:nt, 3:4]: e.reciprocal(o, i),
             reads=[kst], writes=[kst])
        for half in range(2):
            P.op("vector", lambda e, o=qs_tm[0:nt, half * 512:(half + 1) * 512],
                 i=pq[half][0:nt, :], s=st[0:nt, 4:5]:
                 e.tensor_scalar(o, i, s, None, ALU.mult),
                 reads=[("psA", 4 + half), kst, (kQS, "scratch")], writes=[(kQS,)])
        for c in range(NCH):
            pt = C.psT[c % 2][:, 0:128]
            kpt = ("psA", 6 + c % 2)
            P.op("tensor", lambda e, o=pt[:, 0:nt], i=qs_tm[0:nt, c * 128:(c + 1) * 128],
                 idn=C.ident[0:nt, 0:nt]: e.transpose(o, i, idn),
                 reads=[(kQS,)], writes=[kpt])
            P.op("vector", lambda e, o=k_fm[:, c, sl, 0:nt], i=pt[:, 0:nt],
                 s=qkn[:, 8 + c:9 + c]:
                 e.tensor_scalar(o, i, s, None, ALU.mult),
                 reads=[kpt, (kN,)], writes=[(kKF, sl)])

    (wot, kwo) = ws["wo"]
    for b in range(B):
        for jt in range(NT):
            if jt == 0:
                proj_rms2(b, -16, 0)
                proj_v(b, -16, 0)
            proj_rms2(b, jt * 128 + 112, (jt + 1) % 3)
            proj_v(b, jt * 128 + 112, (jt + 1) % 3)
            proj_rms(b, jt * 128, q_fm, kQF, 0, "wq", 0, jt % 2)
            sa = jt % 3
            sb2 = (jt + 1) % 3

            def emit_qk(hh):
                par = (jt * 8 + hh) % 2
                sp = C.psA[0 + 2 * par]
                ksp = ("psA", 0 + 2 * par)
                P.op("tensor", lambda e, o=sp[:, 0:128],
                     l=q_fm[:, hh, :], r=k_fm[:, hh, sa, :]:
                     e.matmul(o, l, r, start=True, stop=True),
                     reads=[(kQF, 0), (kKF, sa)], writes=[ksp])
                P.op("tensor", lambda e, o=sp[:, 128:144],
                     l=q_fm[:, hh, :], r=k_fm[:, hh, sb2, 0:16]:
                     e.matmul(o, l, r, start=True, stop=True),
                     reads=[(kQF, 0), (kKF, sb2)], writes=[ksp])

            def emit_soft(hh):
                par = (jt * 8 + hh) % 2
                par3 = (jt * 8 + hh) % 3
                sp = C.psA[0 + 2 * par]
                ksp = ("psA", 0 + 2 * par)
                sc = sp[:, 144:288]
                ksc = ksp
                P.op("vector", lambda e, o=sc[:, 0:144], i=sp[:, 0:144],
                     a=alibi[:, hh, :]:
                     e.scalar_tensor_tensor(o, i, 0.08838834764831845, a,
                                            ALU.mult, ALU.add),
                     reads=[ksp, (kAL,)], writes=[ksc])
                if jt == 0:
                    P.op("vector", lambda e, o=sc[:, 0:48], i=sp[:, 0:48],
                         a=alibi0[:, hh, :]:
                         e.scalar_tensor_tensor(o, i, 0.08838834764831845, a,
                                                ALU.mult, ALU.add),
                         reads=[ksp, (kAL0,)], writes=[ksc])
                nm2 = st5[:, par, 6:7]
                P.op("vector", lambda e, o=nm2, i=sc[:, 0:144]:
                     e.reduce_max(o, i, axis=AX.X, negate=True),
                     reads=[ksc], writes=[(kS5, par, "nm")])
                den = st5[:, par, 7:8]
                P.op("scalar", lambda e, o=pbn[:, par3, :], i=sc[:, 0:144], bb=nm2, a=den:
                     e.activation(o, i, AF.Exp, bias=bb, accum_out=a),
                     reads=[ksc, (kS5, par, "nm")],
                     writes=[(kPN, par3), (kS5, par, "den")])
                rde = st5[:, par, 5:6]
                P.op("vector", lambda e, o=rde, i=den: e.reciprocal(o, i),
                     reads=[(kS5, par, "den")], writes=[(kS5, par, "rde")])
                P.op("scalar", lambda e, o=pbn[:, par3, :], i=pbn[:, par3, :], s=rde:
                     e.activation(o, i, AF.Identity, scale=s),
                     reads=[(kPN, par3), (kS5, par, "rde")], writes=[(kPN, par3)])

            def emit_pv(hh):
                par = (jt * 8 + hh) % 2
                par3 = (jt * 8 + hh) % 3
                pt = C.psT[par][:, 0:128]
                kpt = ("psA", 6 + par)
                P.op("tensor", lambda e, o=pt, i=pbn[:, par3, 0:128],
                     idn=C.ident[:]: e.transpose(o, i, idn),
                     reads=[(kPN, par3)], writes=[kpt])
                P.op("vector", lambda e, o=pta[:, par3, :], i=pt: e.tensor_copy(o, i),
                     reads=[kpt], writes=[(kTA, par3)])
                P.op("tensor", lambda e, o=pt[0:16, :], i=pbn[:, par3, 128:144],
                     idn=C.ident[:]: e.transpose(o, i, idn),
                     reads=[(kPN, par3)], writes=[kpt])
                P.op("vector", lambda e, o=ptb[0:16, par3, :], i=pt[0:16, :]:
                     e.tensor_copy(o, i), reads=[kpt], writes=[(kTB, par3)])
                op_ = C.psA[1 + 2 * par]
                kop = ("psA", 1 + 2 * par)
                P.op("tensor", lambda e, o=op_[:, 0:128],
                     l=v_tm[:, sa, hh * 128:(hh + 1) * 128], r=pta[:, par3, :]:
                     e.matmul(o, l, r, start=True, stop=False),
                     reads=[(kVT, sa), (kTA, par3)], writes=[kop])
                P.op("tensor", lambda e, o=op_[:, 0:128],
                     l=v_tm[0:16, sb2, hh * 128:(hh + 1) * 128],
                     r=ptb[0:16, par3, :]:
                     e.matmul(o, l, r, start=False, stop=True),
                     reads=[(kVT, sb2), (kTB, par3)], writes=[kop])
                P.op("scalar", lambda e, o=at_fm[:, hh, :],
                     i=op_[:, 0:128]: e.activation(o, i, AF.Identity),
                     reads=[kop], writes=[(kAT, hh)])

            emit_qk(0)
            for hh in range(8):
                if hh + 1 < 8:
                    emit_qk(hh + 1)
                emit_soft(hh)
                emit_pv(hh)
            for half in range(2):
                pp = C.psA[4 + half]
                kpp = ("psA", 4 + half)
                for c in range(NCH):
                    P.op("tensor", lambda e, o=pp[:],
                         l=at_fm[:, c, :],
                         r=wot[:, c, half * 512:(half + 1) * 512],
                         s=(c == 0), t=(c == NCH - 1):
                         e.matmul(o, l, r, start=s, stop=t),
                         reads=[(kwo,)] + [(kAT, c2) for c2 in range(NCH)],
                         writes=[kpp])
                xsl = C.x_tm[:, b, jt, half * 512:(half + 1) * 512]
                kx = ("x", b, jt)
                P.op("vector", lambda e, o=xsl, i=xsl, p=pp[:]: e.tensor_add(o, i, p),
                     reads=[kx, kpp], writes=[kx])



def declare_io(nc, names_shapes, kind):
    return {n: nc.dram_tensor(n, s, F32, kind=kind).ap() for n, s in names_shapes}


SUB_WEIGHTS = {
    "ffn0": [("wrT", [D, D], F32), ("wkT", [D, D], F32), ("wvT", [D, D], F32),
             ("mu", [128, 16], F32)],
    "attn0": [("wqkvT", [D, 3072], F32), ("wprojT", [D, D], F32),
              ("bqkv", [128, 24], F32), ("bproj", [1, D], F32),
              ("msc", [128, NCH, 16], BF16), ("ebc", [16, NCH, 128], BF16)],
    "moe": [("gw", [128, NCH, 8], F32), ("w1s", [WROWS, 4104], BF16),
            ("w2s", [WROWS, 4096], BF16),
            ("b2", [8, D], BF16), ("l2", [128, 128], F32), ("l18", [18, 18], F32),
            ("if32", [128, 128], F32), ("thr", [128, 11], F32),
            ("iotag", [128, NG], F32), ("iotap", [128, NG], F32),
            ("rconst", [128, B, NT, 3], F32)],
    "attn1": [("wq", [D, D], F32), ("wk", [D, D], F32), ("wv", [D, D], F32),
              ("wo", [D, D], F32), ("qkn", [128, 16], F32),
              ("alibi", [128, 8, 144], BF16), ("alibi0", [128, 8, 48], BF16)],
}


def build_program(subs):
    nc = bass.Bass("TRN2", target_bir_lowering=False)
    x_in = nc.dram_tensor("x_in", [B, NTOK, 1024], F32, kind="ExternalInput").ap()
    x_out = nc.dram_tensor("x_out", [B, NTOK, 1024], F32, kind="ExternalOutput").ap()
    ident_in = nc.dram_tensor("ident", [128, 128], BF16, kind="ExternalInput").ap()
    win = {}
    for s in subs:
        for n, shp, dt in SUB_WEIGHTS[s]:
            key = f"{s}_{n}"
            if key not in win:
                win[key] = nc.dram_tensor(key, shp, dt, kind="ExternalInput").ap()
    with ExitStack() as ctx:
        C = Ctx(nc, ctx)
        P = C.P
        P.dma("sync", C.x_tm[:], x_in.rearrange("b (t p) d -> p b t d", p=128),
              writes=[("x", b, j) for b in range(B) for j in range(NT)])
        P.dma("sync", C.ident[:], ident_in, writes=[("ident",)])
        P.op("vector", lambda e: e.memset(C.xf[:, :, :, 0:PAD], 0.0),
             writes=[("xfpad",)])
        P.op("vector", lambda e: e.memset(C.xf[:, :, :, PAD + NTOK - 1:PAD + NTOK], 0.0),
             writes=[("xftail",)])
        P.op("vector", lambda e: e.memset(C.epsv[:, 0:1], LN_EPS), writes=[("epsv",)])
        P.op("vector", lambda e: e.memset(C.epsv[:, 1:2], RMS_EPS), writes=[("epsv",)])
        P.op("vector", lambda e: e.memset(C.ones[:], 1.0), writes=[("ones",)])
        for s in subs:
            if s == "ffn0":
                emit_s1(C, win["ffn0_wrT"], win["ffn0_wkT"], win["ffn0_wvT"],
                        win["ffn0_mu"])
            elif s == "attn0":
                emit_s2(C, win["attn0_wqkvT"], win["attn0_wprojT"],
                        win["attn0_bqkv"], win["attn0_bproj"],
                        win["attn0_msc"], win["attn0_ebc"])
            elif s == "moe":
                emit_moe_sparse(C, win["moe_gw"], win["moe_w1s"], win["moe_w2s"],
                                win["moe_b2"], win["moe_l2"], win["moe_l18"],
                                win["moe_if32"], win["moe_rconst"],
                                win["moe_thr"], win["moe_iotag"], win["moe_iotap"])
            elif s == "attn1":
                emit_s5(C, win["attn1_wq"], win["attn1_wk"], win["attn1_wv"],
                        win["attn1_wo"], win["attn1_qkn"], win["attn1_alibi"],
                        win["attn1_alibi0"])
            else:
                raise NotImplementedError(s)
        P.dma("sync", x_out.rearrange("b (t p) d -> p b t d", p=128), C.x_tm[:],
              reads=[("x", b, j) for b in range(B) for j in range(NT)],
              is_output=True)
        P.emit(nc, ctx)
    return nc



def shard_x(x_full):
    xp = np.zeros((B, HALO + NCORE * TOWN + (NTOK - TLOC), D), np.float32)
    xp[:, HALO:HALO + NCORE * TOWN] = x_full
    shards = []
    for c in range(NCORE):
        sl = np.zeros((B, NTOK, D), np.float32)
        sl[:, :TLOC] = xp[:, c * TOWN: c * TOWN + TLOC]
        shards.append(sl)
    return shards


def unshard_x(results, key="x_out"):
    return np.concatenate([r[key][:, HALO:HALO + TOWN] for r in results], axis=1)


_IDENT = np.eye(128, dtype=BF)


def run_subblocks(subs, x_full, weight_map):
    nc = build_program(subs)
    shards = shard_x(x_full)
    in_maps = []
    for c in range(NCORE):
        m = {"x_in": shards[c], "ident": _IDENT}
        for k, v in weight_map.items():
            m[k] = v[c] if isinstance(v, list) else v
        in_maps.append(m)
    res = run_bass_kernel_spmd(nc, in_maps, core_ids=list(range(NCORE)))
    return unshard_x(res.results)


def attn0_weights(Wqkv, bqkv, Wproj, bproj):
    H16 = 16
    msc = np.zeros((128, NCH, 16), dtype=BF)
    ebc = np.zeros((16, NCH, 128), dtype=BF)
    for c in range(NCH):
        for p in range(128):
            h = 2 * c + (1 if p >= 64 else 0)
            msc[p, c, h] = 1
            ebc[h, c, p] = 1
    return {
        "attn0_wqkvT": np.ascontiguousarray(np.asarray(Wqkv, np.float32).T),
        "attn0_wprojT": np.ascontiguousarray(np.asarray(Wproj, np.float32).T),
        "attn0_bqkv": np.ascontiguousarray(np.asarray(bqkv, np.float32).reshape(24, 128).T),
        "attn0_bproj": np.asarray(bproj, np.float32).reshape(1, D),
        "attn0_msc": msc,
        "attn0_ebc": ebc,
    }


def moe_weights(gate_w, W1, b1, W2, b2):
    gate_w = np.asarray(gate_w, np.float32)
    gw = np.ascontiguousarray(gate_w.T.reshape(8, 128, 8).transpose(1, 0, 2))
    W1 = np.asarray(W1, np.float32)
    W2 = np.asarray(W2, np.float32)
    b1 = np.asarray(b1, np.float32)
    a = W1.reshape(8, 8, 128, 8, 512).transpose(0, 3, 2, 1, 4).reshape(8, 8, 128, 4096)
    w1s = np.zeros((8, 8, 128, 4104), dtype=BF)
    w1s[..., :4096] = a.astype(BF)
    b1r = np.ascontiguousarray(b1.reshape(8, 8, 4, 128).transpose(0, 1, 3, 2)).astype(np.float32)
    w1s[..., 4096:4104] = b1r.view(BF).reshape(8, 8, 128, 8)
    w1s = np.ascontiguousarray(w1s.reshape(WROWS, 4104))
    w2s = np.ascontiguousarray(
        W2.reshape(8, 8, 4, 128, 1024).transpose(0, 1, 3, 2, 4)
        .reshape(WROWS, 4096)).astype(BF)
    l2 = np.zeros((128, 128), np.float32)
    for ss in range(128):
        l2[ss, ss + 1:] = 2.0
    l18 = np.zeros((18, 18), np.float32)
    for ii in range(18):
        l18[ii, ii + 1:] = 1.0
    if32 = np.eye(128, dtype=np.float32)
    thr = np.zeros((128, 11), np.float32)
    for k in range(11):
        thr[:, k] = GRP * (k + 1) - (GRP - 1)
    iotag = np.tile(np.arange(NG, dtype=np.float32)[None, :], (128, 1))
    iotap = np.tile(np.arange(128, dtype=np.float32)[:, None], (1, NG))
    rconst = np.zeros((128, B, NT, 3), np.float32)
    for b_ in range(B):
        for jt in range(NT):
            for p in range(128):
                t = jt * 128 + p
                rconst[p, b_, jt, 0] = 0.5 if t < TLOC else 0.0
                rconst[p, b_, jt, 1] = 0.0 if t < TLOC else 1.0e9
                rconst[p, b_, jt, 2] = b_ * NTOK + t
    return {"moe_gw": gw, "moe_w1s": w1s, "moe_w2s": w2s,
            "moe_b2": np.ascontiguousarray(np.asarray(b2, np.float32).reshape(8, D)).astype(BF),
            "moe_l2": l2, "moe_l18": l18, "moe_if32": if32, "moe_thr": thr,
            "moe_iotag": iotag, "moe_iotap": iotap,
            "moe_rconst": rconst}


def alibi_tiles(core):
    A = np.full((128, 8, 2, 144), -1e9, np.float32)
    for h in range(8):
        sl = 0.5 ** h
        for i in range(128):
            for jj in range(144):
                rel = jj - 16 - i
                if -16 <= rel <= 0:
                    A[i, h, 1, jj] = sl * rel
                    if core != 0 or jj - 16 >= HALO:
                        A[i, h, 0, jj] = sl * rel
    return A


def alibi_split(core):
    A = alibi_tiles(core)
    gen = np.ascontiguousarray(A[:, :, 1, :]).astype(BF)
    patch = np.ascontiguousarray(A[:, :, 0, 0:48]).astype(BF)
    return gen, patch


def attn1_weights(wq, wk, wv, wo, qn, kn):
    qkn = np.concatenate([np.asarray(qn, np.float32).reshape(8, 128).T,
                          np.asarray(kn, np.float32).reshape(8, 128).T], axis=1)
    return {"attn1_wq": np.asarray(wq, np.float32),
            "attn1_wk": np.asarray(wk, np.float32),
            "attn1_wv": np.asarray(wv, np.float32),
            "attn1_wo": np.asarray(wo, np.float32),
            "attn1_qkn": np.ascontiguousarray(qkn),
            "attn1_alibi": [alibi_split(c)[0] for c in range(NCORE)],
            "attn1_alibi0": [alibi_split(c)[1] for c in range(NCORE)]}


def ffn0_weights(W_r, W_k, W_v, mu_r, mu_k):
    mu = np.concatenate([np.asarray(mu_r, np.float32).reshape(8, 128).T,
                         np.asarray(mu_k, np.float32).reshape(8, 128).T], axis=1)
    return {
        "ffn0_wrT": np.ascontiguousarray(np.asarray(W_r, np.float32).T),
        "ffn0_wkT": np.ascontiguousarray(np.asarray(W_k, np.float32).T),
        "ffn0_wvT": np.ascontiguousarray(np.asarray(W_v, np.float32).T),
        "ffn0_mu": np.ascontiguousarray(mu),
    }


def kernel(**inputs):
    x = np.asarray(inputs["L0_attn_x"], np.float32)
    wm_ffn0 = ffn0_weights(inputs["L0_ffn_W_r"], inputs["L0_ffn_W_k"],
                           inputs["L0_ffn_W_v"], inputs["L0_ffn_mu_r"],
                           inputs["L0_ffn_mu_k"])
    wm_attn0 = attn0_weights(inputs["L0_attn_Wqkv"], inputs["L0_attn_bqkv"],
                             inputs["L0_attn_Wproj"], inputs["L0_attn_bproj"])
    wm_moe = moe_weights(inputs["L1_ffn_gate_w"], inputs["L1_ffn_W1"],
                         inputs["L1_ffn_b1"], inputs["L1_ffn_W2"],
                         inputs["L1_ffn_b2"])
    wm_attn1 = attn1_weights(inputs["L1_attn_wq"], inputs["L1_attn_wk"],
                             inputs["L1_attn_wv"], inputs["L1_attn_wo"],
                             inputs["L1_attn_q_norm_w"], inputs["L1_attn_k_norm_w"])
    wm = {}
    for m in (wm_ffn0, wm_attn0, wm_moe, wm_attn1):
        wm.update(m)
    x = run_subblocks(["ffn0", "attn0", "ffn0", "moe", "attn1", "moe"], x, wm)
    return x


# version 24 (final)
# speedup vs baseline: 1.2351x; 1.1367x over previous
import numpy as np
import ml_dtypes
import concourse.bass as bass
import concourse.mybir as mybir
from concourse.bass_utils import run_bass_kernel_spmd
from contextlib import ExitStack

F32 = mybir.dt.float32
BF16 = mybir.dt.bfloat16
AF = mybir.ActivationFunctionType
ALU = mybir.AluOpType
AX = mybir.AxisListType
BF = ml_dtypes.bfloat16
MM2 = mybir.MatmulPerfMode.DoubleRow

NCORE = 8
B = 2
TOWN = 1024
HALO = 18
TLOC = TOWN + HALO
NT = 9
NTOK = NT * 128
PAD = 16
D = 1024
NCH = 8
LN_EPS = 1e-5
RMS_EPS = 1e-6
GRPS = [(0, 512), (512, 1024), (1024, 1152)]
SPARSE_MOE = True


STREAMS = ("tensor", "vector", "scalar", "gpsimd", "sync")


class _Op:
    __slots__ = ("stream", "fn", "idx", "is_dma", "needs_inc", "count",
                 "waits", "snap")

    def __init__(self, stream, fn, idx, is_dma):
        self.stream = stream
        self.fn = fn
        self.idx = idx
        self.is_dma = is_dma
        self.needs_inc = False
        self.count = None
        self.waits = []
        self.snap = None


class Prog:

    def __init__(self):
        self.ops = []
        self.last_w = {}
        self.readers = {}
        self.clock = {s: {t: -1 for t in STREAMS} for s in STREAMS}
        self.out_dmas = []
        self.arena_knames = set()
        self.epoch_fence = []

    def op(self, stream, fn, reads=(), writes=(), is_dma=False, is_output=False):
        o = _Op(stream, fn, len(self.ops), is_dma)
        if is_dma:
            o.needs_inc = True
        deps = []
        for k in reads:
            w = self.last_w.get(k)
            if w is not None:
                deps.append(w)
        for k in writes:
            w = self.last_w.get(k)
            if w is not None:
                deps.append(w)
            deps.extend(self.readers.get(k, ()))
            if self.arena_knames and isinstance(k, tuple) and k and k[0] in self.arena_knames:
                deps.extend(self.epoch_fence)
        ck = self.clock[stream]
        need = {}
        for d in deps:
            if d.stream == "tensor" and stream == "tensor":
                continue
            if ck[d.stream] >= d.idx:
                continue
            cur = need.get(d.stream)
            if cur is None or d.idx > cur.idx:
                need[d.stream] = d
        for d in need.values():
            d.needs_inc = True
            if d.snap is None:
                d.snap = dict(self.clock[d.stream])
                d.snap[d.stream] = d.idx
            o.waits.append(d)
            for s2, v in d.snap.items():
                if v > ck[s2]:
                    ck[s2] = v
        for k in writes:
            self.last_w[k] = o
            self.readers[k] = []
        for k in reads:
            self.readers.setdefault(k, []).append(o)
        self.ops.append(o)
        if is_output:
            o.needs_inc = True
            self.out_dmas.append(o)
        return o

    def dma(self, stream, out_ap, in_ap, reads=(), writes=(), is_output=False, **kw):
        return self.op(stream,
                       (lambda e, o=out_ap, i=in_ap, kw=kw: e.dma_start(out=o, in_=i, **kw)),
                       reads=reads, writes=writes, is_dma=True, is_output=is_output)

    def emit(self, nc, ctx):
        counts = {s: 0 for s in STREAMS}
        for o in self.ops:
            if o.needs_inc:
                counts[o.stream] += 16 if o.is_dma else 1
                o.count = counts[o.stream]
        sems = {s: ctx.enter_context(nc.semaphore("sem_" + s)) for s in STREAMS}
        per = {s: [] for s in STREAMS}
        for o in self.ops:
            per[o.stream].append(o)
        block = ctx.enter_context(nc.Block())

        def run(eng, stream):
            hw = {}
            for o in per[stream]:
                for d in o.waits:
                    if hw.get(d.stream, -1) < d.count:
                        eng.wait_ge(sems[d.stream], d.count)
                        hw[d.stream] = d.count
                inst = o.fn(eng)
                if o.needs_inc:
                    inst.then_inc(sems[o.stream], 16 if o.is_dma else 1)
            fin = [o for o in self.out_dmas if o.stream == stream]
            if fin:
                eng.wait_ge(sems[stream], max(o.count for o in fin))

        @block.tensor
        def _(e):
            run(e, "tensor")

        @block.vector
        def _(e):
            run(e, "vector")

        @block.scalar
        def _(e):
            run(e, "scalar")

        @block.gpsimd
        def _(e):
            run(e, "gpsimd")

        @block.sync
        def _(e):
            run(e, "sync")



class Ctx:

    def __init__(self, nc, ctx):
        self.nc = nc
        self.x = ctx
        self.P = Prog()
        self.uid = 0
        sb = self.sb = lambda name, shape, dt=F32: ctx.enter_context(
            nc.sbuf_tensor(name, shape, dt))
        ps = lambda name, shape, dt=F32: ctx.enter_context(
            nc.psum_tensor(name, shape, dt))
        self.x_tm = sb("x_tm", [128, B, NT, 1024])
        self.xf = sb("xf", [128, NCH, B, PAD + NTOK], BF16)
        self.ident = sb("ident_sb", [128, 128], BF16)
        self.ones = sb("ones_bf", [1, 128], BF16)
        self.bn12 = sb("bn12", [128, 2, 12])
        self.mv = sb("mv", [128, 2, 2])
        self.nm = sb("nm", [128, 2, 1])
        self.sd = sb("sd", [128, 2, 1])
        self.rs = sb("rs", [128, 2, 1])
        self.xf_tm = sb("xf_tm", [128, 2, 1024], BF16)
        self.moe_gcb = sb("moe_gcb", [128, 2, 8], BF16)
        self.smalls = {}
        self.epsv = sb("epsv", [128, 2])
        self.psA = [ps(f"psA{i}", [128, 512]) for i in range(8)]
        self.psT = [self.psA[6].bitcast(BF16), self.psA[7].bitcast(BF16)]
        self.ARENA_B = 90368
        self.arena = sb("arena", [128, self.ARENA_B], mybir.dt.uint8)
        self.arena_off = 0

    def name(self, p):
        self.uid += 1
        return f"{p}{self.uid}"

    def carve(self, kname, shape, dt=BF16):
        assert shape[0] == 128
        esz = {F32: 4, BF16: 2, mybir.dt.uint32: 4}[dt]
        n = 1
        for s in shape[1:]:
            n *= s
        off = (self.arena_off + 31) & ~31
        self.arena_off = off + n * esz
        assert self.arena_off <= self.ARENA_B, \
            f"arena overflow: {self.arena_off} > {self.ARENA_B} at {kname}"
        flat = self.arena.bitcast(dt)
        ap = flat[:, off // esz: off // esz + n]
        if len(shape) > 2:
            dims = " ".join(f"d{i}" for i in range(1, len(shape)))
            kw = {f"d{i}": shape[i] for i in range(1, len(shape))}
            ap = ap.rearrange(f"p ({dims}) -> p {dims}", **kw)
        self.P.arena_knames.add(kname)
        return ap, kname

    def arena_reset(self):
        P = self.P
        fence = []
        for k, op in P.last_w.items():
            if isinstance(k, tuple) and k and k[0] in P.arena_knames:
                fence.append(op)
                fence.extend(P.readers.get(k, ()))
        best = {}
        for o in fence:
            if best.get(o.stream) is None or o.idx > best[o.stream].idx:
                best[o.stream] = o
        P.epoch_fence = list(best.values())
        P.arena_knames = set()
        self.arena_off = 0


def load_const(C, key, ap_in, shape, dt=F32, stream="sync"):
    t = C.sb(C.name("c_"), shape, dt)
    C.P.dma(stream, t[:], ap_in, writes=[(id(t),)])
    return t


def emit_ln(C, b, j, eps, par, dram_out=None):
    P, nc = C.P, C.nc
    x = C.x_tm[:, b, j, :]
    xk = ("x", b, j)
    bn = C.bn12[:, par, :]
    mv = C.mv[:, par, :]
    nm = C.nm[:, par, :]
    sd = C.sd[:, par, :]
    rs = C.rs[:, par, :]
    xt = C.xf_tm[:, par, :]
    kbn, kmv, knm, ksd, krs, kxt = [("ln", par, i) for i in range(6)]
    P.op("vector", lambda e, o=bn[:, 0:6], i=x[:, 0:512]: e.bn_stats(o, i),
         reads=[xk], writes=[kbn])
    P.op("vector", lambda e, o=bn[:, 6:12], i=x[:, 512:1024]: e.bn_stats(o, i),
         reads=[xk], writes=[kbn])
    P.op("vector", lambda e, o=mv, i=bn: e.bn_aggr(o, i), reads=[kbn], writes=[kmv])
    P.op("vector", lambda e, o=nm, i=mv[:, 0:1]: e.tensor_scalar_mul(o, i, -1.0),
         reads=[kmv], writes=[knm])
    ecol = 0 if eps == LN_EPS else 1
    P.op("scalar", lambda e, o=sd, i=mv[:, 1:2], bb=C.epsv[:, ecol:ecol + 1]:
         e.activation(o, i, AF.Sqrt, bias=bb),
         reads=[kmv, ("epsv",)], writes=[ksd])
    P.op("vector", lambda e, o=rs, i=sd: e.reciprocal(o, i), reads=[ksd], writes=[krs])
    P.op("vector", lambda e, o=xt, i=x, a=nm, s=rs:
         e.tensor_scalar(o, i, a, s, ALU.add, ALU.mult),
         reads=[xk, knm, krs], writes=[kxt])
    if dram_out is not None:
        P.dma("sync", dram_out[(b * NTOK + j * 128):(b * NTOK + (j + 1) * 128), :],
              xt, reads=[kxt], writes=[("xfd",)])
    for c in range(NCH):
        pt = C.psT[c % 2][:, 0:128]
        kpt = ("psA", 6 + c % 2)
        P.op("tensor", lambda e, o=pt, i=xt[:, c * 128:(c + 1) * 128], idn=C.ident[:]:
             e.transpose(o, i, idn), reads=[kxt], writes=[kpt])
        dst = C.xf[:, c, b, PAD + j * 128: PAD + (j + 1) * 128]
        eng = "vector" if c % 2 == 0 else "scalar"
        if eng == "vector":
            P.op(eng, lambda e, o=dst, i=pt: e.tensor_copy(o, i),
                 reads=[kpt], writes=[("xf", c, b, j)])
        else:
            P.op(eng, lambda e, o=dst, i=pt: e.activation(o, i, AF.Copy),
                 reads=[kpt], writes=[("xf", c, b, j)])


def emit_ln_all(C, eps, dram_out=None):
    for b in range(B):
        for j in range(NT):
            emit_ln(C, b, j, eps, (b * NT + j) % 2, dram_out)


def xf_keys(C, b, t0, t1, shift=0):
    lo = max(0, (t0 + shift) // 128)
    hi = min(NT - 1, (t1 - 1 + shift) // 128)
    return [("xf", c, b, j) for c in range(NCH) for j in range(lo, hi + 1)]



def emit_s1(C, wr_ap, wk_ap, wv_ap, mu_ap):
    P, nc = C.P, C.nc
    C.arena_reset()
    w_r, kwr = C.carve(C.name("w_r"), [128, NCH, 1024], BF16)
    w_k, kwk = C.carve(C.name("w_k"), [128, NCH, 1024], BF16)
    w_v, kwv = C.carve(C.name("w_v"), [128, NCH, 1024], BF16)
    mu, kmu = C.carve(C.name("mu"), [128, 16], F32)
    for t, ap, key in ((w_r, wr_ap, kwr), (w_k, wk_ap, kwk), (w_v, wv_ap, kwv)):
        P.dma("gpsimd", t[:], ap.rearrange("(c p) m -> p c m", p=128), writes=[(key,)])
    P.dma("sync", mu[:], mu_ap, writes=[(kmu,)])
    P.op("vector", lambda e, o=w_v[:], i=w_v[:]: e.tensor_scalar_mul(o, i, 0.5),
         reads=[(kwv,)], writes=[(kwv,)])

    emit_ln_all(C, LN_EPS)

    xr, kxr = C.carve(C.name("xr"), [128, 2, NCH, 256], BF16)
    xk, kxk = C.carve(C.name("xk"), [128, 2, NCH, 256], BF16)
    dbuf, kd = C.carve(C.name("d"), [128, 2, 256], BF16)
    kk, kkk = C.carve(C.name("kk"), [128, 2, NCH, 256], BF16)
    krelu, kkr = C.carve(C.name("krelu"), [128, 2, 256], BF16)
    r_tm, krt = C.carve(C.name("r_tm"), [128, 2, 1024], F32)
    s1tmp, kst = C.carve(C.name("s1tmp"), [128, 2, 512], F32)

    GR1 = [(g * 256, min(g * 256 + 256, NTOK)) for g in range(5)]
    groups = [(b, t0, t1) for b in range(B) for (t0, t1) in GR1]

    def s1_mix(gidx):
        b, t0, t1 = groups[gidx]
        TT = t1 - t0
        gp = gidx % 2
        rdeps = xf_keys(C, b, t0, t1) + xf_keys(C, b, t0 - 1, t1 - 1) + [("xfpad",)]
        for c in range(NCH):
            cur = C.xf[:, c, b, PAD + t0: PAD + t1]
            prv = C.xf[:, c, b, PAD + t0 - 1: PAD + t1 - 1]
            P.op("vector", lambda e, o=dbuf[:, gp, 0:TT], i=cur, p=prv: e.tensor_sub(o, i, p),
                 reads=rdeps, writes=[(kd, gp)])
            P.op("vector", lambda e, o=xr[:, gp, c, 0:TT], i=dbuf[:, gp, 0:TT],
                 s=mu[:, c:c + 1], p=prv:
                 e.scalar_tensor_tensor(o, i, s, p, ALU.mult, ALU.add),
                 reads=[(kd, gp), (kmu,)] + rdeps, writes=[(kxr, gp, c)])
            P.op("vector", lambda e, o=xk[:, gp, c, 0:TT], i=dbuf[:, gp, 0:TT],
                 s=mu[:, 8 + c:9 + c], p=prv:
                 e.scalar_tensor_tensor(o, i, s, p, ALU.mult, ALU.add),
                 reads=[(kd, gp), (kmu,)] + rdeps, writes=[(kxk, gp, c)])

    def s1_comp(gidx):
        b, t0, t1 = groups[gidx]
        TT = t1 - t0
        gp = gidx % 2
        for co in range(NCH):
                kp = C.psA[co % 2]
                kkp = ("psA", co % 2)
                for ci in range(NCH):
                    P.op("tensor", lambda e, o=kp[:, 0:TT],
                         l=w_k[:, ci, co * 128:(co + 1) * 128],
                         r=xk[:, gp, ci, 0:TT], s=(ci == 0), t=(ci == NCH - 1):
                         e.matmul(o, l, r, start=s, stop=t),
                         reads=[(kwk,)] + [(kxk, gp, ci) for ci in range(NCH)], writes=[kkp])
                P.op("scalar", lambda e, o=krelu[:, gp, 0:TT], i=kp[:, 0:TT]:
                     e.activation(o, i, AF.Relu), reads=[kkp], writes=[(kkr, gp)])
                P.op("vector", lambda e, o=kk[:, gp, co, 0:TT], i=krelu[:, gp, 0:TT]:
                     e.tensor_mul(o, i, i), reads=[(kkr, gp)], writes=[(kkk, gp, co)])
        for jt in range(t0 // 128, t1 // 128):
            toff = jt * 128 - t0
            par = jt % 2
            rp = [C.psA[2], C.psA[3]]
            vp = [C.psA[4], C.psA[5]]
            for half in range(2):
                krp = ("psA", 2 + half)
                for ci in range(NCH):
                    P.op("tensor", lambda e, o=rp[half][:],
                         l=xr[:, gp, ci, toff:toff + 128],
                         r=w_r[:, ci, half * 512:(half + 1) * 512],
                         s=(ci == 0), t=(ci == NCH - 1):
                         e.matmul(o, l, r, start=s, stop=t),
                         reads=[(kwr,)] + [(kxr, gp, ci) for ci in range(NCH)], writes=[krp])
                P.op("scalar", lambda e, o=r_tm[:, par, half * 512:(half + 1) * 512],
                     i=rp[half][:]: e.activation(o, i, AF.Sigmoid),
                     reads=[krp], writes=[(krt, par)])
            for half in range(2):
                kvp = ("psA", 4 + half)
                for ck in range(NCH):
                    P.op("tensor", lambda e, o=vp[half][:],
                         l=kk[:, gp, ck, toff:toff + 128],
                         r=w_v[:, ck, half * 512:(half + 1) * 512],
                         s=(ck == 0), t=(ck == NCH - 1):
                         e.matmul(o, l, r, start=s, stop=t),
                         reads=[(kwv,)] + [(kkk, gp, ck2) for ck2 in range(NCH)], writes=[kvp])
                xsl = C.x_tm[:, b, jt, half * 512:(half + 1) * 512]
                kx = ("x", b, jt)
                P.op("vector", lambda e, o=s1tmp[:, half, :], i=vp[half][:],
                     r=r_tm[:, par, half * 512:(half + 1) * 512]: e.tensor_mul(o, i, r),
                     reads=[kvp, (krt, par)], writes=[(kst, half)])
                P.op("vector", lambda e, o=xsl, i=xsl, t2=s1tmp[:, half, :]:
                     e.tensor_add(o, i, t2),
                     reads=[kx, (kst, half)], writes=[kx])

    for gidx in range(len(groups)):
        s1_mix(gidx)
        if gidx > 0:
            s1_comp(gidx - 1)
    s1_comp(len(groups) - 1)



def emit_s2(C, wqkvT_ap, wprojT_ap, bqkv_ap, bproj_ap, msc_ap, ebc_ap):
    P, nc = C.P, C.nc
    C.arena_reset()
    wqkv, kW = C.carve(C.name("wqkv"), [128, NCH, 3072], BF16)
    wproj, kP2 = C.carve(C.name("wproj"), [128, NCH, 1024], BF16)
    bqkv, kB = C.carve(C.name("bqkv"), [128, 24], F32)
    msc, kM = C.carve(C.name("msc"), [128, NCH, 16], BF16)
    P.dma("gpsimd", wqkv[:], wqkvT_ap.rearrange("(c p) m -> p c m", p=128), writes=[(kW,)])
    P.dma("gpsimd", wproj[:], wprojT_ap.rearrange("(c p) m -> p c m", p=128), writes=[(kP2,)])
    P.dma("sync", bqkv[:], bqkv_ap, writes=[(kB,)])
    P.dma("sync", msc[:], msc_ap, writes=[(kM,)])
    bproj = C.sb(C.name("bproj_"), [1, 1024], BF16)
    ebc = C.sb(C.name("ebc_"), [16, NCH, 128], BF16)
    kBP, kE = (id(bproj),), (id(ebc),)
    P.dma("gpsimd", bproj[:], bproj_ap, writes=[kBP])
    P.dma("sync", ebc[:], ebc_ap, writes=[kE])

    emit_ln_all(C, LN_EPS)

    qkv, kQ = C.carve(C.name("qkv"), [128, 24, B, 128], BF16)
    ssb, kS = C.carve(C.name("ssb"), [128, 2 * 128], F32)
    dsb, kD2 = C.carve(C.name("dsb"), [128, 2 * 128], F32)
    asb, kA = C.carve(C.name("asb"), [128, 2 * 128], BF16)
    dv, kV = C.carve(C.name("dv"), [128, NCH, 128], BF16)
    pbuf, kPB = C.carve(C.name("pbuf"), [128, 4, 128], BF16)
    ofm, kO = C.carve(C.name("ofm"), [128, NCH, B, 128], BF16)
    otmp, kOT = C.carve(C.name("otmp"), [128, 128], BF16)
    TT = 128

    for jt in range(NT):
        t0 = jt * 128
        for b in range(B):
            for oc in range(24):
                pp = C.psA[oc % 2]
                kpp = ("psA", oc % 2)
                for ci in range(NCH):
                    P.op("tensor", lambda e, o=pp[:, 0:TT],
                         l=wqkv[:, ci, oc * 128:(oc + 1) * 128],
                         r=C.xf[:, ci, b, PAD + t0:PAD + t0 + TT],
                         s=(ci == 0), t=(ci == NCH - 1):
                         e.matmul(o, l, r, start=s, stop=t),
                         reads=[(kW,)] + [("xf", ci2, b, jt) for ci2 in range(NCH)],
                         writes=[kpp])
                P.op("scalar", lambda e, o=qkv[:, oc, b, :], i=pp[:, 0:TT],
                     bb=bqkv[:, oc:oc + 1]:
                     e.activation(o, i, AF.Identity, bias=bb),
                     reads=[kpp, (kB,)], writes=[(kQ, oc, b)])
        sps = {}
        for idx, (i, j) in enumerate([(0, 0), (0, 1), (1, 0), (1, 1)]):
            pp = C.psA[2 + idx]
            sps[(i, j)] = pp
            for c in range(NCH):
                pb = pbuf[:, c % 4, :]
                kpb = (kPB, c % 4)
                P.op("vector", lambda e, o=pb, a=qkv[:, c, i, :], b2=qkv[:, 8 + c, j, :]:
                     e.tensor_mul(o, a, b2),
                     reads=[(kQ, c, i), (kQ, 8 + c, j)], writes=[kpb])
                P.op("tensor", lambda e, o=pp[0:16, 0:TT], l=msc[:, c, :], r=pb,
                     s=(c == 0), t=(c == NCH - 1):
                     e.matmul(o, l, r, start=s, stop=t),
                     reads=[(kM,), kpb], writes=[("psA", 2 + idx)])
        for i in range(2):
            P.op("vector", lambda e, o=ssb[0:16, i * TT:(i + 1) * TT],
                 p=sps[(i, 1)][0:16, 0:TT]:
                 e.tensor_copy(o, p), reads=[("psA", 3 + 2 * i)], writes=[(kS, i)])
            P.op("vector", lambda e, o=dsb[0:16, i * TT:(i + 1) * TT],
                 p=sps[(i, 0)][0:16, 0:TT], s2=ssb[0:16, i * TT:(i + 1) * TT]:
                 e.tensor_sub(o, p, s2),
                 reads=[("psA", 2 + 2 * i), (kS, i)], writes=[(kD2, i)])
            P.op("scalar", lambda e, o=asb[0:16, i * TT:(i + 1) * TT],
                 i2=dsb[0:16, i * TT:(i + 1) * TT]:
                 e.activation(o, i2, AF.Sigmoid, scale=0.125),
                 reads=[(kD2, i)], writes=[(kA, i)])
        for c in range(NCH):
            P.op("vector", lambda e, o=dv[:, c, :], a=qkv[:, 16 + c, 0, :],
                 b2=qkv[:, 16 + c, 1, :]: e.tensor_sub(o, a, b2),
                 reads=[(kQ, 16 + c, 0), (kQ, 16 + c, 1)], writes=[(kV, c)])
        for i in range(2):
            for c in range(NCH):
                pp = C.psA[c % 2]
                kpp = ("psA", c % 2)
                P.op("tensor", lambda e, o=pp[:, 0:TT], l=ebc[:, c, :],
                     r=asb[0:16, i * TT:(i + 1) * TT]:
                     e.matmul(o, l, r, start=True, stop=True),
                     reads=[kE, (kA, i)], writes=[kpp])
                P.op("vector", lambda e, o=otmp[:], a=pp[:, 0:TT], b2=dv[:, c, :]:
                     e.tensor_mul(o, a, b2),
                     reads=[kpp, (kV, c)], writes=[(kOT,)])
                P.op("vector", lambda e, o=ofm[:, c, i, :], a=otmp[:],
                     b2=qkv[:, 16 + c, 1, :]: e.tensor_add(o, a, b2),
                     reads=[(kOT,), (kQ, 16 + c, 1)], writes=[(kO, c, i)])
        for b in range(B):
            for half in range(2):
                pp = C.psA[2 + 2 * (jt % 2) + half]
                kpp = ("psA", 2 + 2 * (jt % 2) + half)
                P.op("tensor", lambda e, o=pp[:], l=C.ones[0:1, :],
                     r=bproj[0:1, half * 512:(half + 1) * 512]:
                     e.matmul(o, l, r, start=True, stop=False),
                     reads=[("ones",), kBP], writes=[kpp])
                for c in range(NCH):
                    P.op("tensor", lambda e, o=pp[:], l=ofm[:, c, b, :],
                         r=wproj[:, c, half * 512:(half + 1) * 512],
                         t=(c == NCH - 1):
                         e.matmul(o, l, r, start=False, stop=t),
                         reads=[(kP2,)] + [(kO, c2, b) for c2 in range(NCH)],
                         writes=[kpp])
                xsl = C.x_tm[:, b, jt, half * 512:(half + 1) * 512]
                P.op("vector", lambda e, o=xsl, i=xsl, p=pp[:]: e.tensor_add(o, i, p),
                     reads=[("x", b, jt), kpp], writes=[("x", b, jt)])



def emit_moe(C, gw_ap, w1_ap, b1_ap, w2_ap, b2_ap):
    P, nc = C.P, C.nc
    C.arena_reset()
    gw, kG = C.carve(C.name("gw"), [128, NCH, 8], BF16)
    b1s, kB1 = C.carve(C.name("b1s"), [128, 8, 32], F32)
    P.dma("gpsimd", gw[:], gw_ap, writes=[(kG,)])
    P.dma("sync", b1s[:], b1_ap, writes=[(kB1,)])
    if "b2s" in C.smalls:
        b2s, gfm = C.smalls["b2s"], C.smalls["gfm"]
        kB2, kGF = (id(b2s),), (id(gfm),)
    else:
        b2s = C.smalls["b2s"] = C.sb(C.name("b2s_"), [8, 1024], BF16)
        gfm = C.smalls["gfm"] = C.sb(C.name("gfm_"), [8, 128], BF16)
        kB2, kGF = (id(b2s),), (id(gfm),)
        P.dma("gpsimd", b2s[:], b2_ap, writes=[kB2])

    emit_ln_all(C, LN_EPS)

    gates, kGT = C.carve(C.name("gates"), [128, B, NT, 8], F32)
    gsc, kGS = C.carve(C.name("gsc"), [128, 2, 32], F32)
    for b in range(B):
        for jt in range(NT):
            par = (b * NT + jt) % 2
            pp = C.psA[par]
            kpp = ("psA", par)
            for ci in range(NCH):
                P.op("tensor", lambda e, o=pp[:, 0:8], l=C.xf[:, ci, b, PAD + jt * 128:PAD + (jt + 1) * 128],
                     r=gw[:, ci, :], s=(ci == 0), t=(ci == NCH - 1):
                     e.matmul(o, l, r, start=s, stop=t),
                     reads=[(kG,)] + [("xf", c2, b, jt) for c2 in range(NCH)],
                     writes=[kpp])
            g = gsc[:, par, :]
            kg = (kGS, par)
            m1, m2, dif, wa, wb = (g[:, 0:1], g[:, 1:2], g[:, 2:3], g[:, 3:4], g[:, 4:5])
            eq1, eq2, msk, t1g = (g[:, 8:16], g[:, 16:24], g[:, 24:32], g[:, 5:6])
            P.op("vector", lambda e, o=m1, i=pp[:, 0:8]: e.reduce_max(o, i, axis=AX.X),
                 reads=[kpp], writes=[kg])
            P.op("vector", lambda e, o=eq1, i=pp[:, 0:8], s=m1:
                 e.tensor_scalar(o, i, s, 0.5, ALU.is_ge, ALU.mult),
                 reads=[kpp, kg], writes=[kg])
            P.op("vector", lambda e, o=msk, i=eq1, s=pp[:, 0:8]:
                 e.scalar_tensor_tensor(o, i, -1e9, s, ALU.mult, ALU.add),
                 reads=[kpp, kg], writes=[kg])
            P.op("vector", lambda e, o=m2, i=msk: e.reduce_max(o, i, axis=AX.X),
                 reads=[kg], writes=[kg])
            P.op("vector", lambda e, o=eq2, i=msk, s=m2:
                 e.tensor_scalar(o, i, s, 0.5, ALU.is_ge, ALU.mult),
                 reads=[kg], writes=[kg])
            P.op("vector", lambda e, o=dif, a=m1, b2_=m2: e.tensor_sub(o, a, b2_),
                 reads=[kg], writes=[kg])
            P.op("scalar", lambda e, o=wa, i=dif: e.activation(o, i, AF.Sigmoid),
                 reads=[kg], writes=[kg])
            P.op("scalar", lambda e, o=wb, i=dif: e.activation(o, i, AF.Sigmoid, scale=-1.0),
                 reads=[kg], writes=[kg])
            P.op("vector", lambda e, o=msk, i=eq1, s=wa:
                 e.tensor_scalar(o, i, s, None, ALU.mult),
                 reads=[kg], writes=[kg])
            P.op("vector", lambda e, o=gates[:, b, jt, :], i=eq2, s=wb, a=msk:
                 e.scalar_tensor_tensor(o, i, s, a, ALU.mult, ALU.add),
                 reads=[kg], writes=[(kGT, b, jt)])
            gcb, kgcb = (C.moe_gcb, ("moe_gcb", par))
            P.op("vector", lambda e, o=gcb[:, par, :], i=gates[:, b, jt, :]:
                 e.tensor_copy(o, i), reads=[(kGT, b, jt)], writes=[kgcb])
            pt = C.psT[par][:, 0:128]
            kpt = ("psA", 6 + par)
            P.op("tensor", lambda e, o=pt[0:8, :], i=gcb[:, par, :], idn=C.ident[:]:
                 e.transpose(o, i, idn), reads=[kgcb, ("ident",)], writes=[kpt])
            P.op("vector", lambda e, o=gfm[0:8, :], i=pt[0:8, :]: e.tensor_copy(o, i),
                 reads=[kpt], writes=[kGF])
            for half in range(2):
                bp = C.psA[2 + half]
                kbp = ("psA", 2 + half)
                P.op("tensor", lambda e, o=bp[:], l=gfm[0:8, :],
                     r=b2s[0:8, half * 512:(half + 1) * 512]:
                     e.matmul(o, l, r, start=True, stop=True),
                     reads=[kGF, kB2], writes=[kbp])
                xsl = C.x_tm[:, b, jt, half * 512:(half + 1) * 512]
                P.op("vector", lambda e, o=xsl, i=xsl, p=bp[:]: e.tensor_add(o, i, p),
                     reads=[("x", b, jt), kbp], writes=[("x", b, jt)])

    w1r = []
    w2r = []
    for par in range(2):
        t1, k1 = C.carve(C.name("w1r"), [128, NCH, 1024], BF16)
        t2, k2 = C.carve(C.name("w2r"), [128, NCH, 1024], BF16)
        w1r.append((t1, k1))
        w2r.append((t2, k2))
    h, kH = C.carve(C.name("h"), [128, NCH, 512], BF16)

    for e_ in range(8):
        for q in range(4):
            par = (e_ * 4 + q) % 2
            (w1t, kw1) = w1r[par]
            (w2t, kw2) = w2r[par]
            P.dma("gpsimd", w1t[:],
                  w1_ap[e_, :, q * 1024:(q + 1) * 1024].rearrange("(c p) m -> p c m", p=128),
                  writes=[(kw1,)])
            P.dma("gpsimd", w2t[:],
                  w2_ap[e_, q * 1024:(q + 1) * 1024, :].rearrange("(c p) m -> p c m", p=128),
                  writes=[(kw2,)])
            for b in range(B):
                for (t0, t1) in GRPS:
                    TT = t1 - t0
                    xfdeps = xf_keys(C, b, t0, t1)
                    for hc in range(NCH):
                        hp = C.psA[hc % 2]
                        khp = ("psA", hc % 2)
                        for ci in range(NCH):
                            P.op("tensor", lambda e, o=hp[:, 0:TT],
                                 l=w1t[:, ci, hc * 128:(hc + 1) * 128],
                                 r=C.xf[:, ci, b, PAD + t0:PAD + t1],
                                 s=(ci == 0), t=(ci == NCH - 1):
                                 e.matmul(o, l, r, start=s, stop=t),
                                 reads=[(kw1,)] + xfdeps, writes=[khp])
                        P.op("scalar", lambda e, o=h[:, hc, 0:TT], i=hp[:, 0:TT],
                             bb=b1s[:, e_, q * 8 + hc:q * 8 + hc + 1]:
                             e.activation(o, i, AF.Silu, bias=bb),
                             reads=[khp, (kB1,)], writes=[(kH, hc)])
                    for jt in range(t0 // 128, t1 // 128):
                        toff = jt * 128 - t0
                        yps = [C.psA[2 + 2 * (jt % 2)], C.psA[3 + 2 * (jt % 2)]]
                        kyps = [("psA", 2 + 2 * (jt % 2)), ("psA", 3 + 2 * (jt % 2))]
                        for hc in range(NCH):
                            for half in range(2):
                                P.op("tensor", lambda e, o=yps[half][:],
                                     l=h[:, hc, toff:toff + 128],
                                     r=w2t[:, hc, half * 512:(half + 1) * 512],
                                     s=(hc == 0), t=(hc == NCH - 1):
                                     e.matmul(o, l, r, start=s, stop=t),
                                     reads=[(kw2,)] + [(kH, h2) for h2 in range(NCH)],
                                     writes=[kyps[half]])
                        for half in range(2):
                            xsl = C.x_tm[:, b, jt, half * 512:(half + 1) * 512]
                            kx = ("x", b, jt)
                            P.op("vector", lambda e, o=xsl, i=yps[half][:],
                                 s=gates[:, b, jt, e_:e_ + 1], a=xsl:
                                 e.scalar_tensor_tensor(o, i, s, a, ALU.mult, ALU.add),
                                 reads=[kyps[half], (kGT, b, jt), kx], writes=[kx])


GRP = 384
NG = 19
NTAB = NG * GRP
WROWS = 8 * 8 * 128


def moe_dram(C):
    if not hasattr(C, "xfd"):
        nc = C.nc
        C.xfd = nc.dram_tensor("xfd", [B * NTOK, 1024], BF16, kind="Internal").ap()
        C.tabd = nc.dram_tensor("tabd", [NTAB, 2], F32, kind="Internal").ap()
        C.yslot = nc.dram_tensor("yslot", [NTAB, 1024], F32, kind="Internal").ap()
    return C.xfd, C.tabd, C.yslot


def moe_regs(C):
    if not hasattr(C, "moeregs"):
        C.moeregs = {}

        def setup(e, d=C.moeregs):
            for nm, val in (("tab", NTAB - 1), ("tok", B * NTOK - 1),
                            ("wrow", WROWS - 1)):
                r = e.alloc_register("bc_" + nm)
                inst = e.reg_mov(r, val)
                d[nm] = r
            return inst

        C.P.op("gpsimd", setup)
    return C.moeregs


def emit_moe_sparse(C, gw_ap, w1s_ap, w2s_ap, b2_ap,
                    l2_ap, l18_ap, if32_ap, rconst_ap,
                    thr_ap, iotag_ap, iotap_ap):
    P, nc = C.P, C.nc
    xfd, tabd, yslot = moe_dram(C)
    rgs = moe_regs(C)
    C.arena_reset()
    gw, kG = C.carve(C.name("gw"), [128, NCH, 8], BF16)
    l2c, kL2 = C.carve(C.name("l2c"), [128, 128], F32)
    if32, kI32 = C.carve(C.name("if32"), [128, 128], F32)
    l18, kL18 = C.carve(C.name("l18"), [128, 18], F32)
    rcn, kRC = C.carve(C.name("rcn"), [128, B, NT, 3], F32)
    thr, kTH = C.carve(C.name("thr"), [128, 11], F32)
    iotag, kIG = C.carve(C.name("iotag"), [128, NG], F32)
    iotap, kIP = C.carve(C.name("iotap"), [128, NG], F32)
    twos, kTW = C.carve(C.name("twos"), [128, 1], F32)
    onesr, kOR = C.carve(C.name("onesr"), [128, 128], F32)
    P.dma("gpsimd", gw[:], gw_ap, writes=[(kG,)])
    P.dma("sync", l2c[:], l2_ap, writes=[(kL2,)])
    P.dma("sync", if32[:], if32_ap, writes=[(kI32,)])
    P.dma("sync", l18[0:18, 0:18], l18_ap, writes=[(kL18,)])
    P.dma("sync", rcn[:], rconst_ap, writes=[(kRC,)])
    P.dma("sync", thr[:], thr_ap, writes=[(kTH,)])
    P.dma("sync", iotag[:], iotag_ap, writes=[(kIG,)])
    P.dma("sync", iotap[:], iotap_ap, writes=[(kIP,)])
    P.op("vector", lambda e: e.memset(twos[:], 2.0), writes=[(kTW,)])
    P.op("vector", lambda e: e.memset(onesr[:], 1.0), writes=[(kOR,)])
    if "b2s" in C.smalls:
        b2s, gfm = C.smalls["b2s"], C.smalls["gfm"]
        kB2, kGF = (id(b2s),), (id(gfm),)
    else:
        b2s = C.smalls["b2s"] = C.sb(C.name("b2s_"), [8, 1024], BF16)
        gfm = C.smalls["gfm"] = C.sb(C.name("gfm_"), [8, 128], BF16)
        kB2, kGF = (id(b2s),), (id(gfm),)
        P.dma("gpsimd", b2s[:], b2_ap, writes=[kB2])

    pre, kPRE = C.carve(C.name("pre"), [128, NTAB // 128, 2], F32)
    P.op("vector", lambda e: e.memset(pre[:], 1.0e9), writes=[(kPRE,)])
    P.dma("gpsimd", tabd.rearrange("(t p) w -> p t w", p=128), pre[:],
          reads=[(kPRE,)], writes=[("tabd",)])

    emit_ln_all(C, LN_EPS, dram_out=xfd)

    gates, kGT = C.carve(C.name("gates"), [128, B, NT, 8], F32)
    asn12, kAS = C.carve(C.name("asn12"), [128, B, NT, 16], F32)
    gsc, kGS = C.carve(C.name("gsc"), [128, 2, 32], F32)
    asnc, kAC = C.carve(C.name("asnc"), [128, 2, 8], F32)

    for b in range(B):
        for jt in range(NT):
            ji = b * NT + jt
            par = ji % 2
            pp = C.psA[par]
            kpp = ("psA", par)
            for ci in range(NCH):
                P.op("tensor", lambda e, o=pp[:, 0:8], l=C.xf[:, ci, b, PAD + jt * 128:PAD + (jt + 1) * 128],
                     r=gw[:, ci, :], s=(ci == 0), t=(ci == NCH - 1):
                     e.matmul(o, l, r, start=s, stop=t),
                     reads=[(kG,)] + [("xf", c2, b, jt) for c2 in range(NCH)],
                     writes=[kpp])
            g = gsc[:, par, :]
            kg = (kGS, par)
            m1, m2, dif, wa, wb = (g[:, 0:1], g[:, 1:2], g[:, 2:3], g[:, 3:4], g[:, 4:5])
            msk = g[:, 24:32]
            eq1 = asn12[:, b, jt, 0:8]
            eq2 = asn12[:, b, jt, 8:16]
            kq = (kAS, ji)
            v05 = rcn[:, b, jt, 0:1]
            P.op("vector", lambda e, o=m1, i=pp[:, 0:8]: e.reduce_max(o, i, axis=AX.X),
                 reads=[kpp], writes=[kg])
            P.op("vector", lambda e, o=eq1, i=pp[:, 0:8], s=m1, s2=v05:
                 e.tensor_scalar(o, i, s, s2, ALU.is_ge, ALU.mult),
                 reads=[kpp, kg, (kRC,)], writes=[kq])
            P.op("vector", lambda e, o=msk, i=eq1, s=pp[:, 0:8]:
                 e.scalar_tensor_tensor(o, i, -2e9, s, ALU.mult, ALU.add),
                 reads=[kpp, kq], writes=[kg])
            P.op("vector", lambda e, o=m2, i=msk: e.reduce_max(o, i, axis=AX.X),
                 reads=[kg], writes=[kg])
            P.op("vector", lambda e, o=eq2, i=msk, s=m2, s2=v05:
                 e.tensor_scalar(o, i, s, s2, ALU.is_ge, ALU.mult),
                 reads=[kg, (kRC,)], writes=[kq])
            P.op("vector", lambda e, o=dif, a=m1, b2_=m2: e.tensor_sub(o, a, b2_),
                 reads=[kg], writes=[kg])
            P.op("scalar", lambda e, o=wa, i=dif: e.activation(o, i, AF.Sigmoid),
                 reads=[kg], writes=[kg])
            P.op("scalar", lambda e, o=wb, i=dif: e.activation(o, i, AF.Sigmoid, scale=-1.0),
                 reads=[kg], writes=[kg])
            P.op("vector", lambda e, o=msk, i=eq1, s=wa:
                 e.tensor_scalar(o, i, s, None, ALU.mult),
                 reads=[kg, kq], writes=[kg])
            P.op("vector", lambda e, o=gates[:, b, jt, :], i=eq2, s=wb, a=msk:
                 e.scalar_tensor_tensor(o, i, s, a, ALU.mult, ALU.add),
                 reads=[kg, kq], writes=[(kGT, b, jt)])
            P.op("vector", lambda e, o=asnc[:, par, :], a=eq1, b2_=eq2:
                 e.tensor_add(o, a, b2_), reads=[kq], writes=[(kAC, par)])
            P.op("tensor", lambda e, o=C.psA[4][0:8, ji:ji + 1], l=asnc[:, par, :],
                 r=twos[:], s=True, t=True: e.matmul(o, l, r, start=s, stop=t),
                 reads=[(kAC, par), (kTW,)], writes=[("psA", 4)])
            gcb, kgcb = (C.moe_gcb, ("moe_gcb", par))
            P.op("vector", lambda e, o=gcb[:, par, :], i=gates[:, b, jt, :]:
                 e.tensor_copy(o, i), reads=[(kGT, b, jt)], writes=[kgcb])
            pt = C.psT[par][:, 0:128]
            kpt = ("psA", 6 + par)
            P.op("tensor", lambda e, o=pt[0:8, :], i=gcb[:, par, :], idn=C.ident[:]:
                 e.transpose(o, i, idn), reads=[kgcb, ("ident",)], writes=[kpt])
            P.op("vector", lambda e, o=gfm[0:8, :], i=pt[0:8, :]: e.tensor_copy(o, i),
                 reads=[kpt], writes=[kGF])
            for half in range(2):
                bp = C.psA[2 + half]
                kbp = ("psA", 2 + half)
                P.op("tensor", lambda e, o=bp[:], l=gfm[0:8, :],
                     r=b2s[0:8, half * 512:(half + 1) * 512]:
                     e.matmul(o, l, r, start=True, stop=True),
                     reads=[kGF, kB2], writes=[kbp])
                xsl = C.x_tm[:, b, jt, half * 512:(half + 1) * 512]
                P.op("vector", lambda e, o=xsl, i=xsl, p=bp[:]: e.tensor_add(o, i, p),
                     reads=[("x", b, jt), kbp], writes=[("x", b, jt)])

    cnt_sb, kCS = C.carve(C.name("cnt_sb"), [128, 18], F32)
    cstat, kCT2 = C.carve(C.name("cstat"), [128, 16], F32)
    ge11, kGE = C.carve(C.name("ge11"), [128, 11], F32)
    cntT, kCT = C.carve(C.name("cntT"), [128, 8], F32)
    baseT, kBT = C.carve(C.name("baseT"), [128, 8], F32)
    bases18, kBA = C.carve(C.name("bases18"), [128, 8], F32)
    geof, kGO = C.carve(C.name("geof"), [128, NG], F32)
    eofr, kEO = C.carve(C.name("eofr"), [128, NG], F32)
    widxf, kWF = C.carve(C.name("widxf"), [128, 8, NG], F32)
    widxu, kWU = C.carve(C.name("widxu"), [128, 8, NG], mybir.dt.uint32)

    P.op("vector", lambda e, o=cnt_sb[0:8, :], i=C.psA[4][0:8, 0:18]:
         e.tensor_copy(o, i), reads=[("psA", 4)], writes=[(kCS,)])
    P.op("vector", lambda e, o=cstat[0:8, 0:1], i=cnt_sb[0:8, :]:
         e.reduce_sum(o, i, axis=AX.X), reads=[(kCS,)], writes=[(kCT2,)])
    P.op("vector", lambda e, o=ge11[0:8, :], i=thr[0:8, :], s=cstat[0:8, 0:1]:
         e.tensor_scalar(o, i, s, None, ALU.is_le),
         reads=[(kTH,), (kCT2,)], writes=[(kGE,)])
    P.op("vector", lambda e, o=cstat[0:8, 1:2], i=ge11[0:8, :]:
         e.reduce_sum(o, i, axis=AX.X), reads=[(kGE,)], writes=[(kCT2,)])
    P.op("tensor", lambda e, o=C.psA[5][0:8, 0:1], l=l18[0:8, 0:8],
         r=cstat[0:8, 1:2], s=True, t=True: e.matmul(o, l, r, start=s, stop=t),
         reads=[(kL18,), (kCT2,)], writes=[("psA", 5)])
    P.op("vector", lambda e, o=cstat[0:8, 2:3], i=C.psA[5][0:8, 0:1]:
         e.tensor_copy(o, i), reads=[("psA", 5)], writes=[(kCT2,)])
    P.op("vector", lambda e, o=cstat[0:8, 3:4], a=cstat[0:8, 2:3], b2_=cstat[0:8, 1:2]:
         e.tensor_add(o, a, b2_), reads=[(kCT2,)], writes=[(kCT2,)])
    P.op("vector", lambda e, o=cstat[0:8, 4:5], i=cstat[0:8, 2:3]:
         e.tensor_scalar_mul(o, i, float(GRP)), reads=[(kCT2,)], writes=[(kCT2,)])
    P.op("tensor", lambda e, o=C.psA[6][0:18, 0:8],
         i=cnt_sb[0:8, 0:18], idn=if32[0:8, 0:8]: e.transpose(o, i, idn),
         reads=[(kCS,), (kI32,)], writes=[("psA", 6)])
    P.op("vector", lambda e, o=cntT[0:18, :], i=C.psA[6][0:18, 0:8]:
         e.tensor_copy(o, i), reads=[("psA", 6)], writes=[(kCT,)])
    P.op("tensor", lambda e, o=C.psA[7][0:1, 0:8], i=cstat[0:8, 4:5],
         idn=if32[0:8, 0:8]: e.transpose(o, i, idn),
         reads=[(kCT2,), (kI32,)], writes=[("psA", 7)])
    P.op("vector", lambda e, o=baseT[0:1, :], i=C.psA[7][0:1, 0:8]:
         e.tensor_copy(o, i), reads=[("psA", 7)], writes=[(kBT,)])
    P.op("tensor", lambda e, o=C.psA[2][0:18, 0:8], l=l18[0:18, 0:18],
         r=cntT[0:18, :], s=True, t=False: e.matmul(o, l, r, start=s, stop=t),
         reads=[(kL18,), (kCT,)], writes=[("psA", 2)])
    P.op("tensor", lambda e, o=C.psA[2][0:18, 0:8], l=onesr[0:1, 0:18],
         r=baseT[0:1, :], s=False, t=True: e.matmul(o, l, r, start=s, stop=t),
         reads=[(kOR,), (kBT,)], writes=[("psA", 2)])
    P.op("vector", lambda e, o=bases18[0:18, :], i=C.psA[2][0:18, 0:8]:
         e.tensor_copy(o, i), reads=[("psA", 2)], writes=[(kBA,)])
    P.op("vector", lambda e, o=geof[0:8, :], i=iotag[0:8, :], s=cstat[0:8, 3:4]:
         e.tensor_scalar(o, i, s, None, ALU.is_ge),
         reads=[(kIG,), (kCT2,)], writes=[(kGO,)])
    P.op("tensor", lambda e, o=C.psA[5][0:1, 0:NG], l=onesr[0:8, 0:1],
         r=geof[0:8, :], s=True, t=True: e.matmul(o, l, r, start=s, stop=t),
         reads=[(kOR,), (kGO,)], writes=[("psA", 5)])
    P.op("vector", lambda e, o=eofr[0:1, :], i=C.psA[5][0:1, 0:NG]:
         e.tensor_copy(o, i), reads=[("psA", 5)], writes=[(kEO,)])
    P.op("tensor", lambda e, o=C.psA[5][0:128, 0:NG], l=onesr[0:1, :],
         r=eofr[0:1, :], s=True, t=True: e.matmul(o, l, r, start=s, stop=t),
         reads=[(kOR,), (kEO,)], writes=[("psA", 5)])
    P.op("vector", lambda e, o=widxf[:, 0, :], i=C.psA[5][0:128, 0:NG],
         t2=iotap[:]: e.scalar_tensor_tensor(o, i, 1024.0, t2, ALU.mult, ALU.add),
         reads=[("psA", 5), (kIP,)], writes=[(kWF,)])
    for f in range(1, 8):
        P.op("vector", lambda e, o=widxf[:, f, :], i=widxf[:, 0, :], v=float(f * 128):
             e.tensor_scalar_add(o, i, v), reads=[(kWF,)], writes=[(kWF,)])
    P.op("vector", lambda e, o=widxu[:], i=widxf[:]: e.tensor_copy(o, i),
         reads=[(kWF,)], writes=[(kWU,)])

    brow, kBR = C.carve(C.name("brow"), [128, 2, 8], F32)
    asn2t, kA2 = C.carve(C.name("asn2t"), [128, 2, 16], F32)
    prodt, kPR = C.carve(C.name("prodt"), [128, 2, 8], F32)
    gp8, kGP = C.carve(C.name("gp8"), [128, 2, 8], F32)
    idxg, kIX = C.carve(C.name("idxg"), [128, 2, 2, 3], F32)
    slots_tok, kST = C.carve(C.name("slots_tok"), [128, B, NT, 2], mybir.dt.uint32)

    for b in range(B):
        for jt in range(NT):
            ji = b * NT + jt
            par = ji % 2
            P.op("vector", lambda e, o=asnc[:, par, :], a=asn12[:, b, jt, 0:8],
                 b2_=asn12[:, b, jt, 8:16]: e.tensor_add(o, a, b2_),
                 reads=[(kAS, ji)], writes=[(kAC, par)])
            P.op("tensor", lambda e, o=C.psA[3][0:1, 0:8], l=if32[0:18, ji:ji + 1],
                 r=bases18[0:18, :], s=True, t=True: e.matmul(o, l, r, start=s, stop=t),
                 reads=[(kI32,), (kBA,)], writes=[("psA", 3)])
            P.op("vector", lambda e, o=brow[0:1, par, :], i=C.psA[3][0:1, 0:8]:
                 e.tensor_copy(o, i), reads=[("psA", 3)], writes=[(kBR, par)])
            sg = C.psA[2]
            ksg = ("psA", 2)
            P.op("tensor", lambda e, o=sg[:, 0:8], l=l2c[:], r=asnc[:, par, :],
                 s=True, t=False: e.matmul(o, l, r, start=s, stop=t),
                 reads=[(kL2,), (kAC, par)], writes=[ksg])
            P.op("tensor", lambda e, o=sg[:, 0:8], l=onesr[0:1, :],
                 r=brow[0:1, par, :], s=False, t=True: e.matmul(o, l, r, start=s, stop=t),
                 reads=[(kOR,), (kBR, par)], writes=[ksg])
            P.op("vector", lambda e, o=asn2t[:, par, :], i=asn12[:, b, jt, :]:
                 e.tensor_scalar_mul(o, i, 2.0), reads=[(kAS, ji)], writes=[(kA2, par)])
            for c in range(2):
                P.op("vector", lambda e, o=prodt[:, par, :], a=asn2t[:, par, c * 8:(c + 1) * 8],
                     b2_=sg[:, 0:8]: e.tensor_mul(o, a, b2_),
                     reads=[(kA2, par), ksg], writes=[(kPR, par)])
                P.op("vector", lambda e, o=idxg[:, par, c, 2:3],
                     i=prodt[:, par, :]:
                     e.reduce_sum(o, i, axis=AX.X), reads=[(kPR, par)], writes=[(kIX, par, c)])
                P.op("vector", lambda e, o=idxg[:, par, c, 2:3], i=idxg[:, par, c, 2:3],
                     a=rcn[:, b, jt, 1:2]: e.tensor_add(o, i, a),
                     reads=[(kIX, par, c), (kRC,)], writes=[(kIX, par, c)])
                P.op("vector", lambda e, o=idxg[:, par, c, 0:1], i=rcn[:, b, jt, 2:3]:
                     e.tensor_copy(o, i), reads=[(kRC,)], writes=[(kIX, par, c)])
                P.op("vector", lambda e, o=gp8[:, par, :], a=asn2t[:, par, c * 8:(c + 1) * 8],
                     b2_=gates[:, b, jt, :]: e.tensor_mul(o, a, b2_),
                     reads=[(kA2, par), (kGT, b, jt)], writes=[(kGP, par)])
                P.op("vector", lambda e, o=idxg[:, par, c, 1:2], i=gp8[:, par, :]:
                     e.reduce_sum(o, i, axis=AX.X), reads=[(kGP, par)], writes=[(kIX, par, c)])
                P.op("vector", lambda e, o=slots_tok[:, b, jt, c:c + 1], i=idxg[:, par, c, 2:3]:
                     e.tensor_copy(o, i), reads=[(kIX, par, c)], writes=[(kST, ji, c)])
                P.op("gpsimd",
                     lambda e, o=tabd, ix=slots_tok[:, b, jt, c:c + 1], i=idxg[:, par, c, 0:2],
                     rg=rgs:
                     e.indirect_dma_start(out=o, out_offset=bass.IndirectOffsetOnAxis(ix, 0),
                                          in_=i, in_offset=None,
                                          bounds_check=rg["tab"], oob_is_err=False),
                     reads=[(kST, ji, c), (kIX, par, c)], writes=[("tabd",)],
                     is_dma=True)

    tg, kTGb = C.carve(C.name("tg"), [128, 2, 3, 2], F32)
    idxu, kIU = C.carve(C.name("idxu"), [128, 2, 3], mybir.dt.uint32)
    xg_tm, kXT = C.carve(C.name("xg_tm"), [128, 2, 1024], BF16)
    xg_fm, kXF = C.carve(C.name("xg_fm"), [128, 2, NCH, GRP], BF16)
    h_sb, kH = C.carve(C.name("h_sb"), [128, 2, 4, GRP], BF16)
    ygs, kYG = C.carve(C.name("ygs"), [128, 3, 1024], F32)
    w1g, kW1 = C.carve(C.name("w1g"), [128, 2, 4104], BF16)
    w1b = w1g.bitcast(F32)
    w2g, kW2 = C.carve(C.name("w2g"), [128, 2, 4096], BF16)
    w1p = w1g[:, :, 0:4096].rearrange("p r (cp i m) -> p r cp i m", cp=4, i=2, m=512)
    w2p = w2g.rearrange("p r (cp i m) -> p r cp i m", cp=2, i=2, m=1024)

    for gidx in range(NG):
        parg = gidx % 2
        P.dma("sync", tg[:, parg, :, :],
              tabd[gidx * GRP:(gidx + 1) * GRP, :].rearrange("(t p) w -> p t w", p=128),
              reads=[("tabd",)], writes=[(kTGb, parg)])
        P.op("vector", lambda e, o=idxu[:, parg, :], i=tg[:, parg, :, 0]:
             e.tensor_copy(o, i), reads=[(kTGb, parg)], writes=[(kIU, parg)])
        for ct in range(3):
            pt2 = ct % 2
            P.op("gpsimd",
                 lambda e, o=xg_tm[:, pt2, :], ix=idxu[:, parg, ct:ct + 1], rg=rgs:
                 e.indirect_dma_start(out=o, out_offset=None, in_=xfd,
                                      in_offset=bass.IndirectOffsetOnAxis(ix, 0),
                                      bounds_check=rg["tok"], oob_is_err=False),
                 reads=[(kIU, parg), ("xfd",)], writes=[(kXT, pt2)], is_dma=True)
            for cc in range(NCH):
                pt = C.psT[cc % 2][:, 0:128]
                kpt = ("psA", 6 + cc % 2)
                P.op("tensor", lambda e, o=pt, i=xg_tm[:, pt2, cc * 128:(cc + 1) * 128],
                     idn=C.ident[:]: e.transpose(o, i, idn),
                     reads=[(kXT, pt2), ("ident",)], writes=[kpt])
                dst = xg_fm[:, parg, cc, ct * 128:(ct + 1) * 128]
                if cc % 2 == 0:
                    P.op("vector", lambda e, o=dst, i=pt: e.tensor_copy(o, i),
                         reads=[kpt], writes=[(kXF, parg, cc)])
                else:
                    P.op("scalar", lambda e, o=dst, i=pt: e.activation(o, i, AF.Copy),
                         reads=[kpt], writes=[(kXF, parg, cc)])
        def emit_w2(f):
            parf = f % 2
            for ct in range(3):
                for half in range(2):
                    yb = C.psA[ct * 2 + half]
                    kyb = ("psA", ct * 2 + half)
                    for hc in range(4):
                        P.op("tensor", lambda e, o=yb[:],
                             l=h_sb[:, parf, hc, ct * 128:(ct + 1) * 128],
                             r=w2g[:, parf, hc * 1024 + half * 512: hc * 1024 + (half + 1) * 512],
                             s=(f == 0 and hc == 0), t=(f == 7 and hc == 3):
                             e.matmul(o, l, r, start=s, stop=t),
                             reads=[(kW2, parf)] + [(kH, parf, h2) for h2 in range(4)],
                             writes=[kyb])

        for f in range(8):
            parf = f % 2
            P.op("gpsimd",
                 lambda e, o=w1g[:, parf, :], ix=widxu[:, f, gidx:gidx + 1], rg=rgs:
                 e.indirect_dma_start(out=o, out_offset=None, in_=w1s_ap,
                                      in_offset=bass.IndirectOffsetOnAxis(ix, 0),
                                      bounds_check=rg["wrow"], oob_is_err=False),
                 reads=[(kWU,)], writes=[(kW1, parf)], is_dma=True)
            P.op("gpsimd",
                 lambda e, o=w2g[:, parf, :], ix=widxu[:, f, gidx:gidx + 1], rg=rgs:
                 e.indirect_dma_start(out=o, out_offset=None, in_=w2s_ap,
                                      in_offset=bass.IndirectOffsetOnAxis(ix, 0),
                                      bounds_check=rg["wrow"], oob_is_err=False),
                 reads=[(kWU,)], writes=[(kW2, parf)], is_dma=True)
            for hc in range(4):
                hp = C.psA[6 + hc % 2]
                khp = ("psA", 6 + hc % 2)
                for ci in range(NCH):
                    P.op("tensor", lambda e, o=hp[:, 0:GRP],
                         l=w1g[:, parf, ci * 512 + hc * 128: ci * 512 + (hc + 1) * 128],
                         r=xg_fm[:, parg, ci, :],
                         s=(ci == 0), t=(ci == NCH - 1):
                         e.matmul(o, l, r, start=s, stop=t),
                         reads=[(kW1, parf)] + [(kXF, parg, c2) for c2 in range(NCH)],
                         writes=[khp])
                P.op("scalar", lambda e, o=h_sb[:, parf, hc, :], i=hp[:, 0:GRP],
                     bb=w1b[:, parf, 2048 + hc:2049 + hc]:
                     e.activation(o, i, AF.Silu, bias=bb),
                     reads=[khp, (kW1, parf)], writes=[(kH, parf, hc)])
            if f > 0:
                emit_w2(f - 1)
        emit_w2(7)
        for ct in range(3):
            for half in range(2):
                yb = C.psA[ct * 2 + half]
                kyb = ("psA", ct * 2 + half)
                if half == 0:
                    P.op("vector", lambda e, o=ygs[:, ct, half * 512:(half + 1) * 512],
                         i=yb[:], s=tg[:, parg, ct, 1:2]:
                         e.tensor_scalar(o, i, s, None, ALU.mult),
                         reads=[kyb, (kTGb, parg)], writes=[(kYG, ct)])
                else:
                    P.op("scalar", lambda e, o=ygs[:, ct, half * 512:(half + 1) * 512],
                         i=yb[:], s=tg[:, parg, ct, 1:2]:
                         e.activation(o, i, AF.Identity, scale=s),
                         reads=[kyb, (kTGb, parg)], writes=[(kYG, ct)])
        P.dma("scalar", yslot[gidx * GRP:(gidx + 1) * GRP, :].rearrange("(t p) d -> p t d", p=128),
              ygs[:], reads=[(kYG, 0), (kYG, 1), (kYG, 2)], writes=[("yslot",)])

    rb, kRB = C.carve(C.name("rb"), [128, 2, 1024], F32)
    for b in range(B):
        for jt in range(NT):
            for c in range(2):
                par = c
                P.op("gpsimd",
                     lambda e, o=rb[:, par, :], ix=slots_tok[:, b, jt, c:c + 1], rg=rgs:
                     e.indirect_dma_start(out=o, out_offset=None, in_=yslot,
                                          in_offset=bass.IndirectOffsetOnAxis(ix, 0),
                                          bounds_check=rg["tab"], oob_is_err=False),
                     reads=[(kST, b * NT + jt, c), ("yslot",)], writes=[(kRB, par)],
                     is_dma=True)
                xsl = C.x_tm[:, b, jt, :]
                P.op("vector", lambda e, o=xsl, i=xsl, a=rb[:, par, :]:
                     e.tensor_add(o, i, a),
                     reads=[("x", b, jt), (kRB, par)], writes=[("x", b, jt)])



def emit_s5(C, wq_ap, wk_ap, wv_ap, wo_ap, qkn_ap, alibi_ap, alibi0_ap):
    P, nc = C.P, C.nc
    C.arena_reset()
    ws = {}
    for nm, ap in (("wq", wq_ap), ("wk", wk_ap), ("wv", wv_ap), ("wo", wo_ap)):
        t, k = C.carve(C.name(nm), [128, NCH, 1024], BF16)
        P.dma("gpsimd", t[:], ap.rearrange("(c p) m -> p c m", p=128), writes=[(k,)])
        ws[nm] = (t, k)
    qkn, kN = C.carve(C.name("qkn"), [128, 16], F32)
    P.dma("sync", qkn[:], qkn_ap, writes=[(kN,)])
    alibi, kAL = C.carve(C.name("alibi"), [128, 8, 144], BF16)
    P.dma("sync", alibi[:], alibi_ap, writes=[(kAL,)])
    alibi0, kAL0 = C.carve(C.name("alibi0"), [128, 8, 48], BF16)
    P.dma("sync", alibi0[:], alibi0_ap, writes=[(kAL0,)])

    emit_ln_all(C, LN_EPS)

    q_fm, kQF = C.carve(C.name("q_fm"), [128, NCH, 128], BF16)
    k_fm, kKF = C.carve(C.name("k_fm"), [128, NCH, 3, 128], BF16)
    v_tm, kVT = C.carve(C.name("v_tm"), [128, 3, 1024], BF16)
    at_fm, kAT = C.carve(C.name("at_fm"), [128, NCH, 128], BF16)
    pbn, kPN = C.carve(C.name("pbn"), [128, 3, 144], BF16)
    pta, kTA = C.carve(C.name("pta"), [128, 3, 128], BF16)
    ptb, kTB = C.carve(C.name("ptb"), [128, 3, 128], BF16)
    qs_tm, kQS = C.carve(C.name("qs_tm"), [128, 1024], BF16)
    st5, kS5 = C.carve(C.name("st5"), [128, 4, 8], F32)

    def proj_rms(b, tok0, dst, kdst, dstoff, wname, normcol, par):
        nt = min(128, NTOK - tok0)
        (wt, kw) = ws[wname]
        st = st5[:, par, :]
        kst = (kS5, par)
        xdeps = xf_keys(C, b, max(tok0, 0), tok0 + nt) + [("xfpad",)]
        pq = [C.psA[4], C.psA[5]]
        for half in range(2):
            kqp = ("psA", 4 + half)
            for ci in range(NCH):
                P.op("tensor", lambda e, o=pq[half][0:nt, :],
                     l=C.xf[:, ci, b, PAD + tok0:PAD + tok0 + nt],
                     r=wt[:, ci, half * 512:(half + 1) * 512],
                     s=(ci == 0), t=(ci == NCH - 1):
                     e.matmul(o, l, r, start=s, stop=t),
                     reads=[(kw,)] + xdeps, writes=[kqp])
        if normcol is None:
            for half in range(2):
                P.op("scalar", lambda e, o=dst[:, dstoff + half * 512:dstoff + (half + 1) * 512],
                     i=pq[half][:]: e.activation(o, i, AF.Identity),
                     reads=[("psA", 4 + half)], writes=[(kdst, dstoff)])
            return
        for half in range(2):
            P.op("scalar", lambda e, o=qs_tm[0:nt, half * 512:(half + 1) * 512],
                 i=pq[half][0:nt, :], a=st[0:nt, half:half + 1]:
                 e.activation(o, i, AF.Square, accum_out=a),
                 reads=[("psA", 4 + half)], writes=[kst, (kQS, "scratch")])
        P.op("vector", lambda e, o=st[0:nt, 2:3], a=st[0:nt, 0:1], b2=st[0:nt, 1:2]:
             e.tensor_add(o, a, b2), reads=[kst], writes=[kst])
        P.op("scalar", lambda e, o=st[0:nt, 3:4], i=st[0:nt, 2:3], bb=C.epsv[0:nt, 1:2]:
             e.activation(o, i, AF.Sqrt, scale=1.0 / 1024.0, bias=bb),
             reads=[kst, ("epsv",)], writes=[kst])
        P.op("vector", lambda e, o=st[0:nt, 4:5], i=st[0:nt, 3:4]: e.reciprocal(o, i),
             reads=[kst], writes=[kst])
        for half in range(2):
            P.op("vector", lambda e, o=qs_tm[0:nt, half * 512:(half + 1) * 512],
                 i=pq[half][0:nt, :], s=st[0:nt, 4:5]:
                 e.tensor_scalar(o, i, s, None, ALU.mult),
                 reads=[("psA", 4 + half), kst, (kQS, "scratch")], writes=[(kQS,)])
        for c in range(NCH):
            pt = C.psT[c % 2][:, 0:128]
            kpt = ("psA", 6 + c % 2)
            P.op("tensor", lambda e, o=pt[:, 0:nt], i=qs_tm[0:nt, c * 128:(c + 1) * 128],
                 idn=C.ident[0:nt, 0:nt]: e.transpose(o, i, idn),
                 reads=[(kQS,)], writes=[kpt])
            P.op("vector", lambda e, o=dst[:, c, dstoff:dstoff + nt], i=pt[:, 0:nt],
                 s=qkn[:, normcol + c:normcol + c + 1]:
                 e.tensor_scalar(o, i, s, None, ALU.mult),
                 reads=[kpt, (kN,)], writes=[(kdst, dstoff)])

    def proj_v(b, tok0, vi):
        nt = min(128, NTOK - tok0)
        (wt, kw) = ws["wv"]
        xdeps = xf_keys(C, b, max(0, tok0), tok0 + nt) + [("xfpad",)]
        for half in range(2):
            pv = C.psA[4 + half]
            kvp = ("psA", 4 + half)
            for ci in range(NCH):
                P.op("tensor", lambda e, o=pv[0:nt, :],
                     l=C.xf[:, ci, b, PAD + tok0:PAD + tok0 + nt],
                     r=wt[:, ci, half * 512:(half + 1) * 512],
                     s=(ci == 0), t=(ci == NCH - 1):
                     e.matmul(o, l, r, start=s, stop=t),
                     reads=[(kw,)] + xdeps, writes=[kvp])
            P.op("scalar", lambda e, o=v_tm[0:nt, vi, half * 512:(half + 1) * 512],
                 i=pv[0:nt, :]: e.activation(o, i, AF.Identity),
                 reads=[kvp], writes=[(kVT, vi)])

    def proj_rms2(b, tok0, sl):
        nt = min(128, NTOK - tok0)
        (wt, kw) = ws["wk"]
        par = sl % 2
        st = st5[:, par, :]
        kst = (kS5, par)
        xdeps = xf_keys(C, b, max(tok0, 0), tok0 + nt) + [("xfpad",)]
        pq = [C.psA[4], C.psA[5]]
        for half in range(2):
            kqp = ("psA", 4 + half)
            for ci in range(NCH):
                P.op("tensor", lambda e, o=pq[half][0:nt, :],
                     l=C.xf[:, ci, b, PAD + tok0:PAD + tok0 + nt],
                     r=wt[:, ci, half * 512:(half + 1) * 512],
                     s=(ci == 0), t=(ci == NCH - 1):
                     e.matmul(o, l, r, start=s, stop=t),
                     reads=[(kw,)] + xdeps, writes=[kqp])
        for half in range(2):
            P.op("scalar", lambda e, o=qs_tm[0:nt, half * 512:(half + 1) * 512],
                 i=pq[half][0:nt, :], a=st[0:nt, half:half + 1]:
                 e.activation(o, i, AF.Square, accum_out=a),
                 reads=[("psA", 4 + half)], writes=[kst, (kQS, "scratch")])
        P.op("vector", lambda e, o=st[0:nt, 2:3], a=st[0:nt, 0:1], b2=st[0:nt, 1:2]:
             e.tensor_add(o, a, b2), reads=[kst], writes=[kst])
        P.op("scalar", lambda e, o=st[0:nt, 3:4], i=st[0:nt, 2:3], bb=C.epsv[0:nt, 1:2]:
             e.activation(o, i, AF.Sqrt, scale=1.0 / 1024.0, bias=bb),
             reads=[kst, ("epsv",)], writes=[kst])
        P.op("vector", lambda e, o=st[0:nt, 4:5], i=st[0:nt, 3:4]: e.reciprocal(o, i),
             reads=[kst], writes=[kst])
        for half in range(2):
            P.op("vector", lambda e, o=qs_tm[0:nt, half * 512:(half + 1) * 512],
                 i=pq[half][0:nt, :], s=st[0:nt, 4:5]:
                 e.tensor_scalar(o, i, s, None, ALU.mult),
                 reads=[("psA", 4 + half), kst, (kQS, "scratch")], writes=[(kQS,)])
        for c in range(NCH):
            pt = C.psT[c % 2][:, 0:128]
            kpt = ("psA", 6 + c % 2)
            P.op("tensor", lambda e, o=pt[:, 0:nt], i=qs_tm[0:nt, c * 128:(c + 1) * 128],
                 idn=C.ident[0:nt, 0:nt]: e.transpose(o, i, idn),
                 reads=[(kQS,)], writes=[kpt])
            P.op("vector", lambda e, o=k_fm[:, c, sl, 0:nt], i=pt[:, 0:nt],
                 s=qkn[:, 8 + c:9 + c]:
                 e.tensor_scalar(o, i, s, None, ALU.mult),
                 reads=[kpt, (kN,)], writes=[(kKF, sl)])

    (wot, kwo) = ws["wo"]
    for b in range(B):
        for jt in range(NT):
            if jt == 0:
                proj_rms2(b, -16, 0)
                proj_v(b, -16, 0)
            proj_rms2(b, jt * 128 + 112, (jt + 1) % 3)
            proj_v(b, jt * 128 + 112, (jt + 1) % 3)
            proj_rms(b, jt * 128, q_fm, kQF, 0, "wq", 0, jt % 2)
            sa = jt % 3
            sb2 = (jt + 1) % 3

            def emit_qk(hh):
                par = (jt * 8 + hh) % 2
                sp = C.psA[0 + 2 * par]
                ksp = ("psA", 0 + 2 * par)
                P.op("tensor", lambda e, o=sp[:, 0:128],
                     l=q_fm[:, hh, :], r=k_fm[:, hh, sa, :]:
                     e.matmul(o, l, r, start=True, stop=True),
                     reads=[(kQF, 0), (kKF, sa)], writes=[ksp])
                P.op("tensor", lambda e, o=sp[:, 128:144],
                     l=q_fm[:, hh, :], r=k_fm[:, hh, sb2, 0:16]:
                     e.matmul(o, l, r, start=True, stop=True),
                     reads=[(kQF, 0), (kKF, sb2)], writes=[ksp])

            def emit_soft(hh):
                par = (jt * 8 + hh) % 2
                par3 = (jt * 8 + hh) % 3
                sp = C.psA[0 + 2 * par]
                ksp = ("psA", 0 + 2 * par)
                sc = sp[:, 144:288]
                ksc = ksp
                P.op("vector", lambda e, o=sc[:, 0:144], i=sp[:, 0:144],
                     a=alibi[:, hh, :]:
                     e.scalar_tensor_tensor(o, i, 0.08838834764831845, a,
                                            ALU.mult, ALU.add),
                     reads=[ksp, (kAL,)], writes=[ksc])
                if jt == 0:
                    P.op("vector", lambda e, o=sc[:, 0:48], i=sp[:, 0:48],
                         a=alibi0[:, hh, :]:
                         e.scalar_tensor_tensor(o, i, 0.08838834764831845, a,
                                                ALU.mult, ALU.add),
                         reads=[ksp, (kAL0,)], writes=[ksc])
                nm2 = st5[:, par, 6:7]
                P.op("vector", lambda e, o=nm2, i=sc[:, 0:144]:
                     e.reduce_max(o, i, axis=AX.X, negate=True),
                     reads=[ksc], writes=[(kS5, par, "nm")])
                den = st5[:, par, 7:8]
                P.op("scalar", lambda e, o=pbn[:, par3, :], i=sc[:, 0:144], bb=nm2, a=den:
                     e.activation(o, i, AF.Exp, bias=bb, accum_out=a),
                     reads=[ksc, (kS5, par, "nm")],
                     writes=[(kPN, par3), (kS5, par, "den")])
                rde = st5[:, par, 5:6]
                P.op("vector", lambda e, o=rde, i=den: e.reciprocal(o, i),
                     reads=[(kS5, par, "den")], writes=[(kS5, par, "rde")])
                P.op("vector", lambda e, o=pbn[:, par3, :], i=pbn[:, par3, :], s=rde:
                     e.tensor_scalar(o, i, s, None, ALU.mult),
                     reads=[(kPN, par3), (kS5, par, "rde")], writes=[(kPN, par3)])

            def emit_pv(hh):
                par = (jt * 8 + hh) % 2
                par3 = (jt * 8 + hh) % 3
                pt = C.psT[par][:, 0:128]
                kpt = ("psA", 6 + par)
                P.op("tensor", lambda e, o=pt, i=pbn[:, par3, 0:128],
                     idn=C.ident[:]: e.transpose(o, i, idn),
                     reads=[(kPN, par3)], writes=[kpt])
                P.op("vector", lambda e, o=pta[:, par3, :], i=pt: e.tensor_copy(o, i),
                     reads=[kpt], writes=[(kTA, par3)])
                P.op("tensor", lambda e, o=pt[0:16, :], i=pbn[:, par3, 128:144],
                     idn=C.ident[:]: e.transpose(o, i, idn),
                     reads=[(kPN, par3)], writes=[kpt])
                P.op("vector", lambda e, o=ptb[0:16, par3, :], i=pt[0:16, :]:
                     e.tensor_copy(o, i), reads=[kpt], writes=[(kTB, par3)])
                op_ = C.psA[1 + 2 * par]
                kop = ("psA", 1 + 2 * par)
                P.op("tensor", lambda e, o=op_[:, 0:128],
                     l=v_tm[:, sa, hh * 128:(hh + 1) * 128], r=pta[:, par3, :]:
                     e.matmul(o, l, r, start=True, stop=False),
                     reads=[(kVT, sa), (kTA, par3)], writes=[kop])
                P.op("tensor", lambda e, o=op_[:, 0:128],
                     l=v_tm[0:16, sb2, hh * 128:(hh + 1) * 128],
                     r=ptb[0:16, par3, :]:
                     e.matmul(o, l, r, start=False, stop=True),
                     reads=[(kVT, sb2), (kTB, par3)], writes=[kop])
                P.op("scalar", lambda e, o=at_fm[:, hh, :],
                     i=op_[:, 0:128]: e.activation(o, i, AF.Identity),
                     reads=[kop], writes=[(kAT, hh)])

            emit_qk(0)
            for hh in range(8):
                if hh + 1 < 8:
                    emit_qk(hh + 1)
                emit_soft(hh)
                emit_pv(hh)
            for half in range(2):
                pp = C.psA[4 + half]
                kpp = ("psA", 4 + half)
                for c in range(NCH):
                    P.op("tensor", lambda e, o=pp[:],
                         l=at_fm[:, c, :],
                         r=wot[:, c, half * 512:(half + 1) * 512],
                         s=(c == 0), t=(c == NCH - 1):
                         e.matmul(o, l, r, start=s, stop=t),
                         reads=[(kwo,)] + [(kAT, c2) for c2 in range(NCH)],
                         writes=[kpp])
                xsl = C.x_tm[:, b, jt, half * 512:(half + 1) * 512]
                kx = ("x", b, jt)
                P.op("vector", lambda e, o=xsl, i=xsl, p=pp[:]: e.tensor_add(o, i, p),
                     reads=[kx, kpp], writes=[kx])



def declare_io(nc, names_shapes, kind):
    return {n: nc.dram_tensor(n, s, F32, kind=kind).ap() for n, s in names_shapes}


SUB_WEIGHTS = {
    "ffn0": [("wrT", [D, D], F32), ("wkT", [D, D], F32), ("wvT", [D, D], F32),
             ("mu", [128, 16], F32)],
    "attn0": [("wqkvT", [D, 3072], F32), ("wprojT", [D, D], F32),
              ("bqkv", [128, 24], F32), ("bproj", [1, D], F32),
              ("msc", [128, NCH, 16], BF16), ("ebc", [16, NCH, 128], BF16)],
    "moe": [("gw", [128, NCH, 8], F32), ("w1s", [WROWS, 4104], BF16),
            ("w2s", [WROWS, 4096], BF16),
            ("b2", [8, D], F32), ("l2", [128, 128], F32), ("l18", [18, 18], F32),
            ("if32", [128, 128], F32), ("thr", [128, 11], F32),
            ("iotag", [128, NG], F32), ("iotap", [128, NG], F32),
            ("rconst", [128, B, NT, 3], F32)],
    "attn1": [("wq", [D, D], F32), ("wk", [D, D], F32), ("wv", [D, D], F32),
              ("wo", [D, D], F32), ("qkn", [128, 16], F32),
              ("alibi", [128, 8, 144], BF16), ("alibi0", [128, 8, 48], BF16)],
}


def build_program(subs):
    nc = bass.Bass("TRN2", target_bir_lowering=False)
    x_in = nc.dram_tensor("x_in", [B, NTOK, 1024], F32, kind="ExternalInput").ap()
    x_out = nc.dram_tensor("x_out", [B, NTOK, 1024], F32, kind="ExternalOutput").ap()
    ident_in = nc.dram_tensor("ident", [128, 128], BF16, kind="ExternalInput").ap()
    win = {}
    for s in subs:
        for n, shp, dt in SUB_WEIGHTS[s]:
            key = f"{s}_{n}"
            if key not in win:
                win[key] = nc.dram_tensor(key, shp, dt, kind="ExternalInput").ap()
    with ExitStack() as ctx:
        C = Ctx(nc, ctx)
        P = C.P
        P.dma("sync", C.x_tm[:], x_in.rearrange("b (t p) d -> p b t d", p=128),
              writes=[("x", b, j) for b in range(B) for j in range(NT)])
        P.dma("sync", C.ident[:], ident_in, writes=[("ident",)])
        P.op("vector", lambda e: e.memset(C.xf[:, :, :, 0:PAD], 0.0),
             writes=[("xfpad",)])
        P.op("vector", lambda e: e.memset(C.xf[:, :, :, PAD + NTOK - 1:PAD + NTOK], 0.0),
             writes=[("xftail",)])
        P.op("vector", lambda e: e.memset(C.epsv[:, 0:1], LN_EPS), writes=[("epsv",)])
        P.op("vector", lambda e: e.memset(C.epsv[:, 1:2], RMS_EPS), writes=[("epsv",)])
        P.op("vector", lambda e: e.memset(C.ones[:], 1.0), writes=[("ones",)])
        for s in subs:
            if s == "ffn0":
                emit_s1(C, win["ffn0_wrT"], win["ffn0_wkT"], win["ffn0_wvT"],
                        win["ffn0_mu"])
            elif s == "attn0":
                emit_s2(C, win["attn0_wqkvT"], win["attn0_wprojT"],
                        win["attn0_bqkv"], win["attn0_bproj"],
                        win["attn0_msc"], win["attn0_ebc"])
            elif s == "moe":
                emit_moe_sparse(C, win["moe_gw"], win["moe_w1s"], win["moe_w2s"],
                                win["moe_b2"], win["moe_l2"], win["moe_l18"],
                                win["moe_if32"], win["moe_rconst"],
                                win["moe_thr"], win["moe_iotag"], win["moe_iotap"])
            elif s == "attn1":
                emit_s5(C, win["attn1_wq"], win["attn1_wk"], win["attn1_wv"],
                        win["attn1_wo"], win["attn1_qkn"], win["attn1_alibi"],
                        win["attn1_alibi0"])
            else:
                raise NotImplementedError(s)
        P.dma("sync", x_out.rearrange("b (t p) d -> p b t d", p=128), C.x_tm[:],
              reads=[("x", b, j) for b in range(B) for j in range(NT)],
              is_output=True)
        P.emit(nc, ctx)
    return nc



def shard_x(x_full):
    xp = np.zeros((B, HALO + NCORE * TOWN + (NTOK - TLOC), D), np.float32)
    xp[:, HALO:HALO + NCORE * TOWN] = x_full
    shards = []
    for c in range(NCORE):
        sl = np.zeros((B, NTOK, D), np.float32)
        sl[:, :TLOC] = xp[:, c * TOWN: c * TOWN + TLOC]
        shards.append(sl)
    return shards


def unshard_x(results, key="x_out"):
    return np.concatenate([r[key][:, HALO:HALO + TOWN] for r in results], axis=1)


_IDENT = np.eye(128, dtype=BF)


def run_subblocks(subs, x_full, weight_map):
    nc = build_program(subs)
    shards = shard_x(x_full)
    in_maps = []
    for c in range(NCORE):
        m = {"x_in": shards[c], "ident": _IDENT}
        for k, v in weight_map.items():
            m[k] = v[c] if isinstance(v, list) else v
        in_maps.append(m)
    res = run_bass_kernel_spmd(nc, in_maps, core_ids=list(range(NCORE)))
    return unshard_x(res.results)


def attn0_weights(Wqkv, bqkv, Wproj, bproj):
    H16 = 16
    msc = np.zeros((128, NCH, 16), dtype=BF)
    ebc = np.zeros((16, NCH, 128), dtype=BF)
    for c in range(NCH):
        for p in range(128):
            h = 2 * c + (1 if p >= 64 else 0)
            msc[p, c, h] = 1
            ebc[h, c, p] = 1
    return {
        "attn0_wqkvT": np.ascontiguousarray(np.asarray(Wqkv, np.float32).T),
        "attn0_wprojT": np.ascontiguousarray(np.asarray(Wproj, np.float32).T),
        "attn0_bqkv": np.ascontiguousarray(np.asarray(bqkv, np.float32).reshape(24, 128).T),
        "attn0_bproj": np.asarray(bproj, np.float32).reshape(1, D),
        "attn0_msc": msc,
        "attn0_ebc": ebc,
    }


def moe_weights(gate_w, W1, b1, W2, b2):
    gate_w = np.asarray(gate_w, np.float32)
    gw = np.ascontiguousarray(gate_w.T.reshape(8, 128, 8).transpose(1, 0, 2))
    W1 = np.asarray(W1, np.float32)
    W2 = np.asarray(W2, np.float32)
    b1 = np.asarray(b1, np.float32)
    a = W1.reshape(8, 8, 128, 8, 512).transpose(0, 3, 2, 1, 4).reshape(8, 8, 128, 4096)
    w1s = np.zeros((8, 8, 128, 4104), dtype=BF)
    w1s[..., :4096] = a.astype(BF)
    b1r = np.ascontiguousarray(b1.reshape(8, 8, 4, 128).transpose(0, 1, 3, 2)).astype(np.float32)
    w1s[..., 4096:4104] = b1r.view(BF).reshape(8, 8, 128, 8)
    w1s = np.ascontiguousarray(w1s.reshape(WROWS, 4104))
    w2s = np.ascontiguousarray(
        W2.reshape(8, 8, 4, 128, 1024).transpose(0, 1, 3, 2, 4)
        .reshape(WROWS, 4096)).astype(BF)
    l2 = np.zeros((128, 128), np.float32)
    for ss in range(128):
        l2[ss, ss + 1:] = 2.0
    l18 = np.zeros((18, 18), np.float32)
    for ii in range(18):
        l18[ii, ii + 1:] = 1.0
    if32 = np.eye(128, dtype=np.float32)
    thr = np.zeros((128, 11), np.float32)
    for k in range(11):
        thr[:, k] = GRP * (k + 1) - (GRP - 1)
    iotag = np.tile(np.arange(NG, dtype=np.float32)[None, :], (128, 1))
    iotap = np.tile(np.arange(128, dtype=np.float32)[:, None], (1, NG))
    rconst = np.zeros((128, B, NT, 3), np.float32)
    for b_ in range(B):
        for jt in range(NT):
            for p in range(128):
                t = jt * 128 + p
                rconst[p, b_, jt, 0] = 0.5 if t < TLOC else 0.0
                rconst[p, b_, jt, 1] = 0.0 if t < TLOC else 1.0e9
                rconst[p, b_, jt, 2] = b_ * NTOK + t
    return {"moe_gw": gw, "moe_w1s": w1s, "moe_w2s": w2s,
            "moe_b2": np.ascontiguousarray(np.asarray(b2, np.float32).reshape(8, D)),
            "moe_l2": l2, "moe_l18": l18, "moe_if32": if32, "moe_thr": thr,
            "moe_iotag": iotag, "moe_iotap": iotap,
            "moe_rconst": rconst}


def alibi_tiles(core):
    A = np.full((128, 8, 2, 144), -1e9, np.float32)
    for h in range(8):
        sl = 0.5 ** h
        for i in range(128):
            for jj in range(144):
                rel = jj - 16 - i
                if -16 <= rel <= 0:
                    A[i, h, 1, jj] = sl * rel
                    if core != 0 or jj - 16 >= HALO:
                        A[i, h, 0, jj] = sl * rel
    return A


def alibi_split(core):
    A = alibi_tiles(core)
    gen = np.ascontiguousarray(A[:, :, 1, :]).astype(BF)
    patch = np.ascontiguousarray(A[:, :, 0, 0:48]).astype(BF)
    return gen, patch


def attn1_weights(wq, wk, wv, wo, qn, kn):
    qkn = np.concatenate([np.asarray(qn, np.float32).reshape(8, 128).T,
                          np.asarray(kn, np.float32).reshape(8, 128).T], axis=1)
    return {"attn1_wq": np.asarray(wq, np.float32),
            "attn1_wk": np.asarray(wk, np.float32),
            "attn1_wv": np.asarray(wv, np.float32),
            "attn1_wo": np.asarray(wo, np.float32),
            "attn1_qkn": np.ascontiguousarray(qkn),
            "attn1_alibi": [alibi_split(c)[0] for c in range(NCORE)],
            "attn1_alibi0": [alibi_split(c)[1] for c in range(NCORE)]}


def ffn0_weights(W_r, W_k, W_v, mu_r, mu_k):
    mu = np.concatenate([np.asarray(mu_r, np.float32).reshape(8, 128).T,
                         np.asarray(mu_k, np.float32).reshape(8, 128).T], axis=1)
    return {
        "ffn0_wrT": np.ascontiguousarray(np.asarray(W_r, np.float32).T),
        "ffn0_wkT": np.ascontiguousarray(np.asarray(W_k, np.float32).T),
        "ffn0_wvT": np.ascontiguousarray(np.asarray(W_v, np.float32).T),
        "ffn0_mu": np.ascontiguousarray(mu),
    }


def kernel(**inputs):
    x = np.asarray(inputs["L0_attn_x"], np.float32)
    wm_ffn0 = ffn0_weights(inputs["L0_ffn_W_r"], inputs["L0_ffn_W_k"],
                           inputs["L0_ffn_W_v"], inputs["L0_ffn_mu_r"],
                           inputs["L0_ffn_mu_k"])
    wm_attn0 = attn0_weights(inputs["L0_attn_Wqkv"], inputs["L0_attn_bqkv"],
                             inputs["L0_attn_Wproj"], inputs["L0_attn_bproj"])
    wm_moe = moe_weights(inputs["L1_ffn_gate_w"], inputs["L1_ffn_W1"],
                         inputs["L1_ffn_b1"], inputs["L1_ffn_W2"],
                         inputs["L1_ffn_b2"])
    wm_attn1 = attn1_weights(inputs["L1_attn_wq"], inputs["L1_attn_wk"],
                             inputs["L1_attn_wv"], inputs["L1_attn_wo"],
                             inputs["L1_attn_q_norm_w"], inputs["L1_attn_k_norm_w"])
    wm = {}
    for m in (wm_ffn0, wm_attn0, wm_moe, wm_attn1):
        wm.update(m)
    x = run_subblocks(["ffn0", "attn0", "ffn0", "moe", "attn1", "moe"], x, wm)
    return x
